# version 02
# baseline (speedup 1.0000x reference)
import contextlib
import numpy as np
import ml_dtypes
import concourse.bass as bass
import concourse.mybir as mybir
from concourse.bass_utils import run_bass_kernel_spmd

F32 = mybir.dt.float32
BF16 = mybir.dt.bfloat16
AF = mybir.ActivationFunctionType
ALU = mybir.AluOpType
AX = mybir.AxisListType
BF = ml_dtypes.bfloat16

MAGIC = float(np.float32(12582912.0))

B, S, Dm = 2, 8192, 2048
NCORES = 8
T = 2048
SEGS = 4
H, HKV, DH, CHUNK = 16, 4, 128, 64
NCH = T // CHUNK
NT = T // 128
ND = Dm // 128
GT = 4
NG = NT // GT


def _alibi_lam():
    i = np.arange(1, H + 1, dtype=np.float32)
    return np.exp2((-8.0 * i / H).astype(np.float32))

def host_tables(core_id):
    seg = core_id % SEGS
    lam = _alibi_lam()
    r = np.arange(CHUNK, dtype=np.float32)
    diff = r[:, None] - r[None, :]
    with np.errstate(under="ignore"):
        diag = np.where(diff >= 0, np.exp(-lam[:, None, None] * diff), 0.0).astype(np.float32)
        diagT = np.ascontiguousarray(diag.transpose(2, 0, 1).reshape(CHUNK, H * CHUNK))
        diagT = np.ascontiguousarray(np.concatenate([diagT, diagT], axis=0))
        kdec = np.exp(-lam[None, :] * (CHUNK - 1.0 - r[:, None])).astype(np.float32)
        kdec = np.concatenate([kdec, kdec], axis=0)
        qdec = np.exp(-lam[:, None] * (r[None, :] + 1.0)).astype(np.float32)
        qdec_bc = np.ascontiguousarray(
            np.broadcast_to(qdec.reshape(1, H * CHUNK), (128, H * CHUNK)).astype(BF))
        tloc = np.arange(T, dtype=np.float32)
        qdec_pp = np.exp(-lam[None, :] * (tloc[:, None] + 1.0)).astype(np.float32)
        blk = np.exp(-lam * CHUNK).astype(np.float32)
        blk_seg = np.exp(-lam.astype(np.float64) * T).astype(np.float32)
    d = DH
    inv_freq = (1.0 / 10000.0 ** (np.arange(0, d, 2, dtype=np.float32) / d)).astype(np.float32)
    tg = (np.arange(T, dtype=np.float32) + np.float32(seg * T)).astype(np.float32)
    freqs = (tg[:, None] * inv_freq[None, :]).astype(np.float32)
    cos = np.concatenate([np.cos(freqs), np.cos(freqs)], axis=1)
    sin = np.concatenate([np.sin(freqs), np.sin(freqs)], axis=1)
    return dict(
        diagT=diagT, kdec=kdec, qdec_bc=qdec_bc, qdec_pp=qdec_pp,
        blk=[float(x) for x in blk], blk_seg=[float(x) for x in blk_seg],
        cosT=np.ascontiguousarray(cos.T.astype(BF)),
        sinT=np.ascontiguousarray(sin.T.astype(BF)),
        ident=np.eye(128, dtype=BF),
        rotm=_rot_matrix())

def _rot_matrix():
    R = np.zeros((128, 128), dtype=BF)
    for m in range(64):
        R[m + 64, m] = -1.0
    for m in range(64, 128):
        R[m - 64, m] = 1.0
    return R

def host_weights(inp):
    c = np.ascontiguousarray
    return dict(
        wqT=c(inp["L0_attn_wq"].T), wkT=c(inp["L0_attn_wk"].T),
        wvT=c(inp["L0_attn_wv"].T), woT=c(inp["L0_attn_wo"].T),
        w0fT=c(inp["L0_ffn_weight"].T), b0f=c(inp["L0_ffn_bias"].reshape(1, Dm)),
        winT=c(inp["L1_attn_W_in"].T), convw=c(inp["L1_attn_conv_w"].reshape(4096, 4)),
        convb=c(inp["L1_attn_conv_b"].reshape(4096, 1)),
        woutT=c(inp["L1_attn_W_out"].T), wm1T=c(inp["L1_attn_W_mlp1"].T),
        wm2T=c(inp["L1_attn_W_mlp2"].T), w1T=c(inp["L1_ffn_w1"].T),
        w3T=c(inp["L1_ffn_w3"].T), w2T=c(inp["L1_ffn_w2"].T))


class Ctr:
    def __init__(self, h):
        self.h = h
        self.v = 0

class Prog:
    ENGS = ("sync", "scalar", "vector", "tensor", "gpsimd")
    def __init__(self, nc):
        self.nc = nc
        self.streams = {e: [] for e in self.ENGS}
    def emit(self):
        with self.nc.Block() as blk:
            for e in self.ENGS:
                ops = self.streams[e]
                if not ops:
                    continue
                getattr(blk, e)(lambda eng, ops=ops: [f(eng) for f in ops])

class Eng:
    def __init__(self, p, name, ctr):
        self.p, self.name, self.c = p, name, ctr
    def wait(self, ctr, v):
        if v > 0:
            self.p.streams[self.name].append(
                lambda e, h=ctr.h, v=v: e.wait_ge(h, v))
    def waits(self, deps):
        for (c, v) in deps:
            self.wait(c, v)
    def op(self, fn, deps=(), inc=1):
        self.waits(deps)
        c = self.c
        self.p.streams[self.name].append(
            lambda e, fn=fn, h=c.h, n=inc: fn(e).then_inc(h, n))
        c.v += inc
        return c.v
    def chain(self, fns, deps=(), inc=1):
        self.waits(deps)
        c = self.c
        def f(e, fns=fns, h=c.h, n=inc):
            last = None
            for fn in fns:
                last = fn(e)
            last.then_inc(h, n)
        self.p.streams[self.name].append(f)
        c.v += inc
        return c.v
    def dma(self, out, in_, deps=()):
        return self.op(lambda e, o=out, i=in_: e.dma_start(o, i), deps=deps, inc=16)

class Phase:
    def __init__(self, nc, st, tag):
        self.nc = nc
        p = Prog(nc)
        self.p = p
        self.ctrs = {}
        for nm in ("pe", "ac", "ve", "po", "sy", "gd"):
            self.ctrs[nm] = Ctr(st.enter_context(nc.semaphore(f"{tag}_{nm}")))
        self.TE = Eng(p, "tensor", self.ctrs["pe"])
        self.SC = Eng(p, "scalar", self.ctrs["ac"])
        self.VE = Eng(p, "vector", self.ctrs["ve"])
        self.GP = Eng(p, "gpsimd", self.ctrs["po"])
        self.SY = Eng(p, "sync", self.ctrs["sy"])
        self.GD = Eng(p, "gpsimd", self.ctrs["gd"])
    def emit(self):
        self.p.emit()

def emit_ln(ph, xb, xscr, stats, eps, n=Dm, deps=(), rms_sqrt_sum=False):
    SC, VE = ph.SC, ph.VE
    ac, ve = ph.ctrs["ac"], ph.ctrs["ve"]
    SC.waits(deps)
    SC.op(lambda e: e.activation(xscr[:, 0:n], xb[:, 0:n], AF.Identity,
                                 accum_out=stats[:, 0:1]))
    va = SC.op(lambda e: e.activation(xscr[:, 0:n], xb[:, 0:n], AF.Square,
                                      accum_out=stats[:, 1:2]))
    inv_n = float(np.float32(1.0 / n))
    VE.op(lambda e: e.tensor_scalar(stats[:, 2:3], stats[:, 0:1], inv_n, None, ALU.mult),
          deps=[(ac, va)])
    v1 = VE.op(lambda e: e.tensor_scalar(stats[:, 3:4], stats[:, 1:2], inv_n, None, ALU.mult))
    v2 = VE.op(lambda e: e.scalar_tensor_tensor(stats[:, 4:5], stats[:, 2:3], stats[:, 2:3],
                                                stats[:, 3:4], ALU.mult, ALU.subtract),
               deps=[(ve, v1)])
    v3 = VE.op(lambda e: e.tensor_scalar(stats[:, 5:6], stats[:, 4:5], -1.0, float(eps),
                                         ALU.mult, ALU.add), deps=[(ve, v2)])
    v4 = SC.op(lambda e: e.activation(stats[:, 6:7], stats[:, 5:6], AF.Sqrt),
               deps=[(ve, v3)])
    v5 = VE.op(lambda e: e.reciprocal(stats[:, 6:7], stats[:, 6:7]), deps=[(ac, v4)])
    v6 = VE.op(lambda e: e.scalar_tensor_tensor(stats[:, 7:8], stats[:, 2:3], -1.0,
                                                stats[:, 6:7], ALU.mult, ALU.mult),
               deps=[(ve, v5)])
    return v6


def build_A(tab):
    nc = bass.Bass("TRN2", target_bir_lowering=False)
    dr = {}
    def din(name, shape, dt):
        dr[name] = nc.dram_tensor(name, shape, dt, kind="ExternalInput").ap()
    def dout(name, shape, dt):
        dr[name] = nc.dram_tensor(name, shape, dt, kind="ExternalOutput").ap()
    din("xin", [T, Dm], F32)
    din("wqT", [Dm, 2048], F32); din("wkT", [Dm, 512], F32); din("wvT", [Dm, 512], F32)
    din("diagT", [128, 1024], F32); din("kdec", [128, 16], F32)
    din("rotm", [128, 128], BF16)
    din("qdec_bc", [128, 1024], BF16)
    din("cosT", [128, T], BF16); din("sinT", [128, T], BF16)
    din("ident", [128, 128], BF16)
    dout("qf", [2048, T], BF16)
    dout("oloc", [T, 2048], BF16)
    dout("Dp", [128, 2048], F32)

    with contextlib.ExitStack() as st:
        sb = {}
        def alloc(name, shape, dt, tag="a_"):
            sb[name] = st.enter_context(nc.sbuf_tensor(f"{tag}{name}", shape, dt))
        def palloc(name, shape, dt, tag="a_"):
            sb[name] = st.enter_context(nc.psum_tensor(f"{tag}{name}", shape, dt))
        alloc("xbuf0", [128, Dm], F32); alloc("xbuf1", [128, Dm], F32)
        alloc("xscr", [128, Dm], F32)
        alloc("stats", [128, 8], F32)
        alloc("xh", [128, Dm], BF16)
        for k in range(ND):
            alloc(f"xhT{k}", [128, 512], BF16)
        alloc("wbuf0", [128, 8192], BF16); alloc("wbuf1", [128, 8192], BF16)
        for h in range(H):
            alloc(f"qfg{h}", [128, 512], BF16)
        for k in range(HKV):
            alloc(f"kfg{k}", [128, 512], BF16)
        for ts_ in range(GT):
            alloc(f"ktg{ts_}", [128, 512], BF16)
            alloc(f"vtg{ts_}", [128, 512], BF16)
            alloc(f"olg{ts_}", [128, 2048], BF16)
        alloc("kv", [128, 2048], F32)
        alloc("kvbf", [128, 2048], BF16)
        alloc("ropeA", [128, 512], BF16); alloc("ropeB", [128, 512], BF16)
        alloc("ropeC", [128, 512], BF16)
        for h4 in range(4):
            alloc(f"qd{h4}", [128, 64], BF16)
            alloc(f"kd{h4}", [128, 128], BF16)
        alloc("ssT", [128, 256], BF16)
        alloc("cos", [128, T], BF16); alloc("sin", [128, T], BF16)
        alloc("c_diagT", [128, 1024], F32); alloc("c_kdec", [128, 16], F32)
        alloc("c_rotm", [128, 128], BF16)
        alloc("qdecbc", [128, 1024], BF16)
        alloc("c_ident", [128, 128], BF16)
        palloc("pq0", [128, 512], F32); palloc("pq1", [128, 512], F32)
        palloc("pt0", [128, 128], BF16); palloc("pt1", [128, 128], BF16)
        palloc("psS", [128, 256], F32)
        palloc("psO", [64, 512], F32)
        palloc("psA", [128, 512], F32)
        palloc("pr", [128, 512], F32)

        ph = Phase(nc, st, "A")
        emit_A(ph, dr, sb, tab)
        ph.emit()
    return nc

def emit_A(ph, dr, sb, tab):
    TE, SC, VE, SY, GD = ph.TE, ph.SC, ph.VE, ph.SY, ph.GD
    pe, ac, ve, sy, gd = (ph.ctrs[n] for n in ("pe", "ac", "ve", "sy", "gd"))
    blk = tab["blk"]

    v_tabs = 0
    for nm, src in (("cos", "cosT"), ("sin", "sinT"), ("c_diagT", "diagT"),
                    ("c_kdec", "kdec"), ("qdecbc", "qdec_bc"), ("c_ident", "ident"),
                    ("c_rotm", "rotm")):
        v_tabs = SY.dma(sb[nm][:, :], dr[src])
    v_kv0 = VE.op(lambda e: e.memset(sb["kv"][:, :], 0.0))
    v_kvbf0 = SC.op(lambda e: e.activation(sb["kvbf"][:, :], sb["kv"][:, :], AF.Identity),
                    deps=[(ve, v_kv0)])

    wsched = [("q", 0), ("q", 4), ("q", 8), ("q", 12), ("k", 0), ("v", 0)]
    def wsrc(kind, e0):
        if kind == "q":
            s = dr["wqT"][:, e0 * 128:(e0 + 4) * 128]
        elif kind == "k":
            s = dr["wkT"]
        else:
            s = dr["wvT"]
        return s.rearrange("(c p) j -> p c j", p=128)

    rel_xbuf = [(ac, 0), (ac, 0)]
    rel_xh = (pe, 0)
    rel_xhT = (pe, 0)
    rel_wbuf = [(pe, 0), (pe, 0)]
    rel_pq = [(ac, 0), (ac, 0)]
    rel_pt = [(ve, 0), (ve, 0)]
    rel_qkv_te = (pe, 0)
    rel_pr = (ve, 0)
    rel_olg = (sy, 0)
    v_kvbf = (ac, v_kvbf0)
    v_ssT_free = (pe, 0)
    v_psS_free = (ve, 0)
    v_psO_free = (ac, 0)
    v_psA_free = (ve, 0)

    def wdst(slot):
        return sb[f"wbuf{slot}"][:, :].rearrange("p (c j) -> p c j", j=512)
    wload = {}
    wload[0] = GD.dma(wdst(0), wsrc(*wsched[0]), deps=())

    for g in range(NG):
        v_evac = {}
        for i in range(GT):
            t128 = g * GT + i
            xb = sb[f"xbuf{t128 % 2}"]
            v_x = SY.dma(xb[:, :], dr["xin"][t128 * 128:(t128 + 1) * 128, :],
                         deps=[rel_xbuf[t128 % 2]])
            v_ln = emit_ln(ph, xb, sb["xscr"], sb["stats"], 1e-5,
                           deps=[(sy, v_x)])
            v_norm = SC.op(lambda e, xb=xb: e.activation(
                sb["xh"][:, :], xb[:, :], AF.Identity,
                scale=sb["stats"][:, 6:7], bias=sb["stats"][:, 7:8]),
                deps=[(ve, v_ln), rel_xh])
            rel_xbuf[t128 % 2] = (ac, v_norm)
            for k in range(ND):
                pt = sb[f"pt{k % 2}"]
                v_tr = TE.op(lambda e, k=k, pt=pt: e.transpose(
                    pt[:, :], sb["xh"][:, k * 128:(k + 1) * 128], sb["c_ident"][:, :]),
                    deps=[(ac, v_norm), (sy, v_tabs), rel_pt[k % 2], rel_xhT])
                v_ev = VE.op(lambda e, k=k, i=i, pt=pt: e.tensor_copy(
                    sb[f"xhT{k}"][:, i * 128:(i + 1) * 128], pt[:, :]),
                    deps=[(pe, v_tr)])
                rel_pt[k % 2] = (ve, v_ev)
                v_evac[(i, k)] = v_ev
            rel_xh = (pe, pe.v)
        v_xhT = ve.v

        cosg = sb["cos"][:, g * 512:(g + 1) * 512]
        sing = sb["sin"][:, g * 512:(g + 1) * 512]
        v_vtg = {}
        for bi, (kind, e0) in enumerate(wsched):
            slot = bi % 2
            gi = g * len(wsched) + bi
            v_w = wload[gi]
            if gi + 1 < NG * len(wsched):
                nkind, ne0 = wsched[(gi + 1) % len(wsched)]
                wload[gi + 1] = GD.dma(wdst(1 - slot), wsrc(nkind, ne0),
                                       deps=[rel_wbuf[1 - slot]])
            if kind in ("q", "k"):
                for ec in range(4):
                    eg = (e0 + ec) if kind == "q" else ec
                    pqi = (bi * 4 + ec) % 2
                    pq = sb[f"pq{pqi}"]
                    fns = []
                    for k in range(ND):
                        lhs = sb[f"wbuf{slot}"][:, k * 512 + ec * 128: k * 512 + (ec + 1) * 128]
                        fns.append(lambda e, lhs=lhs, k=k, pq=pq, s=(k == 0), t=(k == ND - 1):
                                   e.matmul(pq[:, :], lhs, sb[f"xhT{k}"][:, :], start=s, stop=t))
                    v_mm = TE.chain(fns, deps=[(gd, v_w), (ve, v_xhT), rel_pq[pqi]])
                    dst = sb[f"qfg{eg}"] if kind == "q" else sb[f"kfg{eg}"]
                    v_eA = SC.op(lambda e, pq=pq: e.activation(sb["ropeA"][:, :], pq[:, :],
                                                               AF.Identity),
                                 deps=[(pe, v_mm), (ve, ve.v)])
                    rel_pq[pqi] = (ac, v_eA)
                    v_rm = TE.op(lambda e: e.matmul(sb["pr"][:, :], sb["c_rotm"][:, :],
                                                    sb["ropeA"][:, :], start=True, stop=True),
                                 deps=[(ac, v_eA), rel_pr])
                    v_r3 = VE.op(lambda e, cosg=cosg: e.tensor_tensor(
                        sb["ropeC"][:, :], sb["ropeA"][:, :], cosg, ALU.mult),
                        deps=[(ac, v_eA)])
                    v_r4 = VE.op(lambda e, sing=sing: e.tensor_tensor(
                        sb["ropeB"][:, :], sb["pr"][:, :], sing, ALU.mult),
                        deps=[(pe, v_rm)])
                    rel_pr = (ve, v_r4)
                    v_r5 = VE.op(lambda e, dst=dst: e.tensor_tensor(
                        dst[:, :], sb["ropeC"][:, :], sb["ropeB"][:, :], ALU.add),
                        deps=[(ve, v_r4), rel_qkv_te])
                    if kind == "q":
                        SY.dma(dr["qf"][eg * 128:(eg + 1) * 128, g * 512:(g + 1) * 512],
                               dst[:, :], deps=[(ve, v_r5)])
                    else:
                        for ts_ in range(GT):
                            pt = sb[f"pt{ts_ % 2}"]
                            v_tr = TE.op(lambda e, dst=dst, ts_=ts_, pt=pt: e.transpose(
                                pt[:, :], dst[:, ts_ * 128:(ts_ + 1) * 128], sb["c_ident"][:, :]),
                                deps=[(ve, v_r5), rel_pt[ts_ % 2]])
                            v_ev = VE.op(lambda e, ts_=ts_, eg=eg, pt=pt: e.tensor_copy(
                                sb[f"ktg{ts_}"][:, eg * 128:(eg + 1) * 128], pt[:, :]),
                                deps=[(pe, v_tr), rel_qkv_te])
                            rel_pt[ts_ % 2] = (ve, v_ev)
            else:
                for ts_ in range(GT):
                    pqi = ts_ % 2
                    pq = sb[f"pq{pqi}"]
                    fns = []
                    for k in range(ND):
                        lhs = sb[f"xhT{k}"][:, ts_ * 128:(ts_ + 1) * 128]
                        rhs = sb[f"wbuf{slot}"][:, k * 512:(k + 1) * 512]
                        fns.append(lambda e, lhs=lhs, rhs=rhs, pq=pq, s=(k == 0), t=(k == ND - 1):
                                   e.matmul(pq[:, :], lhs, rhs, start=s, stop=t))
                    v_mm = TE.chain(fns, deps=[(gd, v_w), (ve, v_xhT), rel_pq[pqi]])
                    v_e = SC.op(lambda e, pq=pq, ts_=ts_: e.activation(
                        sb[f"vtg{ts_}"][:, :], pq[:, :], AF.Identity),
                        deps=[(pe, v_mm), rel_qkv_te])
                    rel_pq[pqi] = (ac, v_e)
                    v_vtg[ts_] = v_e
            rel_wbuf[slot] = (pe, pe.v)
        v_qkv = (ve.v, ac.v)

        for c in range(2 * GT):
            ts_ = c // 2
            half = c % 2
            rows0 = half * 64
            tcol = c * 64
            for g4 in range(HKV):
                hs = [g4 * 4 + j for j in range(4)]
                v_kd = {}
                for j, h in enumerate(hs):
                    v_kd[j] = VE.op(lambda e, j=j, h=h, ts_=ts_, rows0=rows0, g4=g4:
                        e.tensor_scalar(sb[f"kd{j}"][rows0:rows0 + 64, :],
                                        sb[f"ktg{ts_}"][rows0:rows0 + 64, g4 * 128:(g4 + 1) * 128],
                                        sb["c_kdec"][rows0:rows0 + 64, h:h + 1], None, ALU.mult),
                        deps=[(ve, v_qkv[0]), (ac, v_qkv[1])])
                v_qd = {}
                for j, h in enumerate(hs):
                    v_qd[j] = VE.op(lambda e, j=j, h=h, tcol=tcol: e.tensor_tensor(
                        sb[f"qd{j}"][:, :], sb[f"qfg{h}"][:, tcol:tcol + 64],
                        sb["qdecbc"][:, h * 64:(h + 1) * 64], ALU.mult))
                fns = []
                for j, h in enumerate(hs):
                    fns.append(lambda e, j=j, h=h, g4=g4, tcol=tcol, rows0=rows0: e.matmul(
                        sb["psS"][rows0:rows0 + 64, j * 64:(j + 1) * 64],
                        sb[f"kfg{g4}"][:, tcol:tcol + 64],
                        sb[f"qfg{h}"][:, tcol:tcol + 64], start=True, stop=True))
                v_s = TE.chain(fns, deps=[v_psS_free, (ve, v_qkv[0]), (ac, v_qkv[1]),
                                          v_ssT_free])
                v_diag = VE.op(lambda e, g4=g4, rows0=rows0: e.tensor_tensor(
                    sb["ssT"][rows0:rows0 + 64, :], sb["psS"][rows0:rows0 + 64, :],
                    sb["c_diagT"][rows0:rows0 + 64, g4 * 256:(g4 + 1) * 256], ALU.mult),
                    deps=[(pe, v_s)])
                v_psS_free = (ve, v_diag)
                fns = []
                for j, h in enumerate(hs):
                    fns.append(lambda e, j=j, ts_=ts_, rows0=rows0, g4=g4: e.matmul(
                        sb["psO"][:, j * 128:(j + 1) * 128],
                        sb["ssT"][rows0:rows0 + 64, j * 64:(j + 1) * 64],
                        sb[f"vtg{ts_}"][rows0:rows0 + 64, g4 * 128:(g4 + 1) * 128],
                        start=True, stop=False))
                    fns.append(lambda e, j=j, h=h: e.matmul(
                        sb["psO"][:, j * 128:(j + 1) * 128],
                        sb[f"qd{j}"][:, :],
                        sb["kvbf"][:, h * 128:(h + 1) * 128],
                        start=False, stop=True))
                v_o = TE.chain(fns, deps=[(ve, v_diag), (ve, max(v_qd.values())),
                                          v_kvbf, v_psO_free])
                v_ssT_free = (pe, v_o)
                fns = []
                for j, h in enumerate(hs):
                    fns.append(lambda e, j=j, ts_=ts_, rows0=rows0, g4=g4: e.matmul(
                        sb["psA"][:, j * 128:(j + 1) * 128],
                        sb[f"kd{j}"][rows0:rows0 + 64, :],
                        sb[f"vtg{ts_}"][rows0:rows0 + 64, g4 * 128:(g4 + 1) * 128],
                        start=True, stop=True))
                v_A = TE.chain(fns, deps=[(ve, max(v_kd.values())), v_psA_free])
                v_oe = SC.op(lambda e, ts_=ts_, rows0=rows0, g4=g4: e.activation(
                    sb[f"olg{ts_}"][rows0:rows0 + 64, g4 * 512:(g4 + 1) * 512],
                    sb["psO"][0:64, :], AF.Identity),
                    deps=[(pe, v_o), rel_olg])
                v_psO_free = (ac, v_oe)
                v_u = 0
                for j, h in enumerate(hs):
                    v_u = VE.op(lambda e, j=j, h=h: e.scalar_tensor_tensor(
                        sb["kv"][:, h * 128:(h + 1) * 128],
                        sb["kv"][:, h * 128:(h + 1) * 128], blk[h],
                        sb["psA"][:, j * 128:(j + 1) * 128], ALU.mult, ALU.add),
                        deps=[(pe, v_A), v_kvbf])
                v_psA_free = (ve, v_u)
                v_kb = SC.op(lambda e, g4=g4: e.activation(
                    sb["kvbf"][:, g4 * 512:(g4 + 1) * 512],
                    sb["kv"][:, g4 * 512:(g4 + 1) * 512], AF.Identity),
                    deps=[(ve, v_u)])
                v_kvbf = (ac, v_kb)
        rel_qkv_te = (pe, pe.v)
        v_st = 0
        for ts_ in range(GT):
            v_st = SY.dma(dr["oloc"][(g * GT + ts_) * 128:(g * GT + ts_ + 1) * 128, :],
                          sb[f"olg{ts_}"][:, :], deps=[(ac, ac.v)])
        rel_olg = (sy, v_st)
        rel_xhT = (pe, pe.v)

    SY.dma(dr["Dp"], sb["kv"][:, :], deps=[(ve, ve.v), (ac, ac.v)])
    ph.SY.wait(ph.ctrs["sy"], ph.ctrs["sy"].v)


def build_B(tab):
    nc = bass.Bass("TRN2", target_bir_lowering=False)
    dr = {}
    def din(name, shape, dt):
        dr[name] = nc.dram_tensor(name, shape, dt, kind="ExternalInput").ap()
    def dout(name, shape, dt):
        dr[name] = nc.dram_tensor(name, shape, dt, kind="ExternalOutput").ap()
    din("xin", [T, Dm], F32)
    din("qf", [2048, T], BF16)
    din("oloc", [T, 2048], BF16)
    for i in range(3):
        din(f"Dpred{i}", [128, 2048], F32)
    din("woT", [Dm, 2048], F32)
    din("w0fT", [Dm, 2048], F32)
    din("b0f", [1, Dm], F32)
    din("qdec_pp", [T, 16], F32)
    din("ident", [128, 128], BF16)
    dout("S2", [T, Dm], F32)
    S1 = nc.dram_tensor("S1", [T, Dm], F32, kind="Internal").ap()
    dr["S1"] = S1

    with contextlib.ExitStack() as st:
        sb = {}
        def alloc(name, shape, dt, tag="b2_"):
            sb[name] = st.enter_context(nc.sbuf_tensor(f"{tag}{name}", shape, dt))
        def palloc(name, shape, dt, tag="b2_"):
            sb[name] = st.enter_context(nc.psum_tensor(f"{tag}{name}", shape, dt))
        alloc("wo", [128, 16 * 2048], BF16)
        alloc("kvin", [128, 2048], F32)
        alloc("kvbf", [128, 2048], BF16)
        alloc("dpr", [128, 2048], F32)
        alloc("qft", [128, 16 * 128], BF16)
        alloc("olt", [128, 2048], BF16)
        alloc("o_as", [128, 2048], BF16)
        for k in range(ND):
            alloc(f"oT{k}", [128, 128], BF16)
        alloc("qpp", [128, 16], F32)
        alloc("xint", [128, Dm], F32)
        alloc("s1t", [128, Dm], F32)
        alloc("c_ident", [128, 128], BF16)
        palloc("psC0", [128, 512], F32); palloc("psC1", [128, 512], F32)
        palloc("psW0", [128, 512], F32); palloc("psW1", [128, 512], F32)
        palloc("pt0", [128, 128], BF16); palloc("pt1", [128, 128], BF16)
        ph = Phase(nc, st, "B2")
        emit_B2(ph, dr, sb, tab)
        ph.emit()

    with contextlib.ExitStack() as st:
        sb = {}
        def alloc(name, shape, dt, tag="b4_"):
            sb[name] = st.enter_context(nc.sbuf_tensor(f"{tag}{name}", shape, dt))
        def palloc(name, shape, dt, tag="b4_"):
            sb[name] = st.enter_context(nc.psum_tensor(f"{tag}{name}", shape, dt))
        for k in range(ND):
            alloc(f"xqT{k}", [128, T], BF16)
        alloc("wst", [128, 16 * 512], F32)
        for k in range(ND):
            alloc(f"wtn{k}", [128, 512], BF16)
        alloc("s1t", [128, Dm], F32)
        alloc("xn1", [128, Dm], F32)
        alloc("xn2", [128, Dm], F32)
        alloc("xqb", [128, Dm], BF16)
        alloc("stats", [128, 8], F32)
        alloc("sxall", [128, NT], F32)
        alloc("wab", [128, 512], F32)
        alloc("grow", [1, 512], F32)
        alloc("ginvb", [128, 512], F32)
        alloc("gmb", [128, 512], F32)
        alloc("bb", [128, 512], F32)
        alloc("brow", [1, 512], F32)
        alloc("onesc", [128, 1], F32)
        alloc("onesr", [1, 128], F32)
        alloc("ef", [128, 512], F32)
        alloc("e2", [128, 512], F32)
        alloc("s2t", [128, 512], F32)
        alloc("sl1", [128, 512], F32)
        alloc("c_ident2", [128, 128], BF16)
        palloc("psF0", [128, 512], F32); palloc("psF1", [128, 512], F32)
        palloc("psG", [1, 512], F32)
        palloc("psB", [128, 512], F32)
        palloc("pt0b", [128, 128], BF16); palloc("pt1b", [128, 128], BF16)
        ph = Phase(nc, st, "B4")
        emit_B4(ph, dr, sb, tab)
        ph.emit()
    return nc

def emit_B2(ph, dr, sb, tab):
    TE, SC, VE, SY, GD = ph.TE, ph.SC, ph.VE, ph.SY, ph.GD
    pe, ac, ve, sy, gd = (ph.ctrs[n] for n in ("pe", "ac", "ve", "sy", "gd"))
    bs = tab["blk_seg"]

    v_id = SY.dma(sb["c_ident"][:, :], dr["ident"])
    v_wo = GD.dma(sb["wo"][:, :].rearrange("p (c j) -> p c j", j=2048),
                  dr["woT"].rearrange("(c p) j -> p c j", p=128))
    v_kv = VE.op(lambda e: e.memset(sb["kvin"][:, :], 0.0))
    for i in range(3):
        v_d = SY.dma(sb["dpr"][:, :], dr[f"Dpred{i}"], deps=[(ve, v_kv)])
        for h in range(H):
            v_kv = VE.op(lambda e, h=h: e.scalar_tensor_tensor(
                sb["kvin"][:, h * 128:(h + 1) * 128],
                sb["kvin"][:, h * 128:(h + 1) * 128], bs[h],
                sb["dpr"][:, h * 128:(h + 1) * 128], ALU.mult, ALU.add),
                deps=[(sy, v_d), (ve, v_kv)])
    v_kvbf = SC.op(lambda e: e.activation(sb["kvbf"][:, :], sb["kvin"][:, :], AF.Identity),
                   deps=[(ve, v_kv)])

    rel_qft = (pe, 0); rel_olt = (ve, 0); rel_oas = (pe, 0)
    rel_psC = [(ve, 0), (ve, 0)]
    rel_psW = [(ve, 0), (ve, 0)]
    rel_pt = [(ve, 0), (ve, 0)]
    rel_oT = (pe, 0)
    rel_s1t = (sy, 0)
    rel_xint = (ve, 0)
    for t128 in range(NT):
        cols = slice(t128 * 128, (t128 + 1) * 128)
        v_q = SY.dma(sb["qft"][:, :].rearrange("p (c t) -> p c t", t=128),
                     dr["qf"][:, cols].rearrange("(c p) t -> p c t", p=128),
                     deps=[rel_qft])
        v_ol = SY.dma(sb["olt"][:, :], dr["oloc"][cols, :], deps=[rel_olt])
        v_pp = SY.dma(sb["qpp"][:, :], dr["qdec_pp"][cols, :])
        v_xi = SY.dma(sb["xint"][:, :], dr["xin"][cols, :], deps=[rel_xint])
        v_oa = 0
        for g4 in range(HKV):
            psC = sb[f"psC{g4 % 2}"]
            fns = []
            for j in range(4):
                h = g4 * 4 + j
                fns.append(lambda e, j=j, h=h, psC=psC: e.matmul(
                    psC[:, j * 128:(j + 1) * 128],
                    sb["qft"][:, h * 128:(h + 1) * 128],
                    sb["kvbf"][:, h * 128:(h + 1) * 128], start=True, stop=True))
            v_mm = TE.chain(fns, deps=[(sy, v_q), (ac, v_kvbf), rel_psC[g4 % 2]])
            for j in range(4):
                h = g4 * 4 + j
                v_oa = VE.op(lambda e, j=j, h=h, psC=psC: e.scalar_tensor_tensor(
                    sb["o_as"][:, h * 128:(h + 1) * 128], psC[:, j * 128:(j + 1) * 128],
                    sb["qpp"][:, h:h + 1], sb["olt"][:, h * 128:(h + 1) * 128],
                    ALU.mult, ALU.add),
                    deps=[(pe, v_mm), (sy, max(v_ol, v_pp)), rel_oas])
            rel_psC[g4 % 2] = (ve, v_oa)
        rel_olt = (ve, v_oa)
        rel_qft = (pe, pe.v)
        v_tr_ev = {}
        for k in range(ND):
            pt = sb[f"pt{k % 2}"]
            v_tr = TE.op(lambda e, k=k, pt=pt: e.transpose(
                pt[:, :], sb["o_as"][:, k * 128:(k + 1) * 128], sb["c_ident"][:, :]),
                deps=[(ve, v_oa), (sy, v_id), rel_pt[k % 2], rel_oT])
            v_ev = VE.op(lambda e, k=k, pt=pt: e.tensor_copy(sb[f"oT{k}"][:, :], pt[:, :]),
                         deps=[(pe, v_tr)])
            rel_pt[k % 2] = (ve, v_ev)
            v_tr_ev[k] = v_ev
        rel_oas = (pe, pe.v)
        v_trd = max(v_tr_ev.values())
        v_s1 = 0
        for dblk in range(4):
            psW = sb[f"psW{dblk % 2}"]
            fns = []
            for k in range(ND):
                fns.append(lambda e, k=k, dblk=dblk, psW=psW: e.matmul(
                    psW[:, :], sb[f"oT{k}"][:, :],
                    sb["wo"][:, k * 2048 + dblk * 512: k * 2048 + (dblk + 1) * 512],
                    start=(k == 0), stop=(k == ND - 1)))
            v_mm = TE.chain(fns, deps=[(ve, v_trd), (gd, v_wo), rel_psW[dblk % 2]])
            v_s1 = VE.op(lambda e, dblk=dblk, psW=psW: e.tensor_tensor(
                sb["s1t"][:, dblk * 512:(dblk + 1) * 512], psW[:, :],
                sb["xint"][:, dblk * 512:(dblk + 1) * 512], ALU.add),
                deps=[(pe, v_mm), (sy, v_xi), rel_s1t])
            rel_psW[dblk % 2] = (ve, v_s1)
        rel_oT = (pe, pe.v)
        rel_xint = (ve, v_s1)
        v_st = SY.dma(dr["S1"][cols, :], sb["s1t"][:, :], deps=[(ve, v_s1)])
        rel_s1t = (sy, v_st)
    SY.wait(sy, sy.v)
    GD.wait(gd, gd.v)

def emit_B4(ph, dr, sb, tab):
    TE, SC, VE, SY, GD = ph.TE, ph.SC, ph.VE, ph.SY, ph.GD
    pe, ac, ve, sy, gd = (ph.ctrs[n] for n in ("pe", "ac", "ve", "sy", "gd"))

    v_id = SY.dma(sb["c_ident2"][:, :], dr["ident"])
    v_ones = VE.op(lambda e: e.memset(sb["onesc"][:, :], 1.0))
    v_ones = VE.op(lambda e: e.memset(sb["onesr"][:, :], 1.0))

    rel_s1t = (ac, 0)
    rel_xn1 = (ve, 0)
    rel_xn2 = (ve, 0)
    rel_xqb = (pe, 0)
    rel_pt = [(ve, 0), (ve, 0)]
    for t128 in range(NT):
        cols = slice(t128 * 128, (t128 + 1) * 128)
        v_x = SY.dma(sb["s1t"][:, :], dr["S1"][cols, :], deps=[rel_s1t])
        v_ln = emit_ln(ph, sb["s1t"], sb["xn1"], sb["stats"], 1e-5,
                       deps=[(sy, v_x), rel_xn1])
        v_n1 = SC.op(lambda e: e.activation(sb["xn1"][:, :], sb["s1t"][:, :], AF.Identity,
                                            scale=sb["stats"][:, 6:7], bias=sb["stats"][:, 7:8]),
                     deps=[(ve, v_ln)])
        rel_s1t = (ac, v_n1)
        v_sq = SC.op(lambda e: e.activation(sb["xn2"][:, :], sb["xn1"][:, :], AF.Square,
                                            accum_out=sb["stats"][:, 1:2]),
                     deps=[rel_xn2])
        v_r1 = VE.op(lambda e: e.tensor_scalar(sb["stats"][:, 2:3], sb["stats"][:, 1:2],
                                               float(np.float32(1.0 / Dm)), 1e-8, ALU.mult, ALU.add),
                     deps=[(ac, v_sq)])
        v_r2 = SC.op(lambda e: e.activation(sb["stats"][:, 3:4], sb["stats"][:, 2:3], AF.Sqrt),
                     deps=[(ve, v_r1)])
        v_r3 = VE.op(lambda e: e.reciprocal(sb["stats"][:, 3:4], sb["stats"][:, 3:4]),
                     deps=[(ac, v_r2)])
        v_x2 = VE.op(lambda e: e.tensor_scalar(sb["xn2"][:, :], sb["xn1"][:, :],
                                               sb["stats"][:, 3:4], None, ALU.mult),
                     deps=[(ve, v_r3), (ac, v_sq)])
        v_mx = VE.op(lambda e: e.reduce_max(sb["stats"][:, 4:5], sb["xn2"][:, :], axis=AX.X,
                                            apply_absolute_value=True),
                     deps=[(ve, v_x2)])
        v_sc = VE.op(lambda e, t128=t128: e.tensor_scalar(
            sb["sxall"][:, t128:t128 + 1], sb["stats"][:, 4:5],
            float(np.float32(1.0 / 127.0)), None, ALU.mult), deps=[(ve, v_mx)])
        v_e = VE.op(lambda e, t128=t128: e.tensor_scalar(
            sb["stats"][:, 5:6], sb["sxall"][:, t128:t128 + 1], 1e-8, None, ALU.add),
            deps=[(ve, v_sc)])
        v_rc = VE.op(lambda e: e.reciprocal(sb["stats"][:, 5:6], sb["stats"][:, 5:6]),
                     deps=[(ve, v_e)])
        v_q1 = VE.op(lambda e: e.tensor_scalar(sb["xn1"][:, :], sb["xn2"][:, :],
                                               sb["stats"][:, 5:6], MAGIC, ALU.mult, ALU.add),
                     deps=[(ve, v_rc)])
        v_q2 = VE.op(lambda e: e.tensor_scalar(sb["xqb"][:, :], sb["xn1"][:, :],
                                               MAGIC, None, ALU.subtract),
                     deps=[(ve, v_q1), rel_xqb])
        rel_xn1 = (ve, v_q2)
        rel_xn2 = (ve, v_q1)
        for k in range(ND):
            pt = sb[f"pt{k % 2}b"]
            v_tr = TE.op(lambda e, k=k, pt=pt: e.transpose(
                pt[:, :], sb["xqb"][:, k * 128:(k + 1) * 128], sb["c_ident2"][:, :]),
                deps=[(ve, v_q2), (sy, v_id), rel_pt[k % 2]])
            v_ev = VE.op(lambda e, k=k, t128=t128, pt=pt: e.tensor_copy(
                sb[f"xqT{k}"][:, t128 * 128:(t128 + 1) * 128], pt[:, :]),
                deps=[(pe, v_tr)])
            rel_pt[k % 2] = (ve, v_ev)
        rel_xqb = (pe, pe.v)
    v_xq_done = ve.v

    rel_wst = (ac, 0)
    rel_wab = (pe, 0)
    rel_psG = (ve, 0)
    rel_psF = [(ac, 0), (ac, 0)]
    rel_ef = [(ve, 0)]
    rel_s2 = (sy, 0)
    rel_sl1 = (ve, 0)
    for nblk in range(4):
        ncols = slice(nblk * 512, (nblk + 1) * 512)
        v_w = SY.dma(sb["wst"][:, :].rearrange("p (c j) -> p c j", j=512),
                     dr["w0fT"][:, ncols].rearrange("(c p) j -> p c j", p=128),
                     deps=[rel_wst])
        v_g = 0
        for k in range(ND):
            v_ab = SC.op(lambda e, k=k: e.activation(
                sb["wab"][:, :], sb["wst"][:, k * 512:(k + 1) * 512], AF.Abs),
                deps=[(sy, v_w), rel_wab])
            v_g = TE.op(lambda e, k=k, s=(k == 0), t=(k == ND - 1): e.matmul(
                sb["psG"][:, :], sb["onesc"][:, :], sb["wab"][:, :], start=s, stop=t),
                deps=[(ac, v_ab), (ve, v_ones)] + ([rel_psG] if k == 0 else []), inc=1)
            rel_wab = (pe, v_g)
        v_gr = VE.op(lambda e: e.tensor_scalar(sb["grow"][:, :], sb["psG"][:, :],
                                               float(np.float32(1.0 / Dm)), None, ALU.mult),
                     deps=[(pe, v_g)])
        rel_psG = (ve, v_gr)
        v_b1 = TE.op(lambda e: e.matmul(sb["psB"][:, :], sb["onesr"][:, :], sb["grow"][:, :],
                                        start=True, stop=True), deps=[(ve, v_gr)])
        v_gmb = VE.op(lambda e: e.tensor_copy(sb["gmb"][:, :], sb["psB"][:, :]),
                      deps=[(pe, v_b1)])
        v_gi = VE.op(lambda e: e.tensor_scalar(sb["grow"][:, :], sb["grow"][:, :],
                                               1e-8, None, ALU.add),
                     deps=[(ve, v_gr), (pe, v_b1)])
        v_rc = VE.op(lambda e: e.reciprocal(sb["grow"][:, :], sb["grow"][:, :]),
                     deps=[(ve, v_gi)])
        v_b2 = TE.op(lambda e: e.matmul(sb["psB"][:, :], sb["onesr"][:, :], sb["grow"][:, :],
                                        start=True, stop=True), deps=[(ve, v_rc), (ve, v_gmb)])
        v_giv = VE.op(lambda e: e.tensor_copy(sb["ginvb"][:, :], sb["psB"][:, :]),
                      deps=[(pe, v_b2)])
        v_br = SY.dma(sb["brow"][:, :], dr["b0f"][:, ncols])
        v_b3 = TE.op(lambda e: e.matmul(sb["psB"][:, :], sb["onesr"][:, :], sb["brow"][:, :],
                                        start=True, stop=True), deps=[(sy, v_br), (ve, v_giv)])
        v_bb = VE.op(lambda e: e.tensor_copy(sb["bb"][:, :], sb["psB"][:, :]),
                     deps=[(pe, v_b3)])
        v_wt = {}
        for k in range(ND):
            v_t1 = VE.op(lambda e, k=k: e.tensor_tensor(
                sb["ef"][:, :], sb["wst"][:, k * 512:(k + 1) * 512], sb["ginvb"][:, :], ALU.mult),
                deps=[(ve, v_giv), rel_ef[0]])
            v_t2 = VE.op(lambda e: e.tensor_scalar(sb["ef"][:, :], sb["ef"][:, :],
                                                   MAGIC, MAGIC, ALU.add, ALU.subtract),
                         deps=[(ve, v_t1)])
            v_t3 = VE.op(lambda e, k=k: e.tensor_scalar(sb[f"wtn{k}"][:, :], sb["ef"][:, :],
                                                        1.0, -1.0, ALU.min, ALU.max),
                         deps=[(ve, v_t2)])
            rel_ef[0] = (ve, v_t3)
            v_wt[k] = v_t3
        rel_wst = (ve, max(v_wt.values()))
        v_wq = max(v_wt.values())
        for t128 in range(NT):
            cols = slice(t128 * 128, (t128 + 1) * 128)
            psF = sb[f"psF{t128 % 2}"]
            fns = []
            for k in range(ND):
                fns.append(lambda e, k=k, t128=t128, psF=psF: e.matmul(
                    psF[:, :], sb[f"xqT{k}"][:, t128 * 128:(t128 + 1) * 128],
                    sb[f"wtn{k}"][:, :], start=(k == 0), stop=(k == ND - 1)))
            v_mm = TE.chain(fns, deps=[(ve, v_wq), (ve, v_xq_done), rel_psF[t128 % 2]])
            v_e1 = SC.op(lambda e, t128=t128, psF=psF: e.activation(
                sb["ef"][:, :], psF[:, :], AF.Identity, scale=sb["sxall"][:, t128:t128 + 1]),
                deps=[(pe, v_mm), rel_ef[0]])
            rel_psF[t128 % 2] = (ac, v_e1)
            v_sl = SY.dma(sb["sl1"][:, :], dr["S1"][cols, ncols], deps=[rel_sl1])
            v_e2 = VE.op(lambda e: e.tensor_tensor(sb["e2"][:, :], sb["ef"][:, :],
                                                   sb["gmb"][:, :], ALU.mult),
                         deps=[(ac, v_e1), (ve, v_bb)])
            rel_ef[0] = (ve, v_e2)
            v_e3 = VE.op(lambda e: e.tensor_tensor(sb["e2"][:, :], sb["e2"][:, :],
                                                   sb["bb"][:, :], ALU.add),
                         deps=[(ve, v_e2)])
            v_e4 = VE.op(lambda e: e.tensor_tensor(sb["s2t"][:, :], sb["e2"][:, :],
                                                   sb["sl1"][:, :], ALU.add),
                         deps=[(ve, v_e3), (sy, v_sl), rel_s2])
            rel_sl1 = (ve, v_e4)
            v_st = SY.dma(dr["S2"][cols, ncols], sb["s2t"][:, :], deps=[(ve, v_e4)])
            rel_s2 = (sy, v_st)
    SY.wait(sy, sy.v)
    GD.wait(gd, gd.v)


TEXT = T + 128
DI = 4096
DM1 = 8192
DH1 = 5632
NH1 = DH1 // 128

def build_C(tab):
    nc = bass.Bass("TRN2", target_bir_lowering=False)
    dr = {}
    def din(name, shape, dt):
        dr[name] = nc.dram_tensor(name, shape, dt, kind="ExternalInput").ap()
    def dout(name, shape, dt):
        dr[name] = nc.dram_tensor(name, shape, dt, kind="ExternalOutput").ap()
    def dint(name, shape, dt):
        dr[name] = nc.dram_tensor(name, shape, dt, kind="Internal").ap()
    din("S2", [T, Dm], F32)
    din("halo", [3, Dm], F32)
    din("winT", [Dm, DM1], F32)
    din("convw", [DI, 4], F32)
    din("convb", [DI, 1], F32)
    din("woutT", [DI, Dm], F32)
    din("wm1T", [Dm, DM1], F32)
    din("wm2T", [DM1, Dm], F32)
    din("w1T", [Dm, DH1], F32)
    din("w3T", [Dm, DH1], F32)
    din("w2T", [DH1, Dm], F32)
    din("ident", [128, 128], BF16)
    dout("outp", [T, Dm], F32)
    dint("Z0", [T, Dm], F32)
    dint("Z1", [T, Dm], F32)
    dint("S3", [T, Dm], F32)
    dint("Y", [NT, DI, 128], BF16)
    dint("R", [NT, DM1, 128], BF16)
    dint("Hd", [T, DH1], F32)
    dint("HqT", [NT, DH1, 128], BF16)

    with contextlib.ExitStack() as st:
        sb = {}
        def alloc(name, shape, dt):
            sb[name] = st.enter_context(nc.sbuf_tensor(f"c5_{name}", shape, dt))
        def palloc(name, shape, dt):
            sb[name] = st.enter_context(nc.psum_tensor(f"c5_{name}", shape, dt))
        for k in range(ND):
            alloc(f"hT{k}", [128, TEXT], BF16)
        alloc("s2t", [128, Dm], F32)
        alloc("xscr", [128, Dm], F32)
        alloc("z0t", [128, Dm], F32)
        alloc("hb", [128, Dm], BF16)
        alloc("stats", [128, 8], F32)
        alloc("c_ident", [128, 128], BF16)
        alloc("wb0", [128, 16 * 128], BF16); alloc("wb1", [128, 16 * 128], BF16)
        alloc("wz0", [128, 16 * 128], BF16); alloc("wz1", [128, 16 * 128], BF16)
        alloc("xps", [128, TEXT], F32)
        alloc("sz", [128, T], F32)
        alloc("cacc", [128, T], F32)
        alloc("yb", [128, T], BF16)
        alloc("cw", [128, 4], F32)
        alloc("cb", [128, 1], F32)
        palloc("pp0", [128, 512], F32); palloc("pp1", [128, 512], F32)
        palloc("pt0", [128, 128], BF16); palloc("pt1", [128, 128], BF16)
        ph = Phase(nc, st, "C5")
        emit_C5ab(ph, dr, sb, tab)
        ph.emit()

    with contextlib.ExitStack() as st:
        sb = {}
        def alloc(name, shape, dt):
            sb[name] = st.enter_context(nc.sbuf_tensor(f"c5c_{name}", shape, dt))
        def palloc(name, shape, dt):
            sb[name] = st.enter_context(nc.psum_tensor(f"c5c_{name}", shape, dt))
        alloc("wob", [128, 32 * 512], BF16)
        alloc("yc0", [128, 32 * 128], BF16); alloc("yc1", [128, 32 * 128], BF16)
        alloc("z0s", [128, 512], F32)
        alloc("z1s", [128, 512], F32)
        palloc("pp0", [128, 512], F32); palloc("pp1", [128, 512], F32)
        ph = Phase(nc, st, "C5c")
        emit_down(ph, dr, sb, "woutT", DI, "Y", "Z0", "Z1", None)
        ph.emit()

    with contextlib.ExitStack() as st:
        sb = {}
        def alloc(name, shape, dt):
            sb[name] = st.enter_context(nc.sbuf_tensor(f"c6_{name}", shape, dt))
        def palloc(name, shape, dt):
            sb[name] = st.enter_context(nc.psum_tensor(f"c6_{name}", shape, dt))
        for k in range(ND):
            alloc(f"hT{k}", [128, T], BF16)
        alloc("z1t", [128, Dm], F32)
        alloc("hb", [128, Dm], BF16)
        alloc("stats", [128, 8], F32)
        alloc("xscr", [128, Dm], F32)
        alloc("c_ident", [128, 128], BF16)
        alloc("wb0", [128, 16 * 128], BF16); alloc("wb1", [128, 16 * 128], BF16)
        alloc("rb", [128, T], BF16)
        palloc("pp0", [128, 512], F32); palloc("pp1", [128, 512], F32)
        palloc("pt0", [128, 128], BF16); palloc("pt1", [128, 128], BF16)
        ph = Phase(nc, st, "C6")
        emit_C6ab(ph, dr, sb, tab)
        ph.emit()

    with contextlib.ExitStack() as st:
        sb = {}
        def alloc(name, shape, dt):
            sb[name] = st.enter_context(nc.sbuf_tensor(f"c6c_{name}", shape, dt))
        def palloc(name, shape, dt):
            sb[name] = st.enter_context(nc.psum_tensor(f"c6c_{name}", shape, dt))
        alloc("wob", [128, 64 * 512], BF16)
        alloc("yc0", [128, 64 * 128], BF16); alloc("yc1", [128, 64 * 128], BF16)
        alloc("z0s", [128, 512], F32)
        alloc("z1s", [128, 512], F32)
        alloc("s2s", [128, 512], F32)
        palloc("pp0", [128, 512], F32); palloc("pp1", [128, 512], F32)
        ph = Phase(nc, st, "C6c")
        emit_down(ph, dr, sb, "wm2T", DM1, "R", "Z1", "S3", "S2")
        ph.emit()

    with contextlib.ExitStack() as st:
        sb = {}
        def alloc(name, shape, dt):
            sb[name] = st.enter_context(nc.sbuf_tensor(f"c7_{name}", shape, dt))
        def palloc(name, shape, dt):
            sb[name] = st.enter_context(nc.psum_tensor(f"c7_{name}", shape, dt))
        for k in range(ND):
            alloc(f"xqT{k}", [128, T], BF16)
        alloc("wst", [128, 16 * 512], F32)
        for k in range(ND):
            alloc(f"w1q{k}", [128, 512], BF16)
            alloc(f"w3q{k}", [128, 512], BF16)
        alloc("s3t", [128, Dm], F32)
        alloc("xscr", [128, Dm], F32)
        alloc("xn1", [128, Dm], F32)
        alloc("xqb", [128, Dm], BF16)
        alloc("stats", [128, 8], F32)
        alloc("sxall", [128, NT], F32)
        alloc("hmx", [128, NT], F32)
        alloc("dq1", [128, NT], F32)
        alloc("dq3", [128, NT], F32)
        alloc("wabs", [128, 512], BF16)
        alloc("onesb", [128, 1], BF16)
        alloc("onesr", [1, 128], F32)
        alloc("wsums", [1, 512], F32)
        alloc("wtot", [1, 8], F32)
        alloc("swb", [128, 8], F32)
        alloc("s1f", [128, 512], F32)
        alloc("s3f", [128, 512], F32)
        alloc("hf", [128, 512], F32)
        alloc("qs1", [128, 512], F32)
        alloc("qs2", [128, 512], F32)
        alloc("magicc", [128, 1], F32)
        alloc("tmp1", [128, 1], F32)
        alloc("c_ident", [128, 128], BF16)
        palloc("psW", [1, 512], F32)
        palloc("psB", [128, 8], F32)
        palloc("pF10", [128, 512], F32); palloc("pF11", [128, 512], F32)
        palloc("pF30", [128, 512], F32); palloc("pF31", [128, 512], F32)
        palloc("pt0", [128, 128], BF16); palloc("pt1", [128, 128], BF16)
        ph = Phase(nc, st, "C7")
        emit_C7ab(ph, dr, sb, tab)
        ph.emit()

    with contextlib.ExitStack() as st:
        sb = {}
        def alloc(name, shape, dt):
            sb[name] = st.enter_context(nc.sbuf_tensor(f"c7c_{name}", shape, dt))
        def palloc(name, shape, dt):
            sb[name] = st.enter_context(nc.psum_tensor(f"c7c_{name}", shape, dt))
        alloc("hrow", [128, DH1], F32)
        alloc("hqb", [128, DH1], BF16)
        alloc("hqt", [128, NH1 * 128], BF16)
        alloc("hq0", [128, NH1 * 128], BF16); alloc("hq1", [128, NH1 * 128], BF16)
        alloc("wst", [128, 512], F32)
        for k in range(NH1):
            alloc(f"w2q{k}", [128, 512], BF16)
        alloc("shc", [128, NT], F32)
        alloc("dqh", [128, NT], F32)
        alloc("s3s", [128, 512], F32)
        alloc("of", [128, 512], F32)
        alloc("oo", [128, 512], F32)
        alloc("qs1", [128, 512], F32)
        alloc("qs2", [128, 512], F32)
        alloc("magicc", [128, 1], F32)
        alloc("swc", [128, 8], F32)
        alloc("c_ident", [128, 128], BF16)
        palloc("pp0", [128, 512], F32); palloc("pp1", [128, 512], F32)
        palloc("pt0", [128, 128], BF16); palloc("pt1", [128, 128], BF16)
        ph = Phase(nc, st, "C7c")
        emit_C7c(ph, dr, sb, tab)
        ph.emit()
    return nc

def emit_rms_scale(ph, sb, src, stats, deps):
    SC, VE = ph.SC, ph.VE
    ac, ve = ph.ctrs["ac"], ph.ctrs["ve"]
    v_sq = SC.op(lambda e: e.activation(sb["xscr"][:, :], src[:, :], AF.Square,
                                        accum_out=sb["stats"][:, 1:2]), deps=deps)
    v_s = SC.op(lambda e: e.activation(sb["stats"][:, 2:3], sb["stats"][:, 1:2], AF.Sqrt))
    v_m = VE.op(lambda e: e.tensor_scalar(sb["stats"][:, 3:4], sb["stats"][:, 2:3],
                                          1e-12, None, ALU.max), deps=[(ac, v_s)])
    v_r = VE.op(lambda e: e.reciprocal(sb["stats"][:, 3:4], sb["stats"][:, 3:4]),
                deps=[(ve, v_m)])
    v_f = VE.op(lambda e: e.tensor_scalar(sb["stats"][:, 3:4], sb["stats"][:, 3:4],
                                          float(np.float32(np.sqrt(np.float32(Dm)))), None,
                                          ALU.mult), deps=[(ve, v_r)])
    return v_f

def emit_C5ab(ph, dr, sb, tab):
    TE, SC, VE, SY, GD = ph.TE, ph.SC, ph.VE, ph.SY, ph.GD
    pe, ac, ve, sy, gd = (ph.ctrs[n] for n in ("pe", "ac", "ve", "sy", "gd"))
    v_id = SY.dma(sb["c_ident"][:, :], dr["ident"])

    rel_s2t = (ac, 0)
    rel_z0 = []
    rel_hb = (pe, 0)
    rel_pt = [(ve, 0), (ve, 0)]
    for it in range(NT + 1):
        tcol = it * 128
        if it == 0:
            v_ms = VE.op(lambda e: e.memset(sb["s2t"][:, :], 0.0), deps=[rel_s2t])
            v_x = SY.dma(sb["s2t"][125:128, :], dr["halo"], deps=[(ve, v_ms)])
        else:
            t128 = it - 1
            v_x = SY.dma(sb["s2t"][:, :], dr["S2"][t128 * 128:(t128 + 1) * 128, :],
                         deps=[rel_s2t])
        v_ln = emit_ln(ph, sb["s2t"], sb["xscr"], sb["stats"], 1e-5,
                       deps=[(sy, v_x)])
        v_z0 = SC.op(lambda e: e.activation(sb["z0t"][:, :], sb["s2t"][:, :], AF.Identity,
                                            scale=sb["stats"][:, 6:7], bias=sb["stats"][:, 7:8]),
                     deps=[(ve, v_ln)] + rel_z0)
        rel_s2t = (ac, v_z0)
        if it > 0:
            v_w0 = SY.dma(dr["Z0"][(it - 1) * 128:it * 128, :], sb["z0t"][:, :],
                          deps=[(ac, v_z0)])
        v_rs = emit_rms_scale(ph, sb, sb["z0t"], sb["stats"], deps=[(ac, v_z0)])
        v_hb = VE.op(lambda e: e.tensor_scalar(sb["hb"][:, :], sb["z0t"][:, :],
                                               sb["stats"][:, 3:4], None, ALU.mult),
                     deps=[(ve, v_rs), rel_hb])
        rel_z0 = [(ve, v_hb)] if it == 0 else [(ve, v_hb), (sy, v_w0)]
        for k in range(ND):
            pt = sb[f"pt{k % 2}"]
            v_tr = TE.op(lambda e, k=k, pt=pt: e.transpose(
                pt[:, :], sb["hb"][:, k * 128:(k + 1) * 128], sb["c_ident"][:, :]),
                deps=[(ve, v_hb), (sy, v_id), rel_pt[k % 2]])
            v_ev = VE.op(lambda e, k=k, tcol=tcol, pt=pt: e.tensor_copy(
                sb[f"hT{k}"][:, tcol:tcol + 128], pt[:, :]),
                deps=[(pe, v_tr)])
            rel_pt[k % 2] = (ve, v_ev)
        rel_hb = (pe, pe.v)
    v_hT = ve.v

    rel_wb = [(pe, 0), (pe, 0)]
    rel_wz = [(pe, 0), (pe, 0)]
    rel_pp = [(ac, 0), (ac, 0)]
    rel_xps = (ve, 0)
    rel_sz = (ve, 0)
    rel_cacc = (ac, 0)
    rel_xscr5 = (ve, 0)
    rel_yb = (sy, 0)
    tblocks = [(0, 512), (512, 512), (1024, 512), (1536, 512), (2048, 128)]
    for ep in range(DI // 128):
        slot = ep % 2
        v_wx = GD.dma(sb[f"wb{slot}"][:, :].rearrange("p (c j) -> p c j", j=128),
                      dr["winT"][:, ep * 128:(ep + 1) * 128].rearrange("(c p) j -> p c j", p=128),
                      deps=[rel_wb[slot]])
        v_wz = GD.dma(sb[f"wz{slot}"][:, :].rearrange("p (c j) -> p c j", j=128),
                      dr["winT"][:, DI + ep * 128: DI + (ep + 1) * 128].rearrange("(c p) j -> p c j", p=128),
                      deps=[rel_wz[slot]])
        v_cw = SY.dma(sb["cw"][:, :], dr["convw"][ep * 128:(ep + 1) * 128, :])
        v_cb = SY.dma(sb["cb"][:, :], dr["convb"][ep * 128:(ep + 1) * 128, :])
        v_xp = 0
        for (c0, cw_) in tblocks:
            pp = sb[f"pp{(c0 // 512) % 2}"]
            fns = []
            for k in range(ND):
                fns.append(lambda e, k=k, c0=c0, cw_=cw_, pp=pp, slot=slot: e.matmul(
                    pp[:, 0:cw_], sb[f"wb{slot}"][:, k * 128:(k + 1) * 128],
                    sb[f"hT{k}"][:, c0:c0 + cw_], start=(k == 0), stop=(k == ND - 1)))
            v_mm = TE.chain(fns, deps=[(gd, v_wx), (ve, v_hT), rel_pp[(c0 // 512) % 2]])
            v_xp = SC.op(lambda e, c0=c0, cw_=cw_, pp=pp: e.activation(
                sb["xps"][:, c0:c0 + cw_], pp[:, 0:cw_], AF.Identity),
                deps=[(pe, v_mm), rel_xps])
            rel_pp[(c0 // 512) % 2] = (ac, v_xp)
        v_sz = 0
        for c0 in range(0, T, 512):
            pp = sb[f"pp{(c0 // 512) % 2}"]
            fns = []
            for k in range(ND):
                fns.append(lambda e, k=k, c0=c0, pp=pp, slot=slot: e.matmul(
                    pp[:, :], sb[f"wz{slot}"][:, k * 128:(k + 1) * 128],
                    sb[f"hT{k}"][:, 128 + c0:128 + c0 + 512], start=(k == 0), stop=(k == ND - 1)))
            v_mm = TE.chain(fns, deps=[(gd, v_wz), (ve, v_hT), rel_pp[(c0 // 512) % 2]])
            v_sz = SC.op(lambda e, c0=c0, pp=pp: e.activation(
                sb["sz"][:, c0:c0 + 512], pp[:, :], AF.Silu),
                deps=[(pe, v_mm), rel_sz])
            rel_pp[(c0 // 512) % 2] = (ac, v_sz)
        rel_wb[slot] = (pe, pe.v)
        rel_wz[slot] = (pe, pe.v)
        v_c0 = VE.op(lambda e: e.tensor_scalar(sb["cacc"][:, :], sb["xps"][:, 125:125 + T],
                                               sb["cw"][:, 0:1], None, ALU.mult),
                     deps=[(ac, v_xp), (sy, v_cw), rel_cacc])
        v_c1 = VE.op(lambda e: e.scalar_tensor_tensor(sb["cacc"][:, :], sb["xps"][:, 126:126 + T],
                                                      sb["cw"][:, 1:2], sb["cacc"][:, :],
                                                      ALU.mult, ALU.add), deps=[(ve, v_c0)])
        v_c2 = VE.op(lambda e: e.scalar_tensor_tensor(sb["cacc"][:, :], sb["xps"][:, 127:127 + T],
                                                      sb["cw"][:, 2:3], sb["cacc"][:, :],
                                                      ALU.mult, ALU.add), deps=[(ve, v_c1)])
        v_c3 = VE.op(lambda e: e.scalar_tensor_tensor(sb["cacc"][:, :], sb["xps"][:, 128:128 + T],
                                                      sb["cw"][:, 3:4], sb["cacc"][:, :],
                                                      ALU.mult, ALU.add), deps=[(ve, v_c2)])
        rel_xps = (ve, v_c3)
        v_sx = SC.op(lambda e: e.activation(sb["xscr"][:, 0:T], sb["cacc"][:, :], AF.Silu,
                                            bias=sb["cb"][:, 0:1]),
                     deps=[(ve, v_c3), (sy, v_cb), rel_cacc, rel_xscr5])
        rel_cacc = (ac, v_sx)
        v_y = VE.op(lambda e: e.tensor_tensor(sb["yb"][:, :], sb["xscr"][:, 0:T],
                                              sb["sz"][:, :], ALU.mult),
                    deps=[(ac, v_sx), (ac, v_sz), rel_yb])
        rel_sz = (ve, v_y)
        rel_xscr5 = (ve, v_y)
        v_st = SY.dma(dr["Y"][:, ep * 128:(ep + 1) * 128, :].rearrange("tb p t -> p tb t"),
                      sb["yb"][:, :].rearrange("p (tb t) -> p tb t", t=128),
                      deps=[(ve, v_y)])
        rel_yb = (sy, v_st)
    SY.wait(sy, sy.v)
    GD.wait(gd, gd.v)

def emit_down(ph, dr, sb, wname, E, yname, resname, outname, res2name):
    TE, SC, VE, SY, GD = ph.TE, ph.SC, ph.VE, ph.SY, ph.GD
    pe, ac, ve, sy, gd = (ph.ctrs[n] for n in ("pe", "ac", "ve", "sy", "gd"))
    NE = E // 128
    rel_wob = (pe, 0)
    rel_yc = [(pe, 0), (pe, 0)]
    rel_pp = [(ve, 0), (ve, 0)]
    rel_z0 = (ve, 0)
    rel_z1 = (sy, 0)
    rel_s2 = (ve, 0)
    for dblk in range(4):
        dcols = slice(dblk * 512, (dblk + 1) * 512)
        v_w = GD.dma(sb["wob"][:, :].rearrange("p (c j) -> p c j", j=512),
                     dr[wname][:, dcols].rearrange("(c p) j -> p c j", p=128),
                     deps=[rel_wob])
        for t128 in range(NT):
            cols = slice(t128 * 128, (t128 + 1) * 128)
            slot = t128 % 2
            yc = sb[f"yc{slot}"]
            v_y = SY.dma(yc[:, :].rearrange("p (c t) -> p c t", t=128),
                         dr[yname][t128].rearrange("(c p) t -> p c t", p=128),
                         deps=[rel_yc[slot]])
            pp = sb[f"pp{slot}"]
            fns = []
            for k in range(NE):
                fns.append(lambda e, k=k, pp=pp, yc=yc: e.matmul(
                    pp[:, :], yc[:, k * 128:(k + 1) * 128],
                    sb["wob"][:, k * 512:(k + 1) * 512], start=(k == 0), stop=(k == NE - 1)))
            v_mm = TE.chain(fns, deps=[(gd, v_w), (sy, v_y), rel_pp[slot]])
            rel_yc[slot] = (pe, v_mm)
            v_z0 = SY.dma(sb["z0s"][:, :], dr[resname][cols, dcols], deps=[rel_z0])
            v_a = VE.op(lambda e, pp=pp: e.tensor_tensor(sb["z1s"][:, :], pp[:, :],
                                                         sb["z0s"][:, :], ALU.add),
                        deps=[(pe, v_mm), (sy, v_z0), rel_z1])
            rel_pp[slot] = (ve, v_a)
            rel_z0 = (ve, v_a)
            if res2name is not None:
                v_s2 = SY.dma(sb["s2s"][:, :], dr[res2name][cols, dcols], deps=[rel_s2])
                v_a = VE.op(lambda e: e.tensor_tensor(sb["z1s"][:, :], sb["z1s"][:, :],
                                                      sb["s2s"][:, :], ALU.add),
                            deps=[(ve, v_a), (sy, v_s2)])
                rel_s2 = (ve, v_a)
            v_st = SY.dma(dr[outname][cols, dcols], sb["z1s"][:, :], deps=[(ve, v_a)])
            rel_z1 = (sy, v_st)
        rel_wob = (pe, pe.v)
    SY.wait(sy, sy.v)
    GD.wait(gd, gd.v)

def emit_C6ab(ph, dr, sb, tab):
    TE, SC, VE, SY, GD = ph.TE, ph.SC, ph.VE, ph.SY, ph.GD
    pe, ac, ve, sy, gd = (ph.ctrs[n] for n in ("pe", "ac", "ve", "sy", "gd"))
    v_id = SY.dma(sb["c_ident"][:, :], dr["ident"])

    rel_z1t = (ac, 0)
    rel_hb = (pe, 0)
    rel_pt = [(ve, 0), (ve, 0)]
    for t128 in range(NT):
        v_x = SY.dma(sb["z1t"][:, :], dr["Z1"][t128 * 128:(t128 + 1) * 128, :],
                     deps=[rel_z1t])
        v_rs = emit_rms_scale(ph, sb, sb["z1t"], sb["stats"], deps=[(sy, v_x)])
        v_hb = VE.op(lambda e: e.tensor_scalar(sb["hb"][:, :], sb["z1t"][:, :],
                                               sb["stats"][:, 3:4], None, ALU.mult),
                     deps=[(ve, v_rs), rel_hb])
        rel_z1t = (ve, v_hb)
        for k in range(ND):
            pt = sb[f"pt{k % 2}"]
            v_tr = TE.op(lambda e, k=k, pt=pt: e.transpose(
                pt[:, :], sb["hb"][:, k * 128:(k + 1) * 128], sb["c_ident"][:, :]),
                deps=[(ve, v_hb), (sy, v_id), rel_pt[k % 2]])
            v_ev = VE.op(lambda e, k=k, t128=t128, pt=pt: e.tensor_copy(
                sb[f"hT{k}"][:, t128 * 128:(t128 + 1) * 128], pt[:, :]),
                deps=[(pe, v_tr)])
            rel_pt[k % 2] = (ve, v_ev)
        rel_hb = (pe, pe.v)
    v_hT = ve.v

    rel_wb = [(pe, 0), (pe, 0)]
    rel_pp = [(ac, 0), (ac, 0)]
    rel_rb = (sy, 0)
    for ec in range(DM1 // 128):
        slot = ec % 2
        v_w = GD.dma(sb[f"wb{slot}"][:, :].rearrange("p (c j) -> p c j", j=128),
                     dr["wm1T"][:, ec * 128:(ec + 1) * 128].rearrange("(c p) j -> p c j", p=128),
                     deps=[rel_wb[slot]])
        v_r = 0
        for c0 in range(0, T, 512):
            pp = sb[f"pp{(c0 // 512) % 2}"]
            fns = []
            for k in range(ND):
                fns.append(lambda e, k=k, c0=c0, pp=pp, slot=slot: e.matmul(
                    pp[:, :], sb[f"wb{slot}"][:, k * 128:(k + 1) * 128],
                    sb[f"hT{k}"][:, c0:c0 + 512], start=(k == 0), stop=(k == ND - 1)))
            v_mm = TE.chain(fns, deps=[(gd, v_w), (ve, v_hT), rel_pp[(c0 // 512) % 2]])
            v_r = SC.op(lambda e, c0=c0, pp=pp: e.activation(
                sb["rb"][:, c0:c0 + 512], pp[:, :], AF.Relu),
                deps=[(pe, v_mm), rel_rb])
            rel_pp[(c0 // 512) % 2] = (ac, v_r)
        rel_wb[slot] = (pe, pe.v)
        v_st = SY.dma(dr["R"][:, ec * 128:(ec + 1) * 128, :].rearrange("tb p t -> p tb t"),
                      sb["rb"][:, :].rearrange("p (tb t) -> p tb t", t=128),
                      deps=[(ac, v_r)])
        rel_rb = (sy, v_st)
    SY.wait(sy, sy.v)
    GD.wait(gd, gd.v)

def emit_C7ab(ph, dr, sb, tab):
    TE, SC, VE, SY, GD = ph.TE, ph.SC, ph.VE, ph.SY, ph.GD
    pe, ac, ve, sy, gd = (ph.ctrs[n] for n in ("pe", "ac", "ve", "sy", "gd"))
    v_id = SY.dma(sb["c_ident"][:, :], dr["ident"])
    v_ones = VE.op(lambda e: e.memset(sb["onesb"][:, :], 1.0))
    v_ones = VE.op(lambda e: e.memset(sb["onesr"][:, :], 1.0))
    v_w0 = VE.op(lambda e: e.memset(sb["wtot"][:, :], 0.0))
    v_mg = VE.op(lambda e: e.memset(sb["magicc"][:, :], MAGIC))

    rel_wst = (ve, 0)
    rel_wabs = (pe, 0)
    rel_psW = (ve, 0)
    wmeta = [("w1T", ND, DH1, 0), ("w3T", ND, DH1, 2), ("w2T", NH1, Dm, 4)]
    for (wn, NK, NCOL, coli) in wmeta:
        NB = NCOL // 512
        ntile = NK * NB
        ti = 0
        v_g = 0
        for nb in range(NB):
            for k in range(NK):
                v_w = SY.dma(sb["wst"][:, 0:512], dr[wn][k * 128:(k + 1) * 128,
                                                         nb * 512:(nb + 1) * 512],
                             deps=[rel_wst])
                v_ab = SC.op(lambda e: e.activation(
                    sb["wabs"][:, :], sb["wst"][:, 0:512], AF.Abs),
                    deps=[(sy, v_w), rel_wabs])
                rel_wst = (ac, v_ab)
                v_g = TE.op(lambda e, s=(ti == 0), t=(ti == ntile - 1): e.matmul(
                    sb["psW"][:, :], sb["onesb"][:, :], sb["wabs"][:, :], start=s, stop=t),
                    deps=[(ac, v_ab), (ve, v_ones)] + ([rel_psW] if ti == 0 else []))
                rel_wabs = (pe, v_g)
                ti += 1
        v_rs = VE.op(lambda e, coli=coli: e.reduce_sum(
            sb["wtot"][:, coli:coli + 1], sb["psW"][:, :], axis=AX.X),
            deps=[(pe, v_g)])
        rel_psW = (ve, v_rs)
        v_m = VE.op(lambda e, coli=coli: e.tensor_scalar(
            sb["wtot"][:, coli + 1:coli + 2], sb["wtot"][:, coli:coli + 1],
            float(np.float32(1.0 / (DH1 * Dm))), 1e-5,
            ALU.mult, ALU.max), deps=[(ve, v_rs)])
        v_r = VE.op(lambda e, coli=coli: e.reciprocal(
            sb["wtot"][:, coli:coli + 1], sb["wtot"][:, coli + 1:coli + 2]),
            deps=[(ve, v_m)])
    v_bc = TE.op(lambda e: e.matmul(sb["psB"][:, :], sb["onesr"][:, :], sb["wtot"][:, :],
                                    start=True, stop=True), deps=[(ve, ve.v)])
    v_swb = VE.op(lambda e: e.tensor_copy(sb["swb"][:, :], sb["psB"][:, :]),
                  deps=[(pe, v_bc)])

    rel_s3t = (ac, 0)
    rel_xn1 = (ve, 0)
    rel_xscr = (ve, 0)
    rel_xqb = (pe, 0)
    rel_pt = [(ve, 0), (ve, 0)]
    for t128 in range(NT):
        cols = slice(t128 * 128, (t128 + 1) * 128)
        v_x = SY.dma(sb["s3t"][:, :], dr["S3"][cols, :], deps=[rel_s3t])
        v_ln = emit_ln(ph, sb["s3t"], sb["xscr"], sb["stats"], 1e-5,
                       deps=[(sy, v_x), rel_xscr])
        v_n1 = SC.op(lambda e: e.activation(sb["xn1"][:, :], sb["s3t"][:, :], AF.Identity,
                                            scale=sb["stats"][:, 6:7], bias=sb["stats"][:, 7:8]),
                     deps=[(ve, v_ln), rel_xn1])
        rel_s3t = (ac, v_n1)
        v_mx = VE.op(lambda e: e.reduce_max(sb["stats"][:, 4:5], sb["xn1"][:, :], axis=AX.X,
                                            apply_absolute_value=True),
                     deps=[(ac, v_n1)])
        v_c = VE.op(lambda e: e.tensor_scalar(sb["stats"][:, 4:5], sb["stats"][:, 4:5],
                                              1e-5, None, ALU.max), deps=[(ve, v_mx)])
        v_dq = VE.op(lambda e, t128=t128: e.tensor_scalar(
            sb["sxall"][:, t128:t128 + 1], sb["stats"][:, 4:5],
            float(np.float32(1.0 / 127.0)), None, ALU.mult), deps=[(ve, v_c)])
        v_r = VE.op(lambda e: e.reciprocal(sb["stats"][:, 5:6], sb["stats"][:, 4:5]),
                    deps=[(ve, v_dq)])
        v_s = VE.op(lambda e: e.tensor_scalar(sb["stats"][:, 5:6], sb["stats"][:, 5:6],
                                              127.0, None, ALU.mult), deps=[(ve, v_r)])
        v_q1 = VE.op(lambda e: e.tensor_scalar(sb["xscr"][:, :], sb["xn1"][:, :],
                                               sb["stats"][:, 5:6], MAGIC, ALU.mult, ALU.add),
                     deps=[(ve, v_s)])
        v_q2 = VE.op(lambda e: e.tensor_scalar(sb["xqb"][:, :], sb["xscr"][:, :],
                                               MAGIC, None, ALU.subtract),
                     deps=[(ve, v_q1), rel_xqb])
        rel_xn1 = (ve, v_q2)
        rel_xscr = (ve, v_q2)
        for k in range(ND):
            pt = sb[f"pt{k % 2}"]
            v_tr = TE.op(lambda e, k=k, pt=pt: e.transpose(
                pt[:, :], sb["xqb"][:, k * 128:(k + 1) * 128], sb["c_ident"][:, :]),
                deps=[(ve, v_q2), (sy, v_id), rel_pt[k % 2]])
            v_ev = VE.op(lambda e, k=k, t128=t128, pt=pt: e.tensor_copy(
                sb[f"xqT{k}"][:, t128 * 128:(t128 + 1) * 128], pt[:, :]),
                deps=[(pe, v_tr)])
            rel_pt[k % 2] = (ve, v_ev)
        rel_xqb = (pe, pe.v)
    v_xq = ve.v
    v_d1 = VE.op(lambda e: e.tensor_scalar(sb["dq1"][:, :], sb["sxall"][:, :],
                                           sb["swb"][:, 1:2], None, ALU.mult),
                 deps=[(ve, v_swb), (ve, v_xq)])
    v_d3 = VE.op(lambda e: e.tensor_scalar(sb["dq3"][:, :], sb["sxall"][:, :],
                                           sb["swb"][:, 3:4], None, ALU.mult))
    v_hm = VE.op(lambda e: e.memset(sb["hmx"][:, :], 0.0))

    rel_wst2 = (ac, 0)
    rel_pF1 = [(ac, 0), (ac, 0)]
    rel_pF3 = [(ac, 0), (ac, 0)]
    rel_s1f = (ve, 0)
    rel_s3f = (ve, 0)
    rel_hf = (sy, 0)
    for nblk in range(DH1 // 512):
        ncols = slice(nblk * 512, (nblk + 1) * 512)
        for (wn, qpre, swcol) in (("w1T", "w1q", 0), ("w3T", "w3q", 2)):
            v_w = SY.dma(sb["wst"][:, :].rearrange("p (c j) -> p c j", j=512),
                         dr[wn][:, ncols].rearrange("(c p) j -> p c j", p=128),
                         deps=[rel_wst2])
            v_l = 0
            rel_qs = (ve, 0)
            for k in range(ND):
                v_t1 = SC.op(lambda e, k=k, swcol=swcol: e.activation(
                    sb["qs1"][:, :], sb["wst"][:, k * 512:(k + 1) * 512], AF.Identity,
                    scale=sb["swb"][:, swcol:swcol + 1], bias=sb["magicc"][:, 0:1]),
                    deps=[(sy, v_w), (ve, v_swb), (ve, v_xq), (ve, v_mg), rel_qs])
                v_t2 = VE.op(lambda e: e.tensor_scalar(sb["qs2"][:, :], sb["qs1"][:, :],
                                                       MAGIC, 1.0, ALU.subtract, ALU.min),
                             deps=[(ac, v_t1)])
                v_l = VE.op(lambda e, k=k, qpre=qpre: e.tensor_scalar(
                    sb[f"{qpre}{k}"][:, :], sb["qs2"][:, :], -1.0, None, ALU.max),
                    deps=[(ve, v_t2)])
                rel_qs = (ve, v_l)
            rel_wst2 = (ve, v_l)
        v_wq = ve.v
        for t128 in range(NT):
            cols = slice(t128 * 128, (t128 + 1) * 128)
            slot = t128 % 2
            pF1, pF3 = sb[f"pF1{slot}"], sb[f"pF3{slot}"]
            fns = []
            for k in range(ND):
                fns.append(lambda e, k=k, pF1=pF1, t128=t128: e.matmul(
                    pF1[:, :], sb[f"xqT{k}"][:, t128 * 128:(t128 + 1) * 128],
                    sb[f"w1q{k}"][:, :], start=(k == 0), stop=(k == ND - 1)))
            v_m1 = TE.chain(fns, deps=[(ve, v_wq), (ve, v_xq), rel_pF1[slot]])
            fns = []
            for k in range(ND):
                fns.append(lambda e, k=k, pF3=pF3, t128=t128: e.matmul(
                    pF3[:, :], sb[f"xqT{k}"][:, t128 * 128:(t128 + 1) * 128],
                    sb[f"w3q{k}"][:, :], start=(k == 0), stop=(k == ND - 1)))
            v_m3 = TE.chain(fns, deps=[rel_pF3[slot]])
            v_s1 = SC.op(lambda e, t128=t128, pF1=pF1: e.activation(
                sb["s1f"][:, :], pF1[:, :], AF.Silu, scale=sb["dq1"][:, t128:t128 + 1]),
                deps=[(pe, v_m1), (ve, v_d1), rel_s1f])
            rel_pF1[slot] = (ac, v_s1)
            v_s3 = SC.op(lambda e, t128=t128, pF3=pF3: e.activation(
                sb["s3f"][:, :], pF3[:, :], AF.Identity, scale=sb["dq3"][:, t128:t128 + 1]),
                deps=[(pe, v_m3), (ve, v_d3), rel_s3f])
            rel_pF3[slot] = (ac, v_s3)
            v_h = VE.op(lambda e: e.tensor_tensor(sb["hf"][:, :], sb["s1f"][:, :],
                                                  sb["s3f"][:, :], ALU.mult),
                        deps=[(ac, v_s1), (ac, v_s3), rel_hf])
            rel_s1f = (ve, v_h)
            rel_s3f = (ve, v_h)
            v_mx = VE.op(lambda e: e.reduce_max(sb["tmp1"][:, :], sb["hf"][:, :], axis=AX.X,
                                                apply_absolute_value=True),
                         deps=[(ve, v_h)])
            v_hm = VE.op(lambda e, t128=t128: e.tensor_tensor(
                sb["hmx"][:, t128:t128 + 1], sb["hmx"][:, t128:t128 + 1],
                sb["tmp1"][:, :], ALU.max), deps=[(ve, v_mx), (ve, v_hm)])
            v_st = SY.dma(dr["Hd"][cols, ncols], sb["hf"][:, :], deps=[(ve, v_h)])
            rel_hf = (sy, v_st)
    v_c = VE.op(lambda e: e.tensor_scalar(sb["hmx"][:, :], sb["hmx"][:, :],
                                          1e-5, None, ALU.max), deps=[(ve, ve.v)])
    v_r = VE.op(lambda e: e.reciprocal(sb["dq1"][:, :], sb["hmx"][:, :]),
                deps=[(ve, v_c), (ac, ac.v)])
    v_s = VE.op(lambda e: e.tensor_scalar(sb["dq1"][:, :], sb["dq1"][:, :],
                                          127.0, None, ALU.mult), deps=[(ve, v_r)])
    v_q = VE.op(lambda e: e.tensor_scalar(sb["dq3"][:, :], sb["hmx"][:, :],
                                          float(np.float32(1.0 / 127.0)), None, ALU.mult),
                deps=[(ve, v_s), (ac, ac.v)])
    v_q2 = VE.op(lambda e: e.tensor_scalar(sb["dq3"][:, :], sb["dq3"][:, :],
                                           sb["swb"][:, 5:6], None, ALU.mult),
                 deps=[(ve, v_q)])
    SY.dma(dr["Z0"][0:128, 0:NT], sb["dq1"][:, :], deps=[(ve, v_s)])
    SY.dma(dr["Z0"][128:256, 0:NT], sb["dq3"][:, :], deps=[(ve, v_q2)])
    SY.dma(dr["Z0"][256:384, 0:8], sb["swb"][:, :], deps=[(ve, v_swb)])
    SY.wait(sy, sy.v)
    GD.wait(gd, gd.v)

def emit_C7c(ph, dr, sb, tab):
    TE, SC, VE, SY, GD = ph.TE, ph.SC, ph.VE, ph.SY, ph.GD
    pe, ac, ve, sy, gd = (ph.ctrs[n] for n in ("pe", "ac", "ve", "sy", "gd"))
    v_id = SY.dma(sb["c_ident"][:, :], dr["ident"])
    v_sc = SY.dma(sb["shc"][:, :], dr["Z0"][0:128, 0:NT])
    v_dq = SY.dma(sb["dqh"][:, :], dr["Z0"][128:256, 0:NT])
    v_sw = SY.dma(sb["swc"][:, :], dr["Z0"][256:384, 0:8])
    v_mg = VE.op(lambda e: e.memset(sb["magicc"][:, :], MAGIC))

    rel_hrow = (ve, 0)
    rel_hqb = (pe, 0)
    rel_hqt = (sy, 0)
    rel_pt = [(ve, 0), (ve, 0)]
    for t128 in range(NT):
        cols = slice(t128 * 128, (t128 + 1) * 128)
        v_x = SY.dma(sb["hrow"][:, :], dr["Hd"][cols, :], deps=[rel_hrow])
        v_q1 = VE.op(lambda e, t128=t128: e.tensor_scalar(
            sb["hrow"][:, :], sb["hrow"][:, :], sb["shc"][:, t128:t128 + 1], MAGIC,
            ALU.mult, ALU.add), deps=[(sy, v_x), (sy, v_sc)])
        v_q2 = VE.op(lambda e: e.tensor_scalar(sb["hqb"][:, :], sb["hrow"][:, :],
                                               MAGIC, None, ALU.subtract),
                     deps=[(ve, v_q1), rel_hqb])
        rel_hrow = (ve, v_q2)
        v_tre = 0
        for k in range(NH1):
            pt = sb[f"pt{k % 2}"]
            v_tr = TE.op(lambda e, k=k, pt=pt: e.transpose(
                pt[:, :], sb["hqb"][:, k * 128:(k + 1) * 128], sb["c_ident"][:, :]),
                deps=[(ve, v_q2), (sy, v_id), rel_pt[k % 2], rel_hqt])
            v_tre = VE.op(lambda e, k=k, pt=pt: e.tensor_copy(
                sb["hqt"][:, k * 128:(k + 1) * 128], pt[:, :]),
                deps=[(pe, v_tr)])
            rel_pt[k % 2] = (ve, v_tre)
        rel_hqb = (pe, pe.v)
        v_st = SY.dma(dr["HqT"][t128].rearrange("(c p) t -> p c t", p=128),
                      sb["hqt"][:, :].rearrange("p (c t) -> p c t", t=128),
                      deps=[(ve, v_tre)])
        rel_hqt = (sy, v_st)

    v_p1 = sy.v
    rel_wst = (ac, 0)
    rel_w2q = (pe, 0)
    rel_hq = [(pe, 0), (pe, 0)]
    rel_pp = [(ac, 0), (ac, 0)]
    rel_of = (ve, 0)
    rel_oo = (sy, 0)
    rel_s3s = (ve, 0)
    for dblk in range(4):
        dcols = slice(dblk * 512, (dblk + 1) * 512)
        v_l = 0
        rel_qs = (ve, 0)
        for k in range(NH1):
            v_w = SY.dma(sb["wst"][:, :], dr["w2T"][k * 128:(k + 1) * 128, dcols],
                         deps=[rel_wst])
            v_t1 = SC.op(lambda e, k=k: e.activation(
                sb["qs1"][:, :], sb["wst"][:, :], AF.Identity,
                scale=sb["swc"][:, 4:5], bias=sb["magicc"][:, 0:1]),
                deps=[(sy, v_w), (sy, v_sw), (ve, v_mg), rel_qs, rel_w2q])
            v_t2 = VE.op(lambda e: e.tensor_scalar(sb["qs2"][:, :], sb["qs1"][:, :],
                                                   MAGIC, 1.0, ALU.subtract, ALU.min),
                         deps=[(ac, v_t1)])
            v_l = VE.op(lambda e, k=k: e.tensor_scalar(
                sb[f"w2q{k}"][:, :], sb["qs2"][:, :], -1.0, None, ALU.max),
                deps=[(ve, v_t2)])
            rel_qs = (ve, v_l)
            rel_wst = (ac, ac.v)
        v_wq = ve.v
        for t128 in range(NT):
            cols = slice(t128 * 128, (t128 + 1) * 128)
            slot = t128 % 2
            hq = sb[f"hq{slot}"]
            v_h = SY.dma(hq[:, :].rearrange("p (c t) -> p c t", t=128),
                         dr["HqT"][t128].rearrange("(c p) t -> p c t", p=128),
                         deps=[rel_hq[slot], (sy, v_p1)])
            pp = sb[f"pp{slot}"]
            fns = []
            for k in range(NH1):
                fns.append(lambda e, k=k, pp=pp, hq=hq: e.matmul(
                    pp[:, :], hq[:, k * 128:(k + 1) * 128],
                    sb[f"w2q{k}"][:, :], start=(k == 0), stop=(k == NH1 - 1)))
            v_mm = TE.chain(fns, deps=[(ve, v_wq), (sy, v_h), rel_pp[slot]])
            rel_hq[slot] = (pe, v_mm)
            v_e1 = SC.op(lambda e, t128=t128, pp=pp: e.activation(
                sb["of"][:, :], pp[:, :], AF.Identity, scale=sb["dqh"][:, t128:t128 + 1]),
                deps=[(pe, v_mm), (sy, v_dq), rel_of])
            rel_pp[slot] = (ac, v_e1)
            v_s3 = SY.dma(sb["s3s"][:, :], dr["S3"][cols, dcols], deps=[rel_s3s])
            v_o = VE.op(lambda e: e.tensor_tensor(sb["oo"][:, :], sb["of"][:, :],
                                                  sb["s3s"][:, :], ALU.add),
                        deps=[(ac, v_e1), (sy, v_s3), rel_oo])
            rel_of = (ve, v_o)
            rel_s3s = (ve, v_o)
            v_st = SY.dma(dr["outp"][cols, dcols], sb["oo"][:, :], deps=[(ve, v_o)])
            rel_oo = (sy, v_st)
        rel_w2q = (pe, pe.v)
    SY.wait(sy, sy.v)
    GD.wait(gd, gd.v)


def shard_inputs(inputs):
    x = inputs["L0_attn_x"]
    shards = []
    for core in range(NCORES):
        b, seg = divmod(core, SEGS)
        shards.append(np.ascontiguousarray(x[b, seg * T:(seg + 1) * T, :]))
    return shards

def kernel(**inputs):
    w = host_weights(inputs)
    tabs = [host_tables(ci) for ci in range(NCORES)]
    xsh = shard_inputs(inputs)
    zeros = np.zeros((128, 2048), np.float32)

    ncA = build_A(tabs[0])
    imsA = []
    for ci in range(NCORES):
        tab = tabs[ci]
        imsA.append(dict(xin=xsh[ci], wqT=w["wqT"], wkT=w["wkT"], wvT=w["wvT"],
                         diagT=tab["diagT"], kdec=tab["kdec"], qdec_bc=tab["qdec_bc"],
                         cosT=tab["cosT"], sinT=tab["sinT"], ident=tab["ident"],
                         rotm=tab["rotm"]))
    resA = run_bass_kernel_spmd(ncA, imsA, core_ids=list(range(NCORES)))

    ncB = build_B(tabs[0])
    imsB = []
    for ci in range(NCORES):
        b, seg = divmod(ci, SEGS)
        a = resA.results[ci]
        preds = [resA.results[b * SEGS + q]["Dp"] for q in range(seg)]
        preds = [zeros] * (3 - len(preds)) + preds
        imsB.append(dict(xin=xsh[ci], qf=a["qf"], oloc=a["oloc"],
                         Dpred0=preds[0], Dpred1=preds[1], Dpred2=preds[2],
                         woT=w["woT"], w0fT=w["w0fT"], b0f=w["b0f"],
                         qdec_pp=tabs[ci]["qdec_pp"], ident=tabs[ci]["ident"]))
    resB = run_bass_kernel_spmd(ncB, imsB, core_ids=list(range(NCORES)))

    ncC = build_C(tabs[0])
    imsC = []
    for ci in range(NCORES):
        b, seg = divmod(ci, SEGS)
        if seg == 0:
            halo = np.zeros((3, Dm), np.float32)
        else:
            halo = np.ascontiguousarray(resB.results[ci - 1]["S2"][T - 3:T])
        imsC.append(dict(S2=resB.results[ci]["S2"], halo=halo,
                         winT=w["winT"], convw=w["convw"], convb=w["convb"],
                         woutT=w["woutT"], wm1T=w["wm1T"], wm2T=w["wm2T"],
                         w1T=w["w1T"], w3T=w["w3T"], w2T=w["w2T"],
                         ident=tabs[ci]["ident"]))
    resC = run_bass_kernel_spmd(ncC, imsC, core_ids=list(range(NCORES)))

    out = np.empty((B, S, Dm), np.float32)
    for ci in range(NCORES):
        b, seg = divmod(ci, SEGS)
        out[b, seg * T:(seg + 1) * T, :] = resC.results[ci]["outp"]
    return out


# version 04
# speedup vs baseline: 1.1271x; 1.1271x over previous
import contextlib
import numpy as np
import ml_dtypes
import concourse.bass as bass
import concourse.mybir as mybir
from concourse.bass_utils import run_bass_kernel_spmd

F32 = mybir.dt.float32
BF16 = mybir.dt.bfloat16
AF = mybir.ActivationFunctionType
ALU = mybir.AluOpType
AX = mybir.AxisListType
BF = ml_dtypes.bfloat16

MAGIC = float(np.float32(12582912.0))

B, S, Dm = 2, 8192, 2048
NCORES = 8
T = 2048
SEGS = 4
H, HKV, DH, CHUNK = 16, 4, 128, 64
NCH = T // CHUNK
NT = T // 128
ND = Dm // 128
GT = 4
NG = NT // GT


def _alibi_lam():
    i = np.arange(1, H + 1, dtype=np.float32)
    return np.exp2((-8.0 * i / H).astype(np.float32))

def host_tables(core_id):
    seg = core_id % SEGS
    lam = _alibi_lam()
    r = np.arange(CHUNK, dtype=np.float32)
    diff = r[:, None] - r[None, :]
    with np.errstate(under="ignore"):
        diag = np.where(diff >= 0, np.exp(-lam[:, None, None] * diff), 0.0).astype(np.float32)
        diagT = np.ascontiguousarray(diag.transpose(2, 0, 1).reshape(CHUNK, H * CHUNK))
        diagT = np.ascontiguousarray(np.concatenate([diagT, diagT], axis=0))
        kdec = np.exp(-lam[None, :] * (CHUNK - 1.0 - r[:, None])).astype(np.float32)
        kdec = np.concatenate([kdec, kdec], axis=0)
        qdec = np.exp(-lam[:, None] * (r[None, :] + 1.0)).astype(np.float32)
        qdec_bc = np.ascontiguousarray(
            np.broadcast_to(qdec.reshape(1, H * CHUNK), (128, H * CHUNK)).astype(BF))
        tloc = np.arange(T, dtype=np.float32)
        qdec_pp = np.exp(-lam[None, :] * (tloc[:, None] + 1.0)).astype(np.float32)
        blk = np.exp(-lam * CHUNK).astype(np.float32)
        blk_seg = np.exp(-lam.astype(np.float64) * T).astype(np.float32)
    d = DH
    inv_freq = (1.0 / 10000.0 ** (np.arange(0, d, 2, dtype=np.float32) / d)).astype(np.float32)
    tg = (np.arange(T, dtype=np.float32) + np.float32(seg * T)).astype(np.float32)
    freqs = (tg[:, None] * inv_freq[None, :]).astype(np.float32)
    cos = np.concatenate([np.cos(freqs), np.cos(freqs)], axis=1)
    sin = np.concatenate([np.sin(freqs), np.sin(freqs)], axis=1)
    return dict(
        diagT=diagT, kdec=kdec, qdec_bc=qdec_bc, qdec_pp=qdec_pp,
        blk=[float(x) for x in blk], blk_seg=[float(x) for x in blk_seg],
        cosT=np.ascontiguousarray(cos.T.astype(BF)),
        sinT=np.ascontiguousarray(sin.T.astype(BF)),
        ident=np.eye(128, dtype=BF),
        rotm=_rot_matrix())

def _rot_matrix():
    R = np.zeros((128, 128), dtype=BF)
    for m in range(64):
        R[m + 64, m] = -1.0
    for m in range(64, 128):
        R[m - 64, m] = 1.0
    return R

def host_weights(inp):
    c = np.ascontiguousarray
    return dict(
        wqT=c(inp["L0_attn_wq"].T), wkT=c(inp["L0_attn_wk"].T),
        wvT=c(inp["L0_attn_wv"].T), woT=c(inp["L0_attn_wo"].T),
        w0fT=c(inp["L0_ffn_weight"].T), b0f=c(inp["L0_ffn_bias"].reshape(1, Dm)),
        winT=c(inp["L1_attn_W_in"].T), convw=c(inp["L1_attn_conv_w"].reshape(4096, 4)),
        convb=c(inp["L1_attn_conv_b"].reshape(4096, 1)),
        woutT=c(inp["L1_attn_W_out"].T), wm1T=c(inp["L1_attn_W_mlp1"].T),
        wm2T=c(inp["L1_attn_W_mlp2"].T), w1T=c(inp["L1_ffn_w1"].T),
        w3T=c(inp["L1_ffn_w3"].T), w2T=c(inp["L1_ffn_w2"].T))


class Ctr:
    def __init__(self, h):
        self.h = h
        self.v = 0

class Prog:
    ENGS = ("sync", "scalar", "vector", "tensor", "gpsimd")
    def __init__(self, nc):
        self.nc = nc
        self.streams = {e: [] for e in self.ENGS}
    def emit(self):
        with self.nc.Block() as blk:
            for e in self.ENGS:
                ops = self.streams[e]
                if not ops:
                    continue
                getattr(blk, e)(lambda eng, ops=ops: [f(eng) for f in ops])

class Eng:
    def __init__(self, p, name, ctr):
        self.p, self.name, self.c = p, name, ctr
    def wait(self, ctr, v):
        if v > 0:
            self.p.streams[self.name].append(
                lambda e, h=ctr.h, v=v: e.wait_ge(h, v))
    def waits(self, deps):
        for (c, v) in deps:
            self.wait(c, v)
    def op(self, fn, deps=(), inc=1):
        self.waits(deps)
        c = self.c
        self.p.streams[self.name].append(
            lambda e, fn=fn, h=c.h, n=inc: fn(e).then_inc(h, n))
        c.v += inc
        return c.v
    def chain(self, fns, deps=(), inc=1):
        self.waits(deps)
        c = self.c
        def f(e, fns=fns, h=c.h, n=inc):
            last = None
            for fn in fns:
                last = fn(e)
            last.then_inc(h, n)
        self.p.streams[self.name].append(f)
        c.v += inc
        return c.v
    def dma(self, out, in_, deps=()):
        return self.op(lambda e, o=out, i=in_: e.dma_start(o, i), deps=deps, inc=16)

class Phase:
    def __init__(self, nc, st, tag):
        self.nc = nc
        self.tag = tag
        p = Prog(nc)
        self.p = p
        self.ctrs = {}
        for nm in ("pe", "ac", "ve", "po", "sy", "gd"):
            self.ctrs[nm] = Ctr(st.enter_context(nc.semaphore(f"{tag}_{nm}")))
        self.TE = Eng(p, "tensor", self.ctrs["pe"])
        self.SC = Eng(p, "scalar", self.ctrs["ac"])
        self.VE = Eng(p, "vector", self.ctrs["ve"])
        self.GP = Eng(p, "gpsimd", self.ctrs["po"])
        self.SY = Eng(p, "sync", self.ctrs["sy"])
        self.GD = Eng(p, "gpsimd", self.ctrs["gd"])
    def emit(self):
        with self.nc.named_scope(self.tag):
            self.p.emit()

def emit_ln(ph, xb, xscr, stats, eps, n=Dm, deps=(), rms_sqrt_sum=False):
    SC, VE = ph.SC, ph.VE
    ac, ve = ph.ctrs["ac"], ph.ctrs["ve"]
    SC.waits(deps)
    SC.op(lambda e: e.activation(xscr[:, 0:n], xb[:, 0:n], AF.Identity,
                                 accum_out=stats[:, 0:1]))
    va = SC.op(lambda e: e.activation(xscr[:, 0:n], xb[:, 0:n], AF.Square,
                                      accum_out=stats[:, 1:2]))
    inv_n = float(np.float32(1.0 / n))
    VE.op(lambda e: e.tensor_scalar(stats[:, 2:3], stats[:, 0:1], inv_n, None, ALU.mult),
          deps=[(ac, va)])
    v1 = VE.op(lambda e: e.tensor_scalar(stats[:, 3:4], stats[:, 1:2], inv_n, None, ALU.mult))
    v2 = VE.op(lambda e: e.scalar_tensor_tensor(stats[:, 4:5], stats[:, 2:3], stats[:, 2:3],
                                                stats[:, 3:4], ALU.mult, ALU.subtract),
               deps=[(ve, v1)])
    v3 = VE.op(lambda e: e.tensor_scalar(stats[:, 5:6], stats[:, 4:5], -1.0, float(eps),
                                         ALU.mult, ALU.add), deps=[(ve, v2)])
    v4 = SC.op(lambda e: e.activation(stats[:, 6:7], stats[:, 5:6], AF.Sqrt),
               deps=[(ve, v3)])
    v5 = VE.op(lambda e: e.reciprocal(stats[:, 6:7], stats[:, 6:7]), deps=[(ac, v4)])
    v6 = VE.op(lambda e: e.scalar_tensor_tensor(stats[:, 7:8], stats[:, 2:3], -1.0,
                                                stats[:, 6:7], ALU.mult, ALU.mult),
               deps=[(ve, v5)])
    return v6


def decl_A_inputs(din):
    din("xin", [T, Dm], F32)
    din("wqT", [Dm, 2048], F32); din("wkT", [Dm, 512], F32); din("wvT", [Dm, 512], F32)
    din("diagT", [128, 1024], F32); din("kdec", [128, 16], F32)
    din("rotm", [128, 128], BF16)
    din("qdec_bc", [128, 1024], BF16)
    din("cosT", [128, T], BF16); din("sinT", [128, T], BF16)
    din("ident", [128, 128], BF16)

def build_A(tab):
    nc = bass.Bass("TRN2", target_bir_lowering=False)
    dr = {}
    def din(name, shape, dt):
        dr[name] = nc.dram_tensor(name, shape, dt, kind="ExternalInput").ap()
    def dout(name, shape, dt):
        dr[name] = nc.dram_tensor(name, shape, dt, kind="ExternalOutput").ap()
    decl_A_inputs(din)
    dout("qf", [2048, T], BF16)
    dout("oloc", [T, 2048], BF16)
    dout("Dp", [128, 2048], F32)
    phases_A(nc, dr, tab)
    return nc

def phases_A(nc, dr, tab):
    with contextlib.ExitStack() as st:
        sb = {}
        def alloc(name, shape, dt, tag="a_"):
            sb[name] = st.enter_context(nc.sbuf_tensor(f"{tag}{name}", shape, dt))
        def palloc(name, shape, dt, tag="a_"):
            sb[name] = st.enter_context(nc.psum_tensor(f"{tag}{name}", shape, dt))
        alloc("xbuf0", [128, Dm], F32); alloc("xbuf1", [128, Dm], F32)
        alloc("xscr", [128, Dm], F32)
        alloc("stats", [128, 8], F32)
        alloc("xh", [128, Dm], BF16)
        for k in range(ND):
            alloc(f"xhT{k}", [128, 512], BF16)
        alloc("wbuf0", [128, 8192], BF16); alloc("wbuf1", [128, 8192], BF16)
        for h in range(H):
            alloc(f"qfg{h}", [128, 512], BF16)
        for k in range(HKV):
            alloc(f"kfg{k}", [128, 512], BF16)
        for ts_ in range(GT):
            alloc(f"ktg{ts_}", [128, 512], BF16)
            alloc(f"vtg{ts_}", [128, 512], BF16)
            alloc(f"olg{ts_}", [128, 2048], BF16)
        alloc("kv", [128, 2048], F32)
        alloc("kvbf", [128, 2048], BF16)
        alloc("ropeA", [128, 512], BF16); alloc("ropeB", [128, 512], BF16)
        alloc("ropeC", [128, 512], BF16)
        for h4 in range(4):
            alloc(f"qd{h4}", [128, 64], BF16)
            alloc(f"kd{h4}", [128, 128], BF16)
        alloc("ssT", [128, 256], BF16)
        alloc("cos", [128, T], BF16); alloc("sin", [128, T], BF16)
        alloc("c_diagT", [128, 1024], F32); alloc("c_kdec", [128, 16], F32)
        alloc("c_rotm", [128, 128], BF16)
        alloc("qdecbc", [128, 1024], BF16)
        alloc("c_ident", [128, 128], BF16)
        palloc("pq0", [128, 512], F32); palloc("pq1", [128, 512], F32)
        palloc("pt0", [128, 128], BF16); palloc("pt1", [128, 128], BF16)
        palloc("psS", [128, 256], F32)
        palloc("psO", [64, 512], F32)
        palloc("psA", [128, 512], F32)
        palloc("pr", [128, 512], F32)

        ph = Phase(nc, st, "A")
        emit_A(ph, dr, sb, tab)
        ph.emit()

def emit_A(ph, dr, sb, tab):
    TE, SC, VE, SY, GD = ph.TE, ph.SC, ph.VE, ph.SY, ph.GD
    pe, ac, ve, sy, gd = (ph.ctrs[n] for n in ("pe", "ac", "ve", "sy", "gd"))
    blk = tab["blk"]

    v_tabs = 0
    for nm, src in (("cos", "cosT"), ("sin", "sinT"), ("c_diagT", "diagT"),
                    ("c_kdec", "kdec"), ("qdecbc", "qdec_bc"), ("c_ident", "ident"),
                    ("c_rotm", "rotm")):
        v_tabs = SY.dma(sb[nm][:, :], dr[src])
    v_kv0 = VE.op(lambda e: e.memset(sb["kv"][:, :], 0.0))
    v_kvbf0 = SC.op(lambda e: e.activation(sb["kvbf"][:, :], sb["kv"][:, :], AF.Identity),
                    deps=[(ve, v_kv0)])

    wsched = [("q", 0), ("q", 4), ("q", 8), ("q", 12), ("k", 0), ("v", 0)]
    def wsrc(kind, e0):
        if kind == "q":
            s = dr["wqT"][:, e0 * 128:(e0 + 4) * 128]
        elif kind == "k":
            s = dr["wkT"]
        else:
            s = dr["wvT"]
        return s.rearrange("(c p) j -> p c j", p=128)

    rel_xbuf = [(ac, 0), (ac, 0)]
    rel_xh = (pe, 0)
    rel_xhT = (pe, 0)
    rel_wbuf = [(pe, 0), (pe, 0)]
    rel_pq = [(ac, 0), (ac, 0)]
    rel_pt = [(ve, 0), (ve, 0)]
    rel_qkv_te = (pe, 0)
    rel_pr = (ve, 0)
    rel_olg = (sy, 0)
    v_kvbf = (ac, v_kvbf0)
    v_ssT_free = (pe, 0)
    v_psS_free = (ve, 0)
    v_psO_free = (ac, 0)
    v_psA_free = (ve, 0)

    def wdst(slot):
        return sb[f"wbuf{slot}"][:, :].rearrange("p (c j) -> p c j", j=512)
    wload = {}
    wload[0] = GD.dma(wdst(0), wsrc(*wsched[0]), deps=())

    for g in range(NG):
        v_evac = {}
        for i in range(GT):
            t128 = g * GT + i
            xb = sb[f"xbuf{t128 % 2}"]
            v_x = SY.dma(xb[:, :], dr["xin"][t128 * 128:(t128 + 1) * 128, :],
                         deps=[rel_xbuf[t128 % 2]])
            v_ln = emit_ln(ph, xb, sb["xscr"], sb["stats"], 1e-5,
                           deps=[(sy, v_x)])
            v_norm = SC.op(lambda e, xb=xb: e.activation(
                sb["xh"][:, :], xb[:, :], AF.Identity,
                scale=sb["stats"][:, 6:7], bias=sb["stats"][:, 7:8]),
                deps=[(ve, v_ln), rel_xh])
            rel_xbuf[t128 % 2] = (ac, v_norm)
            for k in range(ND):
                pt = sb[f"pt{k % 2}"]
                v_tr = TE.op(lambda e, k=k, pt=pt: e.transpose(
                    pt[:, :], sb["xh"][:, k * 128:(k + 1) * 128], sb["c_ident"][:, :]),
                    deps=[(ac, v_norm), (sy, v_tabs), rel_pt[k % 2], rel_xhT])
                v_ev = VE.op(lambda e, k=k, i=i, pt=pt: e.tensor_copy(
                    sb[f"xhT{k}"][:, i * 128:(i + 1) * 128], pt[:, :]),
                    deps=[(pe, v_tr)])
                rel_pt[k % 2] = (ve, v_ev)
                v_evac[(i, k)] = v_ev
            rel_xh = (pe, pe.v)
        v_xhT = ve.v

        cosg = sb["cos"][:, g * 512:(g + 1) * 512]
        sing = sb["sin"][:, g * 512:(g + 1) * 512]
        v_vtg = {}
        for bi, (kind, e0) in enumerate(wsched):
            slot = bi % 2
            gi = g * len(wsched) + bi
            v_w = wload[gi]
            if gi + 1 < NG * len(wsched):
                nkind, ne0 = wsched[(gi + 1) % len(wsched)]
                wload[gi + 1] = GD.dma(wdst(1 - slot), wsrc(nkind, ne0),
                                       deps=[rel_wbuf[1 - slot]])
            if kind in ("q", "k"):
                for ec in range(4):
                    eg = (e0 + ec) if kind == "q" else ec
                    pqi = (bi * 4 + ec) % 2
                    pq = sb[f"pq{pqi}"]
                    fns = []
                    for k in range(ND):
                        lhs = sb[f"wbuf{slot}"][:, k * 512 + ec * 128: k * 512 + (ec + 1) * 128]
                        fns.append(lambda e, lhs=lhs, k=k, pq=pq, s=(k == 0), t=(k == ND - 1):
                                   e.matmul(pq[:, :], lhs, sb[f"xhT{k}"][:, :], start=s, stop=t))
                    v_mm = TE.chain(fns, deps=[(gd, v_w), (ve, v_xhT), rel_pq[pqi]])
                    dst = sb[f"qfg{eg}"] if kind == "q" else sb[f"kfg{eg}"]
                    v_eA = SC.op(lambda e, pq=pq: e.activation(sb["ropeA"][:, :], pq[:, :],
                                                               AF.Identity),
                                 deps=[(pe, v_mm), (ve, ve.v)])
                    rel_pq[pqi] = (ac, v_eA)
                    v_rm = TE.op(lambda e: e.matmul(sb["pr"][:, :], sb["c_rotm"][:, :],
                                                    sb["ropeA"][:, :], start=True, stop=True),
                                 deps=[(ac, v_eA), rel_pr])
                    v_r3 = VE.op(lambda e, cosg=cosg: e.tensor_tensor(
                        sb["ropeC"][:, :], sb["ropeA"][:, :], cosg, ALU.mult),
                        deps=[(ac, v_eA)])
                    v_r4 = VE.op(lambda e, sing=sing: e.tensor_tensor(
                        sb["ropeB"][:, :], sb["pr"][:, :], sing, ALU.mult),
                        deps=[(pe, v_rm)])
                    rel_pr = (ve, v_r4)
                    v_r5 = VE.op(lambda e, dst=dst: e.tensor_tensor(
                        dst[:, :], sb["ropeC"][:, :], sb["ropeB"][:, :], ALU.add),
                        deps=[(ve, v_r4), rel_qkv_te])
                    if kind == "q":
                        SY.dma(dr["qf"][eg * 128:(eg + 1) * 128, g * 512:(g + 1) * 512],
                               dst[:, :], deps=[(ve, v_r5)])
                    else:
                        for ts_ in range(GT):
                            pt = sb[f"pt{ts_ % 2}"]
                            v_tr = TE.op(lambda e, dst=dst, ts_=ts_, pt=pt: e.transpose(
                                pt[:, :], dst[:, ts_ * 128:(ts_ + 1) * 128], sb["c_ident"][:, :]),
                                deps=[(ve, v_r5), rel_pt[ts_ % 2]])
                            v_ev = VE.op(lambda e, ts_=ts_, eg=eg, pt=pt: e.tensor_copy(
                                sb[f"ktg{ts_}"][:, eg * 128:(eg + 1) * 128], pt[:, :]),
                                deps=[(pe, v_tr), rel_qkv_te])
                            rel_pt[ts_ % 2] = (ve, v_ev)
            else:
                for ts_ in range(GT):
                    pqi = ts_ % 2
                    pq = sb[f"pq{pqi}"]
                    fns = []
                    for k in range(ND):
                        lhs = sb[f"xhT{k}"][:, ts_ * 128:(ts_ + 1) * 128]
                        rhs = sb[f"wbuf{slot}"][:, k * 512:(k + 1) * 512]
                        fns.append(lambda e, lhs=lhs, rhs=rhs, pq=pq, s=(k == 0), t=(k == ND - 1):
                                   e.matmul(pq[:, :], lhs, rhs, start=s, stop=t))
                    v_mm = TE.chain(fns, deps=[(gd, v_w), (ve, v_xhT), rel_pq[pqi]])
                    v_e = SC.op(lambda e, pq=pq, ts_=ts_: e.activation(
                        sb[f"vtg{ts_}"][:, :], pq[:, :], AF.Identity),
                        deps=[(pe, v_mm), rel_qkv_te])
                    rel_pq[pqi] = (ac, v_e)
                    v_vtg[ts_] = v_e
            rel_wbuf[slot] = (pe, pe.v)
        v_qkv = (ve.v, ac.v)

        for c in range(2 * GT):
            ts_ = c // 2
            half = c % 2
            rows0 = half * 64
            tcol = c * 64
            for g4 in range(HKV):
                hs = [g4 * 4 + j for j in range(4)]
                v_kd = {}
                for j, h in enumerate(hs):
                    v_kd[j] = VE.op(lambda e, j=j, h=h, ts_=ts_, rows0=rows0, g4=g4:
                        e.tensor_scalar(sb[f"kd{j}"][rows0:rows0 + 64, :],
                                        sb[f"ktg{ts_}"][rows0:rows0 + 64, g4 * 128:(g4 + 1) * 128],
                                        sb["c_kdec"][rows0:rows0 + 64, h:h + 1], None, ALU.mult),
                        deps=[(ve, v_qkv[0]), (ac, v_qkv[1])])
                v_qd = {}
                for j, h in enumerate(hs):
                    v_qd[j] = VE.op(lambda e, j=j, h=h, tcol=tcol: e.tensor_tensor(
                        sb[f"qd{j}"][:, :], sb[f"qfg{h}"][:, tcol:tcol + 64],
                        sb["qdecbc"][:, h * 64:(h + 1) * 64], ALU.mult))
                fns = []
                for j, h in enumerate(hs):
                    fns.append(lambda e, j=j, h=h, g4=g4, tcol=tcol, rows0=rows0: e.matmul(
                        sb["psS"][rows0:rows0 + 64, j * 64:(j + 1) * 64],
                        sb[f"kfg{g4}"][:, tcol:tcol + 64],
                        sb[f"qfg{h}"][:, tcol:tcol + 64], start=True, stop=True))
                v_s = TE.chain(fns, deps=[v_psS_free, (ve, v_qkv[0]), (ac, v_qkv[1]),
                                          v_ssT_free])
                v_diag = VE.op(lambda e, g4=g4, rows0=rows0: e.tensor_tensor(
                    sb["ssT"][rows0:rows0 + 64, :], sb["psS"][rows0:rows0 + 64, :],
                    sb["c_diagT"][rows0:rows0 + 64, g4 * 256:(g4 + 1) * 256], ALU.mult),
                    deps=[(pe, v_s)])
                v_psS_free = (ve, v_diag)
                fns = []
                for j, h in enumerate(hs):
                    fns.append(lambda e, j=j, ts_=ts_, rows0=rows0, g4=g4: e.matmul(
                        sb["psO"][:, j * 128:(j + 1) * 128],
                        sb["ssT"][rows0:rows0 + 64, j * 64:(j + 1) * 64],
                        sb[f"vtg{ts_}"][rows0:rows0 + 64, g4 * 128:(g4 + 1) * 128],
                        start=True, stop=False))
                    fns.append(lambda e, j=j, h=h: e.matmul(
                        sb["psO"][:, j * 128:(j + 1) * 128],
                        sb[f"qd{j}"][:, :],
                        sb["kvbf"][:, h * 128:(h + 1) * 128],
                        start=False, stop=True))
                v_o = TE.chain(fns, deps=[(ve, v_diag), (ve, max(v_qd.values())),
                                          v_kvbf, v_psO_free])
                v_ssT_free = (pe, v_o)
                fns = []
                for j, h in enumerate(hs):
                    fns.append(lambda e, j=j, ts_=ts_, rows0=rows0, g4=g4: e.matmul(
                        sb["psA"][:, j * 128:(j + 1) * 128],
                        sb[f"kd{j}"][rows0:rows0 + 64, :],
                        sb[f"vtg{ts_}"][rows0:rows0 + 64, g4 * 128:(g4 + 1) * 128],
                        start=True, stop=True))
                v_A = TE.chain(fns, deps=[(ve, max(v_kd.values())), v_psA_free])
                v_oe = SC.op(lambda e, ts_=ts_, rows0=rows0, g4=g4: e.activation(
                    sb[f"olg{ts_}"][rows0:rows0 + 64, g4 * 512:(g4 + 1) * 512],
                    sb["psO"][0:64, :], AF.Identity),
                    deps=[(pe, v_o), rel_olg])
                v_psO_free = (ac, v_oe)
                v_u = 0
                for j, h in enumerate(hs):
                    v_u = VE.op(lambda e, j=j, h=h: e.scalar_tensor_tensor(
                        sb["kv"][:, h * 128:(h + 1) * 128],
                        sb["kv"][:, h * 128:(h + 1) * 128], blk[h],
                        sb["psA"][:, j * 128:(j + 1) * 128], ALU.mult, ALU.add),
                        deps=[(pe, v_A), v_kvbf])
                v_psA_free = (ve, v_u)
                v_kb = SC.op(lambda e, g4=g4: e.activation(
                    sb["kvbf"][:, g4 * 512:(g4 + 1) * 512],
                    sb["kv"][:, g4 * 512:(g4 + 1) * 512], AF.Identity),
                    deps=[(ve, v_u)])
                v_kvbf = (ac, v_kb)
        rel_qkv_te = (pe, pe.v)
        v_st = 0
        for ts_ in range(GT):
            v_st = SY.dma(dr["oloc"][(g * GT + ts_) * 128:(g * GT + ts_ + 1) * 128, :],
                          sb[f"olg{ts_}"][:, :], deps=[(ac, ac.v)])
        rel_olg = (sy, v_st)
        rel_xhT = (pe, pe.v)

    SY.dma(dr["Dp"], sb["kv"][:, :], deps=[(ve, ve.v), (ac, ac.v)])
    ph.SY.wait(ph.ctrs["sy"], ph.ctrs["sy"].v)


def decl_B_weights(din):
    din("woT", [Dm, 2048], F32)
    din("w0fT", [Dm, 2048], F32)
    din("b0f", [1, Dm], F32)
    din("qdec_pp", [T, 16], F32)

def build_B(tab):
    nc = bass.Bass("TRN2", target_bir_lowering=False)
    dr = {}
    def din(name, shape, dt):
        dr[name] = nc.dram_tensor(name, shape, dt, kind="ExternalInput").ap()
    def dout(name, shape, dt):
        dr[name] = nc.dram_tensor(name, shape, dt, kind="ExternalOutput").ap()
    din("xin", [T, Dm], F32)
    din("qf", [2048, T], BF16)
    din("oloc", [T, 2048], BF16)
    for i in range(3):
        din(f"Dpred{i}", [128, 2048], F32)
    decl_B_weights(din)
    din("ident", [128, 128], BF16)
    dout("S2", [T, Dm], F32)
    S1 = nc.dram_tensor("S1", [T, Dm], F32, kind="Internal").ap()
    dr["S1"] = S1
    phases_B(nc, dr, tab, fused=False)
    return nc

def phases_B(nc, dr, tab, fused):
    with contextlib.ExitStack() as st:
        sb = {}
        def alloc(name, shape, dt, tag="b2_"):
            sb[name] = st.enter_context(nc.sbuf_tensor(f"{tag}{name}", shape, dt))
        def palloc(name, shape, dt, tag="b2_"):
            sb[name] = st.enter_context(nc.psum_tensor(f"{tag}{name}", shape, dt))
        alloc("wo", [128, 16 * 2048], BF16)
        alloc("kvin", [128, 2048], F32)
        alloc("kvbf", [128, 2048], BF16)
        alloc("dpr", [128, 2048], F32)
        alloc("qft", [128, 16 * 128], BF16)
        alloc("olt", [128, 2048], BF16)
        alloc("o_as", [128, 2048], BF16)
        for k in range(ND):
            alloc(f"oT{k}", [128, 128], BF16)
        alloc("qpp", [128, 16], F32)
        alloc("xint", [128, Dm], F32)
        alloc("s1t", [128, Dm], F32)
        alloc("c_ident", [128, 128], BF16)
        if fused:
            alloc("wsc_st", [128, 16 * 512], F32)
            alloc("wabsb", [128, 16 * 512], BF16)
            alloc("onesb2", [128, 1], BF16)
            alloc("wtot2", [1, 16], F32)
        palloc("psC0", [128, 512], F32); palloc("psC1", [128, 512], F32)
        palloc("psW0", [128, 512], F32); palloc("psW1", [128, 512], F32)
        palloc("pt0", [128, 128], BF16); palloc("pt1", [128, 128], BF16)
        if fused:
            palloc("psW2", [1, 512], F32)
        ph = Phase(nc, st, "B2")
        emit_B2(ph, dr, sb, tab, fused=fused)
        ph.emit()

    with contextlib.ExitStack() as st:
        sb = {}
        def alloc(name, shape, dt, tag="b4_"):
            sb[name] = st.enter_context(nc.sbuf_tensor(f"{tag}{name}", shape, dt))
        def palloc(name, shape, dt, tag="b4_"):
            sb[name] = st.enter_context(nc.psum_tensor(f"{tag}{name}", shape, dt))
        for k in range(ND):
            alloc(f"xqT{k}", [128, T], BF16)
        alloc("wst", [128, 16 * 512], F32)
        for k in range(ND):
            alloc(f"wtn{k}", [128, 512], BF16)
        alloc("s1t", [128, Dm], F32)
        alloc("xn1", [128, Dm], F32)
        alloc("xn2", [128, Dm], F32)
        alloc("xqb", [128, Dm], BF16)
        alloc("stats", [128, 8], F32)
        alloc("sxall", [128, NT], F32)
        alloc("wab", [128, 512], F32)
        alloc("grow", [1, 512], F32)
        alloc("ginvb", [128, 512], F32)
        alloc("gmb", [128, 512], F32)
        alloc("bb", [128, 512], F32)
        alloc("brow", [1, 512], F32)
        alloc("onesc", [128, 1], F32)
        alloc("onesr", [1, 128], F32)
        alloc("ef", [128, 512], F32)
        alloc("e2", [128, 512], F32)
        alloc("s2t", [128, 512], F32)
        alloc("sl1", [128, 512], F32)
        alloc("c_ident2", [128, 128], BF16)
        palloc("psF0", [128, 512], F32); palloc("psF1", [128, 512], F32)
        palloc("psG", [1, 512], F32)
        palloc("psB", [128, 512], F32)
        palloc("pt0b", [128, 128], BF16); palloc("pt1b", [128, 128], BF16)
        ph = Phase(nc, st, "B4")
        emit_B4(ph, dr, sb, tab)
        ph.emit()

def emit_wscale(ph, dr, sb):
    TE, SC, VE, SY, GD = ph.TE, ph.SC, ph.VE, ph.SY, ph.GD
    pe, ac, ve, sy, gd = (ph.ctrs[n] for n in ("pe", "ac", "ve", "sy", "gd"))
    v_ones = VE.op(lambda e: e.memset(sb["onesb2"][:, :], 1.0))
    rel_st = (ac, 0)
    rel_ab = (pe, 0)
    rel_psW = (ve, 0)
    wmeta = [("w1T", ND, DH1, 0), ("w3T", ND, DH1, 2), ("w2T", NH1, Dm, 4)]
    for (wn, NK, NCOL, coli) in wmeta:
        rgs = [(r0, min(16, NK - r0)) for r0 in range(0, NK, 16)]
        blocks = [(r0, nt, nb) for (r0, nt) in rgs for nb in range(NCOL // 512)]
        v_g = 0
        for bi, (r0, ntl, nb) in enumerate(blocks):
            v_w = GD.dma(sb["wsc_st"][:, 0:ntl * 512].rearrange("p (c j) -> p c j", j=512),
                         dr[wn][r0 * 128:(r0 + ntl) * 128,
                                nb * 512:(nb + 1) * 512].rearrange("(c p) j -> p c j", p=128),
                         deps=[rel_st])
            v_ab = SC.op(lambda e, ntl=ntl: e.activation(
                sb["wabsb"][:, 0:ntl * 512], sb["wsc_st"][:, 0:ntl * 512], AF.Abs),
                deps=[(gd, v_w), rel_ab])
            rel_st = (ac, v_ab)
            fns = []
            for k in range(ntl):
                first = (bi == 0 and k == 0)
                last = (bi == len(blocks) - 1 and k == ntl - 1)
                fns.append(lambda e, k=k, first=first, last=last: e.matmul(
                    sb["psW2"][:, :], sb["onesb2"][:, :],
                    sb["wabsb"][:, k * 512:(k + 1) * 512], start=first, stop=last))
            v_g = TE.chain(fns, deps=[(ac, v_ab), (ve, v_ones)] +
                           ([rel_psW] if bi == 0 else []))
            rel_ab = (pe, v_g)
        v_rs = VE.op(lambda e, coli=coli: e.reduce_sum(
            sb["wtot2"][:, coli:coli + 1], sb["psW2"][:, :], axis=AX.X),
            deps=[(pe, v_g)])
        rel_psW = (ve, v_rs)
        v_m = VE.op(lambda e, coli=coli: e.tensor_scalar(
            sb["wtot2"][:, coli + 1:coli + 2], sb["wtot2"][:, coli:coli + 1],
            float(np.float32(1.0 / (DH1 * Dm))), 1e-5, ALU.mult, ALU.max),
            deps=[(ve, v_rs)])
        VE.op(lambda e, coli=coli: e.reciprocal(
            sb["wtot2"][:, coli:coli + 1], sb["wtot2"][:, coli + 1:coli + 2]),
            deps=[(ve, v_m)])
    SY.dma(dr["WSC"], sb["wtot2"][:, :], deps=[(ve, ve.v)])

def emit_B2(ph, dr, sb, tab, fused=False):
    TE, SC, VE, SY, GD = ph.TE, ph.SC, ph.VE, ph.SY, ph.GD
    pe, ac, ve, sy, gd = (ph.ctrs[n] for n in ("pe", "ac", "ve", "sy", "gd"))
    bs = tab["blk_seg"]

    v_id = SY.dma(sb["c_ident"][:, :], dr["ident"])
    v_wo = GD.dma(sb["wo"][:, :].rearrange("p (c j) -> p c j", j=2048),
                  dr["woT"].rearrange("(c p) j -> p c j", p=128))
    if fused:
        emit_wscale(ph, dr, sb)
    if fused:
        v_d = SY.dma(sb["kvin"][:, :], dr["KVIN"])
        v_kvbf = SC.op(lambda e: e.activation(sb["kvbf"][:, :], sb["kvin"][:, :],
                                              AF.Identity), deps=[(sy, v_d)])
    else:
        v_kv = VE.op(lambda e: e.memset(sb["kvin"][:, :], 0.0))
        for i in range(3):
            v_d = SY.dma(sb["dpr"][:, :], dr[f"Dpred{i}"], deps=[(ve, v_kv)])
            for h in range(H):
                v_kv = VE.op(lambda e, h=h: e.scalar_tensor_tensor(
                    sb["kvin"][:, h * 128:(h + 1) * 128],
                    sb["kvin"][:, h * 128:(h + 1) * 128], bs[h],
                    sb["dpr"][:, h * 128:(h + 1) * 128], ALU.mult, ALU.add),
                    deps=[(sy, v_d), (ve, v_kv)])
        v_kvbf = SC.op(lambda e: e.activation(sb["kvbf"][:, :], sb["kvin"][:, :],
                                              AF.Identity), deps=[(ve, v_kv)])

    rel_qft = (pe, 0); rel_olt = (ve, 0); rel_oas = (pe, 0)
    rel_psC = [(ve, 0), (ve, 0)]
    rel_psW = [(ve, 0), (ve, 0)]
    rel_pt = [(ve, 0), (ve, 0)]
    rel_oT = (pe, 0)
    rel_s1t = (sy, 0)
    rel_xint = (ve, 0)
    for t128 in range(NT):
        cols = slice(t128 * 128, (t128 + 1) * 128)
        v_q = SY.dma(sb["qft"][:, :].rearrange("p (c t) -> p c t", t=128),
                     dr["qf"][:, cols].rearrange("(c p) t -> p c t", p=128),
                     deps=[rel_qft])
        v_ol = SY.dma(sb["olt"][:, :], dr["oloc"][cols, :], deps=[rel_olt])
        v_pp = SY.dma(sb["qpp"][:, :], dr["qdec_pp"][cols, :])
        v_xi = SY.dma(sb["xint"][:, :], dr["xin"][cols, :], deps=[rel_xint])
        v_oa = 0
        for g4 in range(HKV):
            psC = sb[f"psC{g4 % 2}"]
            fns = []
            for j in range(4):
                h = g4 * 4 + j
                fns.append(lambda e, j=j, h=h, psC=psC: e.matmul(
                    psC[:, j * 128:(j + 1) * 128],
                    sb["qft"][:, h * 128:(h + 1) * 128],
                    sb["kvbf"][:, h * 128:(h + 1) * 128], start=True, stop=True))
            v_mm = TE.chain(fns, deps=[(sy, v_q), (ac, v_kvbf), rel_psC[g4 % 2]])
            for j in range(4):
                h = g4 * 4 + j
                v_oa = VE.op(lambda e, j=j, h=h, psC=psC: e.scalar_tensor_tensor(
                    sb["o_as"][:, h * 128:(h + 1) * 128], psC[:, j * 128:(j + 1) * 128],
                    sb["qpp"][:, h:h + 1], sb["olt"][:, h * 128:(h + 1) * 128],
                    ALU.mult, ALU.add),
                    deps=[(pe, v_mm), (sy, max(v_ol, v_pp)), rel_oas])
            rel_psC[g4 % 2] = (ve, v_oa)
        rel_olt = (ve, v_oa)
        rel_qft = (pe, pe.v)
        v_tr_ev = {}
        for k in range(ND):
            pt = sb[f"pt{k % 2}"]
            v_tr = TE.op(lambda e, k=k, pt=pt: e.transpose(
                pt[:, :], sb["o_as"][:, k * 128:(k + 1) * 128], sb["c_ident"][:, :]),
                deps=[(ve, v_oa), (sy, v_id), rel_pt[k % 2], rel_oT])
            v_ev = VE.op(lambda e, k=k, pt=pt: e.tensor_copy(sb[f"oT{k}"][:, :], pt[:, :]),
                         deps=[(pe, v_tr)])
            rel_pt[k % 2] = (ve, v_ev)
            v_tr_ev[k] = v_ev
        rel_oas = (pe, pe.v)
        v_trd = max(v_tr_ev.values())
        v_s1 = 0
        for dblk in range(4):
            psW = sb[f"psW{dblk % 2}"]
            fns = []
            for k in range(ND):
                fns.append(lambda e, k=k, dblk=dblk, psW=psW: e.matmul(
                    psW[:, :], sb[f"oT{k}"][:, :],
                    sb["wo"][:, k * 2048 + dblk * 512: k * 2048 + (dblk + 1) * 512],
                    start=(k == 0), stop=(k == ND - 1)))
            v_mm = TE.chain(fns, deps=[(ve, v_trd), (gd, v_wo), rel_psW[dblk % 2]])
            v_s1 = VE.op(lambda e, dblk=dblk, psW=psW: e.tensor_tensor(
                sb["s1t"][:, dblk * 512:(dblk + 1) * 512], psW[:, :],
                sb["xint"][:, dblk * 512:(dblk + 1) * 512], ALU.add),
                deps=[(pe, v_mm), (sy, v_xi), rel_s1t])
            rel_psW[dblk % 2] = (ve, v_s1)
        rel_oT = (pe, pe.v)
        rel_xint = (ve, v_s1)
        v_st = SY.dma(dr["S1"][cols, :], sb["s1t"][:, :], deps=[(ve, v_s1)])
        rel_s1t = (sy, v_st)
    SY.wait(sy, sy.v)
    GD.wait(gd, gd.v)

def emit_B4(ph, dr, sb, tab):
    TE, SC, VE, SY, GD = ph.TE, ph.SC, ph.VE, ph.SY, ph.GD
    pe, ac, ve, sy, gd = (ph.ctrs[n] for n in ("pe", "ac", "ve", "sy", "gd"))

    v_id = SY.dma(sb["c_ident2"][:, :], dr["ident"])
    v_ones = VE.op(lambda e: e.memset(sb["onesc"][:, :], 1.0))
    v_ones = VE.op(lambda e: e.memset(sb["onesr"][:, :], 1.0))

    rel_s1t = (ac, 0)
    rel_xn1 = (ve, 0)
    rel_xn2 = (ve, 0)
    rel_xqb = (pe, 0)
    rel_pt = [(ve, 0), (ve, 0)]
    for t128 in range(NT):
        cols = slice(t128 * 128, (t128 + 1) * 128)
        v_x = SY.dma(sb["s1t"][:, :], dr["S1"][cols, :], deps=[rel_s1t])
        v_ln = emit_ln(ph, sb["s1t"], sb["xn1"], sb["stats"], 1e-5,
                       deps=[(sy, v_x), rel_xn1])
        v_n1 = SC.op(lambda e: e.activation(sb["xn1"][:, :], sb["s1t"][:, :], AF.Identity,
                                            scale=sb["stats"][:, 6:7], bias=sb["stats"][:, 7:8]),
                     deps=[(ve, v_ln)])
        rel_s1t = (ac, v_n1)
        v_sq = SC.op(lambda e: e.activation(sb["xn2"][:, :], sb["xn1"][:, :], AF.Square,
                                            accum_out=sb["stats"][:, 1:2]),
                     deps=[rel_xn2])
        v_r1 = VE.op(lambda e: e.tensor_scalar(sb["stats"][:, 2:3], sb["stats"][:, 1:2],
                                               float(np.float32(1.0 / Dm)), 1e-8, ALU.mult, ALU.add),
                     deps=[(ac, v_sq)])
        v_r2 = SC.op(lambda e: e.activation(sb["stats"][:, 3:4], sb["stats"][:, 2:3], AF.Sqrt),
                     deps=[(ve, v_r1)])
        v_r3 = VE.op(lambda e: e.reciprocal(sb["stats"][:, 3:4], sb["stats"][:, 3:4]),
                     deps=[(ac, v_r2)])
        v_x2 = VE.op(lambda e: e.tensor_scalar(sb["xn2"][:, :], sb["xn1"][:, :],
                                               sb["stats"][:, 3:4], None, ALU.mult),
                     deps=[(ve, v_r3), (ac, v_sq)])
        v_mx = VE.op(lambda e: e.reduce_max(sb["stats"][:, 4:5], sb["xn2"][:, :], axis=AX.X,
                                            apply_absolute_value=True),
                     deps=[(ve, v_x2)])
        v_sc = VE.op(lambda e, t128=t128: e.tensor_scalar(
            sb["sxall"][:, t128:t128 + 1], sb["stats"][:, 4:5],
            float(np.float32(1.0 / 127.0)), None, ALU.mult), deps=[(ve, v_mx)])
        v_e = VE.op(lambda e, t128=t128: e.tensor_scalar(
            sb["stats"][:, 5:6], sb["sxall"][:, t128:t128 + 1], 1e-8, None, ALU.add),
            deps=[(ve, v_sc)])
        v_rc = VE.op(lambda e: e.reciprocal(sb["stats"][:, 5:6], sb["stats"][:, 5:6]),
                     deps=[(ve, v_e)])
        v_q1 = VE.op(lambda e: e.tensor_scalar(sb["xn1"][:, :], sb["xn2"][:, :],
                                               sb["stats"][:, 5:6], MAGIC, ALU.mult, ALU.add),
                     deps=[(ve, v_rc)])
        v_q2 = VE.op(lambda e: e.tensor_scalar(sb["xqb"][:, :], sb["xn1"][:, :],
                                               MAGIC, None, ALU.subtract),
                     deps=[(ve, v_q1), rel_xqb])
        rel_xn1 = (ve, v_q2)
        rel_xn2 = (ve, v_q1)
        for k in range(ND):
            pt = sb[f"pt{k % 2}b"]
            v_tr = TE.op(lambda e, k=k, pt=pt: e.transpose(
                pt[:, :], sb["xqb"][:, k * 128:(k + 1) * 128], sb["c_ident2"][:, :]),
                deps=[(ve, v_q2), (sy, v_id), rel_pt[k % 2]])
            v_ev = VE.op(lambda e, k=k, t128=t128, pt=pt: e.tensor_copy(
                sb[f"xqT{k}"][:, t128 * 128:(t128 + 1) * 128], pt[:, :]),
                deps=[(pe, v_tr)])
            rel_pt[k % 2] = (ve, v_ev)
        rel_xqb = (pe, pe.v)
    v_xq_done = ve.v

    rel_wst = (ac, 0)
    rel_wab = (pe, 0)
    rel_psG = (ve, 0)
    rel_psF = [(ac, 0), (ac, 0)]
    rel_ef = [(ve, 0)]
    rel_s2 = (sy, 0)
    rel_sl1 = (ve, 0)
    for nblk in range(4):
        ncols = slice(nblk * 512, (nblk + 1) * 512)
        v_w = SY.dma(sb["wst"][:, :].rearrange("p (c j) -> p c j", j=512),
                     dr["w0fT"][:, ncols].rearrange("(c p) j -> p c j", p=128),
                     deps=[rel_wst])
        v_g = 0
        for k in range(ND):
            v_ab = SC.op(lambda e, k=k: e.activation(
                sb["wab"][:, :], sb["wst"][:, k * 512:(k + 1) * 512], AF.Abs),
                deps=[(sy, v_w), rel_wab])
            v_g = TE.op(lambda e, k=k, s=(k == 0), t=(k == ND - 1): e.matmul(
                sb["psG"][:, :], sb["onesc"][:, :], sb["wab"][:, :], start=s, stop=t),
                deps=[(ac, v_ab), (ve, v_ones)] + ([rel_psG] if k == 0 else []), inc=1)
            rel_wab = (pe, v_g)
        v_gr = VE.op(lambda e: e.tensor_scalar(sb["grow"][:, :], sb["psG"][:, :],
                                               float(np.float32(1.0 / Dm)), None, ALU.mult),
                     deps=[(pe, v_g)])
        rel_psG = (ve, v_gr)
        v_b1 = TE.op(lambda e: e.matmul(sb["psB"][:, :], sb["onesr"][:, :], sb["grow"][:, :],
                                        start=True, stop=True), deps=[(ve, v_gr)])
        v_gmb = VE.op(lambda e: e.tensor_copy(sb["gmb"][:, :], sb["psB"][:, :]),
                      deps=[(pe, v_b1)])
        v_gi = VE.op(lambda e: e.tensor_scalar(sb["grow"][:, :], sb["grow"][:, :],
                                               1e-8, None, ALU.add),
                     deps=[(ve, v_gr), (pe, v_b1)])
        v_rc = VE.op(lambda e: e.reciprocal(sb["grow"][:, :], sb["grow"][:, :]),
                     deps=[(ve, v_gi)])
        v_b2 = TE.op(lambda e: e.matmul(sb["psB"][:, :], sb["onesr"][:, :], sb["grow"][:, :],
                                        start=True, stop=True), deps=[(ve, v_rc), (ve, v_gmb)])
        v_giv = VE.op(lambda e: e.tensor_copy(sb["ginvb"][:, :], sb["psB"][:, :]),
                      deps=[(pe, v_b2)])
        v_br = SY.dma(sb["brow"][:, :], dr["b0f"][:, ncols])
        v_b3 = TE.op(lambda e: e.matmul(sb["psB"][:, :], sb["onesr"][:, :], sb["brow"][:, :],
                                        start=True, stop=True), deps=[(sy, v_br), (ve, v_giv)])
        v_bb = VE.op(lambda e: e.tensor_copy(sb["bb"][:, :], sb["psB"][:, :]),
                     deps=[(pe, v_b3)])
        v_wt = {}
        for k in range(ND):
            v_t1 = VE.op(lambda e, k=k: e.tensor_tensor(
                sb["ef"][:, :], sb["wst"][:, k * 512:(k + 1) * 512], sb["ginvb"][:, :], ALU.mult),
                deps=[(ve, v_giv), rel_ef[0]])
            v_t2 = VE.op(lambda e: e.tensor_scalar(sb["ef"][:, :], sb["ef"][:, :],
                                                   MAGIC, MAGIC, ALU.add, ALU.subtract),
                         deps=[(ve, v_t1)])
            v_t3 = VE.op(lambda e, k=k: e.tensor_scalar(sb[f"wtn{k}"][:, :], sb["ef"][:, :],
                                                        1.0, -1.0, ALU.min, ALU.max),
                         deps=[(ve, v_t2)])
            rel_ef[0] = (ve, v_t3)
            v_wt[k] = v_t3
        rel_wst = (ve, max(v_wt.values()))
        v_wq = max(v_wt.values())
        for t128 in range(NT):
            cols = slice(t128 * 128, (t128 + 1) * 128)
            psF = sb[f"psF{t128 % 2}"]
            fns = []
            for k in range(ND):
                fns.append(lambda e, k=k, t128=t128, psF=psF: e.matmul(
                    psF[:, :], sb[f"xqT{k}"][:, t128 * 128:(t128 + 1) * 128],
                    sb[f"wtn{k}"][:, :], start=(k == 0), stop=(k == ND - 1)))
            v_mm = TE.chain(fns, deps=[(ve, v_wq), (ve, v_xq_done), rel_psF[t128 % 2]])
            v_e1 = SC.op(lambda e, t128=t128, psF=psF: e.activation(
                sb["ef"][:, :], psF[:, :], AF.Identity, scale=sb["sxall"][:, t128:t128 + 1]),
                deps=[(pe, v_mm), rel_ef[0]])
            rel_psF[t128 % 2] = (ac, v_e1)
            v_sl = SY.dma(sb["sl1"][:, :], dr["S1"][cols, ncols], deps=[rel_sl1])
            v_e2 = VE.op(lambda e: e.tensor_tensor(sb["e2"][:, :], sb["ef"][:, :],
                                                   sb["gmb"][:, :], ALU.mult),
                         deps=[(ac, v_e1), (ve, v_bb)])
            rel_ef[0] = (ve, v_e2)
            v_e3 = VE.op(lambda e: e.tensor_tensor(sb["e2"][:, :], sb["e2"][:, :],
                                                   sb["bb"][:, :], ALU.add),
                         deps=[(ve, v_e2)])
            v_e4 = VE.op(lambda e: e.tensor_tensor(sb["s2t"][:, :], sb["e2"][:, :],
                                                   sb["sl1"][:, :], ALU.add),
                         deps=[(ve, v_e3), (sy, v_sl), rel_s2])
            rel_sl1 = (ve, v_e4)
            v_st = SY.dma(dr["S2"][cols, ncols], sb["s2t"][:, :], deps=[(ve, v_e4)])
            rel_s2 = (sy, v_st)
    SY.wait(sy, sy.v)
    GD.wait(gd, gd.v)


TEXT = T + 128
DI = 4096
DM1 = 8192
DH1 = 5632
NH1 = DH1 // 128

def decl_C_weights(din):
    din("winT", [Dm, DM1], F32)
    din("convw", [DI, 4], F32)
    din("convb", [DI, 1], F32)
    din("woutT", [DI, Dm], F32)
    din("wm1T", [Dm, DM1], F32)
    din("wm2T", [DM1, Dm], F32)
    din("w1T", [Dm, DH1], F32)
    din("w3T", [Dm, DH1], F32)
    din("w2T", [DH1, Dm], F32)

def decl_C_scratch(dint):
    dint("Z0", [T, Dm], F32)
    dint("Z1", [T, Dm], F32)
    dint("S3", [T, Dm], F32)
    dint("Y", [NT, DI, 128], BF16)
    dint("R", [NT, DM1, 128], BF16)
    dint("Hd", [T, DH1], F32)
    dint("HqT", [NT, DH1, 128], BF16)

def build_C(tab):
    nc = bass.Bass("TRN2", target_bir_lowering=False)
    dr = {}
    def din(name, shape, dt):
        dr[name] = nc.dram_tensor(name, shape, dt, kind="ExternalInput").ap()
    def dout(name, shape, dt):
        dr[name] = nc.dram_tensor(name, shape, dt, kind="ExternalOutput").ap()
    def dint(name, shape, dt):
        dr[name] = nc.dram_tensor(name, shape, dt, kind="Internal").ap()
    din("S2", [T, Dm], F32)
    din("halo", [3, Dm], F32)
    decl_C_weights(din)
    din("ident", [128, 128], BF16)
    dout("outp", [T, Dm], F32)
    decl_C_scratch(dint)
    phases_C(nc, dr, tab)
    return nc

def phases_C(nc, dr, tab, fused=False):
    with contextlib.ExitStack() as st:
        sb = {}
        def alloc(name, shape, dt):
            sb[name] = st.enter_context(nc.sbuf_tensor(f"c5_{name}", shape, dt))
        def palloc(name, shape, dt):
            sb[name] = st.enter_context(nc.psum_tensor(f"c5_{name}", shape, dt))
        for k in range(ND):
            alloc(f"hT{k}", [128, TEXT], BF16)
        alloc("s2t", [128, Dm], F32)
        alloc("xscr", [128, Dm], F32)
        alloc("z0t", [128, Dm], F32)
        alloc("hb", [128, Dm], BF16)
        alloc("stats", [128, 8], F32)
        alloc("c_ident", [128, 128], BF16)
        alloc("wb0", [128, 16 * 128], BF16); alloc("wb1", [128, 16 * 128], BF16)
        alloc("wz0", [128, 16 * 128], BF16); alloc("wz1", [128, 16 * 128], BF16)
        alloc("xps", [128, TEXT], F32)
        alloc("sz", [128, T], F32)
        alloc("cacc", [128, T], F32)
        alloc("yb", [128, T], BF16)
        alloc("cw", [128, 4], F32)
        alloc("cb", [128, 1], F32)
        palloc("pp0", [128, 512], F32); palloc("pp1", [128, 512], F32)
        palloc("pt0", [128, 128], BF16); palloc("pt1", [128, 128], BF16)
        ph = Phase(nc, st, "C5")
        emit_C5ab(ph, dr, sb, tab)
        ph.emit()

    with contextlib.ExitStack() as st:
        sb = {}
        def alloc(name, shape, dt):
            sb[name] = st.enter_context(nc.sbuf_tensor(f"c5c_{name}", shape, dt))
        def palloc(name, shape, dt):
            sb[name] = st.enter_context(nc.psum_tensor(f"c5c_{name}", shape, dt))
        alloc("wob", [128, 32 * 512], BF16)
        alloc("yc0", [128, 32 * 128], BF16); alloc("yc1", [128, 32 * 128], BF16)
        alloc("z0s", [128, 512], F32)
        alloc("z1s", [128, 512], F32)
        palloc("pp0", [128, 512], F32); palloc("pp1", [128, 512], F32)
        ph = Phase(nc, st, "C5c")
        emit_down(ph, dr, sb, "woutT", DI, "Y", "Z0", "Z1", None)
        ph.emit()

    with contextlib.ExitStack() as st:
        sb = {}
        def alloc(name, shape, dt):
            sb[name] = st.enter_context(nc.sbuf_tensor(f"c6_{name}", shape, dt))
        def palloc(name, shape, dt):
            sb[name] = st.enter_context(nc.psum_tensor(f"c6_{name}", shape, dt))
        for k in range(ND):
            alloc(f"hT{k}", [128, T], BF16)
        alloc("z1t", [128, Dm], F32)
        alloc("hb", [128, Dm], BF16)
        alloc("stats", [128, 8], F32)
        alloc("xscr", [128, Dm], F32)
        alloc("c_ident", [128, 128], BF16)
        alloc("wb0", [128, 16 * 128], BF16); alloc("wb1", [128, 16 * 128], BF16)
        alloc("rb", [128, T], BF16)
        palloc("pp0", [128, 512], F32); palloc("pp1", [128, 512], F32)
        palloc("pt0", [128, 128], BF16); palloc("pt1", [128, 128], BF16)
        ph = Phase(nc, st, "C6")
        emit_C6ab(ph, dr, sb, tab)
        ph.emit()

    with contextlib.ExitStack() as st:
        sb = {}
        def alloc(name, shape, dt):
            sb[name] = st.enter_context(nc.sbuf_tensor(f"c6c_{name}", shape, dt))
        def palloc(name, shape, dt):
            sb[name] = st.enter_context(nc.psum_tensor(f"c6c_{name}", shape, dt))
        alloc("wob", [128, 64 * 512], BF16)
        alloc("yc0", [128, 64 * 128], BF16); alloc("yc1", [128, 64 * 128], BF16)
        alloc("z0s", [128, 512], F32)
        alloc("z1s", [128, 512], F32)
        alloc("s2s", [128, 512], F32)
        palloc("pp0", [128, 512], F32); palloc("pp1", [128, 512], F32)
        ph = Phase(nc, st, "C6c")
        emit_down(ph, dr, sb, "wm2T", DM1, "R", "Z1", "S3", "S2")
        ph.emit()

    with contextlib.ExitStack() as st:
        sb = {}
        def alloc(name, shape, dt):
            sb[name] = st.enter_context(nc.sbuf_tensor(f"c7_{name}", shape, dt))
        def palloc(name, shape, dt):
            sb[name] = st.enter_context(nc.psum_tensor(f"c7_{name}", shape, dt))
        for k in range(ND):
            alloc(f"xqT{k}", [128, T], BF16)
        alloc("wst", [128, 512], F32)
        alloc("wsta", [128, 512], F32)
        alloc("wstb", [128, 512], F32)
        for k in range(ND):
            alloc(f"w1q{k}a", [128, 512], BF16)
            alloc(f"w3q{k}a", [128, 512], BF16)
            alloc(f"w1q{k}b", [128, 512], BF16)
            alloc(f"w3q{k}b", [128, 512], BF16)
        alloc("s3t", [128, Dm], F32)
        alloc("xscr", [128, Dm], F32)
        alloc("xn1", [128, Dm], F32)
        alloc("xqb", [128, Dm], BF16)
        alloc("stats", [128, 8], F32)
        alloc("sxall", [128, NT], F32)
        alloc("hmx", [128, NT], F32)
        alloc("dq1", [128, NT], F32)
        alloc("dq3", [128, NT], F32)
        alloc("wabs", [128, 512], BF16)
        alloc("onesb", [128, 1], BF16)
        alloc("onesr", [1, 128], F32)
        alloc("wsums", [1, 512], F32)
        alloc("wtot", [1, 8], F32)
        alloc("swb", [128, 8], F32)
        alloc("s1f", [128, 512], F32)
        alloc("s3f", [128, 512], F32)
        alloc("hf", [128, 512], F32)
        alloc("qs1", [128, 512], F32)
        alloc("qs2", [128, 512], F32)
        alloc("magicc", [128, 1], F32)
        alloc("tmp1", [128, 1], F32)
        alloc("c_ident", [128, 128], BF16)
        palloc("psW", [1, 512], F32)
        palloc("psB", [128, 8], F32)
        palloc("pF10", [128, 512], F32); palloc("pF11", [128, 512], F32)
        palloc("pF30", [128, 512], F32); palloc("pF31", [128, 512], F32)
        palloc("pt0", [128, 128], BF16); palloc("pt1", [128, 128], BF16)
        ph = Phase(nc, st, "C7")
        emit_C7ab(ph, dr, sb, tab, skip_wscale=fused)
        ph.emit()

    with contextlib.ExitStack() as st:
        sb = {}
        def alloc(name, shape, dt):
            sb[name] = st.enter_context(nc.sbuf_tensor(f"c7c_{name}", shape, dt))
        def palloc(name, shape, dt):
            sb[name] = st.enter_context(nc.psum_tensor(f"c7c_{name}", shape, dt))
        alloc("hrow", [128, DH1], F32)
        alloc("hqb", [128, DH1], BF16)
        alloc("hqt", [128, NH1 * 128], BF16)
        alloc("hq0", [128, NH1 * 128], BF16); alloc("hq1", [128, NH1 * 128], BF16)
        alloc("wsta", [128, 512], F32)
        alloc("wstb", [128, 512], F32)
        for k in range(NH1):
            alloc(f"w2q{k}A", [128, 512], BF16)
            alloc(f"w2q{k}B", [128, 512], BF16)
        alloc("shc", [128, NT], F32)
        alloc("dqh", [128, NT], F32)
        alloc("s3s", [128, 512], F32)
        alloc("of", [128, 512], F32)
        alloc("oo", [128, 512], F32)
        alloc("qs1", [128, 512], F32)
        alloc("qs2", [128, 512], F32)
        alloc("magicc", [128, 1], F32)
        alloc("swc", [128, 8], F32)
        alloc("c_ident", [128, 128], BF16)
        palloc("pp0", [128, 512], F32); palloc("pp1", [128, 512], F32)
        palloc("pt0", [128, 128], BF16); palloc("pt1", [128, 128], BF16)
        ph = Phase(nc, st, "C7c")
        emit_C7c(ph, dr, sb, tab)
        ph.emit()

def emit_rms_scale(ph, sb, src, stats, deps):
    SC, VE = ph.SC, ph.VE
    ac, ve = ph.ctrs["ac"], ph.ctrs["ve"]
    v_sq = SC.op(lambda e: e.activation(sb["xscr"][:, :], src[:, :], AF.Square,
                                        accum_out=sb["stats"][:, 1:2]), deps=deps)
    v_s = SC.op(lambda e: e.activation(sb["stats"][:, 2:3], sb["stats"][:, 1:2], AF.Sqrt))
    v_m = VE.op(lambda e: e.tensor_scalar(sb["stats"][:, 3:4], sb["stats"][:, 2:3],
                                          1e-12, None, ALU.max), deps=[(ac, v_s)])
    v_r = VE.op(lambda e: e.reciprocal(sb["stats"][:, 3:4], sb["stats"][:, 3:4]),
                deps=[(ve, v_m)])
    v_f = VE.op(lambda e: e.tensor_scalar(sb["stats"][:, 3:4], sb["stats"][:, 3:4],
                                          float(np.float32(np.sqrt(np.float32(Dm)))), None,
                                          ALU.mult), deps=[(ve, v_r)])
    return v_f

def emit_C5ab(ph, dr, sb, tab):
    TE, SC, VE, SY, GD = ph.TE, ph.SC, ph.VE, ph.SY, ph.GD
    pe, ac, ve, sy, gd = (ph.ctrs[n] for n in ("pe", "ac", "ve", "sy", "gd"))
    v_id = SY.dma(sb["c_ident"][:, :], dr["ident"])

    rel_s2t = (ac, 0)
    rel_z0 = []
    rel_hb = (pe, 0)
    rel_pt = [(ve, 0), (ve, 0)]
    for it in range(NT + 1):
        tcol = it * 128
        if it == 0:
            v_ms = VE.op(lambda e: e.memset(sb["s2t"][:, :], 0.0), deps=[rel_s2t])
            v_x = SY.dma(sb["s2t"][125:128, :], dr["halo"], deps=[(ve, v_ms)])
        else:
            t128 = it - 1
            v_x = SY.dma(sb["s2t"][:, :], dr["S2"][t128 * 128:(t128 + 1) * 128, :],
                         deps=[rel_s2t])
        v_ln = emit_ln(ph, sb["s2t"], sb["xscr"], sb["stats"], 1e-5,
                       deps=[(sy, v_x)])
        v_z0 = SC.op(lambda e: e.activation(sb["z0t"][:, :], sb["s2t"][:, :], AF.Identity,
                                            scale=sb["stats"][:, 6:7], bias=sb["stats"][:, 7:8]),
                     deps=[(ve, v_ln)] + rel_z0)
        rel_s2t = (ac, v_z0)
        if it > 0:
            v_w0 = SY.dma(dr["Z0"][(it - 1) * 128:it * 128, :], sb["z0t"][:, :],
                          deps=[(ac, v_z0)])
        v_rs = emit_rms_scale(ph, sb, sb["z0t"], sb["stats"], deps=[(ac, v_z0)])
        v_hb = VE.op(lambda e: e.tensor_scalar(sb["hb"][:, :], sb["z0t"][:, :],
                                               sb["stats"][:, 3:4], None, ALU.mult),
                     deps=[(ve, v_rs), rel_hb])
        rel_z0 = [(ve, v_hb)] if it == 0 else [(ve, v_hb), (sy, v_w0)]
        for k in range(ND):
            pt = sb[f"pt{k % 2}"]
            v_tr = TE.op(lambda e, k=k, pt=pt: e.transpose(
                pt[:, :], sb["hb"][:, k * 128:(k + 1) * 128], sb["c_ident"][:, :]),
                deps=[(ve, v_hb), (sy, v_id), rel_pt[k % 2]])
            v_ev = VE.op(lambda e, k=k, tcol=tcol, pt=pt: e.tensor_copy(
                sb[f"hT{k}"][:, tcol:tcol + 128], pt[:, :]),
                deps=[(pe, v_tr)])
            rel_pt[k % 2] = (ve, v_ev)
        rel_hb = (pe, pe.v)
    v_hT = ve.v

    rel_wb = [(pe, 0), (pe, 0)]
    rel_wz = [(pe, 0), (pe, 0)]
    rel_pp = [(ac, 0), (ac, 0)]
    rel_xps = (ve, 0)
    rel_sz = (ve, 0)
    rel_cacc = (ac, 0)
    rel_xscr5 = (ve, 0)
    rel_yb = (sy, 0)
    tblocks = [(0, 512), (512, 512), (1024, 512), (1536, 512), (2048, 128)]
    for ep in range(DI // 128):
        slot = ep % 2
        v_wx = GD.dma(sb[f"wb{slot}"][:, :].rearrange("p (c j) -> p c j", j=128),
                      dr["winT"][:, ep * 128:(ep + 1) * 128].rearrange("(c p) j -> p c j", p=128),
                      deps=[rel_wb[slot]])
        v_wz = GD.dma(sb[f"wz{slot}"][:, :].rearrange("p (c j) -> p c j", j=128),
                      dr["winT"][:, DI + ep * 128: DI + (ep + 1) * 128].rearrange("(c p) j -> p c j", p=128),
                      deps=[rel_wz[slot]])
        v_cw = SY.dma(sb["cw"][:, :], dr["convw"][ep * 128:(ep + 1) * 128, :])
        v_cb = SY.dma(sb["cb"][:, :], dr["convb"][ep * 128:(ep + 1) * 128, :])
        v_xp = 0
        for (c0, cw_) in tblocks:
            pp = sb[f"pp{(c0 // 512) % 2}"]
            fns = []
            for k in range(ND):
                fns.append(lambda e, k=k, c0=c0, cw_=cw_, pp=pp, slot=slot: e.matmul(
                    pp[:, 0:cw_], sb[f"wb{slot}"][:, k * 128:(k + 1) * 128],
                    sb[f"hT{k}"][:, c0:c0 + cw_], start=(k == 0), stop=(k == ND - 1)))
            v_mm = TE.chain(fns, deps=[(gd, v_wx), (ve, v_hT), rel_pp[(c0 // 512) % 2]])
            v_xp = SC.op(lambda e, c0=c0, cw_=cw_, pp=pp: e.activation(
                sb["xps"][:, c0:c0 + cw_], pp[:, 0:cw_], AF.Identity),
                deps=[(pe, v_mm), rel_xps])
            rel_pp[(c0 // 512) % 2] = (ac, v_xp)
        v_sz = 0
        for c0 in range(0, T, 512):
            pp = sb[f"pp{(c0 // 512) % 2}"]
            fns = []
            for k in range(ND):
                fns.append(lambda e, k=k, c0=c0, pp=pp, slot=slot: e.matmul(
                    pp[:, :], sb[f"wz{slot}"][:, k * 128:(k + 1) * 128],
                    sb[f"hT{k}"][:, 128 + c0:128 + c0 + 512], start=(k == 0), stop=(k == ND - 1)))
            v_mm = TE.chain(fns, deps=[(gd, v_wz), (ve, v_hT), rel_pp[(c0 // 512) % 2]])
            v_sz = SC.op(lambda e, c0=c0, pp=pp: e.activation(
                sb["sz"][:, c0:c0 + 512], pp[:, :], AF.Silu),
                deps=[(pe, v_mm), rel_sz])
            rel_pp[(c0 // 512) % 2] = (ac, v_sz)
        rel_wb[slot] = (pe, pe.v)
        rel_wz[slot] = (pe, pe.v)
        v_c0 = VE.op(lambda e: e.tensor_scalar(sb["cacc"][:, :], sb["xps"][:, 125:125 + T],
                                               sb["cw"][:, 0:1], None, ALU.mult),
                     deps=[(ac, v_xp), (sy, v_cw), rel_cacc])
        v_c1 = VE.op(lambda e: e.scalar_tensor_tensor(sb["cacc"][:, :], sb["xps"][:, 126:126 + T],
                                                      sb["cw"][:, 1:2], sb["cacc"][:, :],
                                                      ALU.mult, ALU.add), deps=[(ve, v_c0)])
        v_c2 = VE.op(lambda e: e.scalar_tensor_tensor(sb["cacc"][:, :], sb["xps"][:, 127:127 + T],
                                                      sb["cw"][:, 2:3], sb["cacc"][:, :],
                                                      ALU.mult, ALU.add), deps=[(ve, v_c1)])
        v_c3 = VE.op(lambda e: e.scalar_tensor_tensor(sb["cacc"][:, :], sb["xps"][:, 128:128 + T],
                                                      sb["cw"][:, 3:4], sb["cacc"][:, :],
                                                      ALU.mult, ALU.add), deps=[(ve, v_c2)])
        rel_xps = (ve, v_c3)
        v_sx = SC.op(lambda e: e.activation(sb["xscr"][:, 0:T], sb["cacc"][:, :], AF.Silu,
                                            bias=sb["cb"][:, 0:1]),
                     deps=[(ve, v_c3), (sy, v_cb), rel_cacc, rel_xscr5])
        rel_cacc = (ac, v_sx)
        v_y = VE.op(lambda e: e.tensor_tensor(sb["yb"][:, :], sb["xscr"][:, 0:T],
                                              sb["sz"][:, :], ALU.mult),
                    deps=[(ac, v_sx), (ac, v_sz), rel_yb])
        rel_sz = (ve, v_y)
        rel_xscr5 = (ve, v_y)
        v_st = SY.dma(dr["Y"][:, ep * 128:(ep + 1) * 128, :].rearrange("tb p t -> p tb t"),
                      sb["yb"][:, :].rearrange("p (tb t) -> p tb t", t=128),
                      deps=[(ve, v_y)])
        rel_yb = (sy, v_st)
    SY.wait(sy, sy.v)
    GD.wait(gd, gd.v)

def emit_down(ph, dr, sb, wname, E, yname, resname, outname, res2name):
    TE, SC, VE, SY, GD = ph.TE, ph.SC, ph.VE, ph.SY, ph.GD
    pe, ac, ve, sy, gd = (ph.ctrs[n] for n in ("pe", "ac", "ve", "sy", "gd"))
    NE = E // 128
    rel_wob = (pe, 0)
    rel_yc = [(pe, 0), (pe, 0)]
    rel_pp = [(ve, 0), (ve, 0)]
    rel_z0 = (ve, 0)
    rel_z1 = (sy, 0)
    rel_s2 = (ve, 0)
    for dblk in range(4):
        dcols = slice(dblk * 512, (dblk + 1) * 512)
        v_w = GD.dma(sb["wob"][:, :].rearrange("p (c j) -> p c j", j=512),
                     dr[wname][:, dcols].rearrange("(c p) j -> p c j", p=128),
                     deps=[rel_wob])
        for t128 in range(NT):
            cols = slice(t128 * 128, (t128 + 1) * 128)
            slot = t128 % 2
            yc = sb[f"yc{slot}"]
            v_y = SY.dma(yc[:, :].rearrange("p (c t) -> p c t", t=128),
                         dr[yname][t128].rearrange("(c p) t -> p c t", p=128),
                         deps=[rel_yc[slot]])
            pp = sb[f"pp{slot}"]
            fns = []
            for k in range(NE):
                fns.append(lambda e, k=k, pp=pp, yc=yc: e.matmul(
                    pp[:, :], yc[:, k * 128:(k + 1) * 128],
                    sb["wob"][:, k * 512:(k + 1) * 512], start=(k == 0), stop=(k == NE - 1)))
            v_mm = TE.chain(fns, deps=[(gd, v_w), (sy, v_y), rel_pp[slot]])
            rel_yc[slot] = (pe, v_mm)
            v_z0 = SY.dma(sb["z0s"][:, :], dr[resname][cols, dcols], deps=[rel_z0])
            v_a = VE.op(lambda e, pp=pp: e.tensor_tensor(sb["z1s"][:, :], pp[:, :],
                                                         sb["z0s"][:, :], ALU.add),
                        deps=[(pe, v_mm), (sy, v_z0), rel_z1])
            rel_pp[slot] = (ve, v_a)
            rel_z0 = (ve, v_a)
            if res2name is not None:
                v_s2 = SY.dma(sb["s2s"][:, :], dr[res2name][cols, dcols], deps=[rel_s2])
                v_a = VE.op(lambda e: e.tensor_tensor(sb["z1s"][:, :], sb["z1s"][:, :],
                                                      sb["s2s"][:, :], ALU.add),
                            deps=[(ve, v_a), (sy, v_s2)])
                rel_s2 = (ve, v_a)
            v_st = SY.dma(dr[outname][cols, dcols], sb["z1s"][:, :], deps=[(ve, v_a)])
            rel_z1 = (sy, v_st)
        rel_wob = (pe, pe.v)
    SY.wait(sy, sy.v)
    GD.wait(gd, gd.v)

def emit_C6ab(ph, dr, sb, tab):
    TE, SC, VE, SY, GD = ph.TE, ph.SC, ph.VE, ph.SY, ph.GD
    pe, ac, ve, sy, gd = (ph.ctrs[n] for n in ("pe", "ac", "ve", "sy", "gd"))
    v_id = SY.dma(sb["c_ident"][:, :], dr["ident"])

    rel_z1t = (ac, 0)
    rel_hb = (pe, 0)
    rel_pt = [(ve, 0), (ve, 0)]
    for t128 in range(NT):
        v_x = SY.dma(sb["z1t"][:, :], dr["Z1"][t128 * 128:(t128 + 1) * 128, :],
                     deps=[rel_z1t])
        v_rs = emit_rms_scale(ph, sb, sb["z1t"], sb["stats"], deps=[(sy, v_x)])
        v_hb = VE.op(lambda e: e.tensor_scalar(sb["hb"][:, :], sb["z1t"][:, :],
                                               sb["stats"][:, 3:4], None, ALU.mult),
                     deps=[(ve, v_rs), rel_hb])
        rel_z1t = (ve, v_hb)
        for k in range(ND):
            pt = sb[f"pt{k % 2}"]
            v_tr = TE.op(lambda e, k=k, pt=pt: e.transpose(
                pt[:, :], sb["hb"][:, k * 128:(k + 1) * 128], sb["c_ident"][:, :]),
                deps=[(ve, v_hb), (sy, v_id), rel_pt[k % 2]])
            v_ev = VE.op(lambda e, k=k, t128=t128, pt=pt: e.tensor_copy(
                sb[f"hT{k}"][:, t128 * 128:(t128 + 1) * 128], pt[:, :]),
                deps=[(pe, v_tr)])
            rel_pt[k % 2] = (ve, v_ev)
        rel_hb = (pe, pe.v)
    v_hT = ve.v

    rel_wb = [(pe, 0), (pe, 0)]
    rel_pp = [(ac, 0), (ac, 0)]
    rel_rb = (sy, 0)
    for ec in range(DM1 // 128):
        slot = ec % 2
        v_w = GD.dma(sb[f"wb{slot}"][:, :].rearrange("p (c j) -> p c j", j=128),
                     dr["wm1T"][:, ec * 128:(ec + 1) * 128].rearrange("(c p) j -> p c j", p=128),
                     deps=[rel_wb[slot]])
        v_r = 0
        for c0 in range(0, T, 512):
            pp = sb[f"pp{(c0 // 512) % 2}"]
            fns = []
            for k in range(ND):
                fns.append(lambda e, k=k, c0=c0, pp=pp, slot=slot: e.matmul(
                    pp[:, :], sb[f"wb{slot}"][:, k * 128:(k + 1) * 128],
                    sb[f"hT{k}"][:, c0:c0 + 512], start=(k == 0), stop=(k == ND - 1)))
            v_mm = TE.chain(fns, deps=[(gd, v_w), (ve, v_hT), rel_pp[(c0 // 512) % 2]])
            v_r = SC.op(lambda e, c0=c0, pp=pp: e.activation(
                sb["rb"][:, c0:c0 + 512], pp[:, :], AF.Relu),
                deps=[(pe, v_mm), rel_rb])
            rel_pp[(c0 // 512) % 2] = (ac, v_r)
        rel_wb[slot] = (pe, pe.v)
        v_st = SY.dma(dr["R"][:, ec * 128:(ec + 1) * 128, :].rearrange("tb p t -> p tb t"),
                      sb["rb"][:, :].rearrange("p (tb t) -> p tb t", t=128),
                      deps=[(ac, v_r)])
        rel_rb = (sy, v_st)
    SY.wait(sy, sy.v)
    GD.wait(gd, gd.v)

def emit_C7ab(ph, dr, sb, tab, skip_wscale=False):
    TE, SC, VE, SY, GD = ph.TE, ph.SC, ph.VE, ph.SY, ph.GD
    pe, ac, ve, sy, gd = (ph.ctrs[n] for n in ("pe", "ac", "ve", "sy", "gd"))
    v_id = SY.dma(sb["c_ident"][:, :], dr["ident"])
    v_ones = VE.op(lambda e: e.memset(sb["onesb"][:, :], 1.0))
    v_ones = VE.op(lambda e: e.memset(sb["onesr"][:, :], 1.0))
    v_w0 = VE.op(lambda e: e.memset(sb["wtot"][:, :], 0.0))
    v_mg = VE.op(lambda e: e.memset(sb["magicc"][:, :], MAGIC))

    if skip_wscale:
        v_ld = SY.dma(sb["wtot"][:, :], dr["WSC"][:, 0:8])
        v_bc = TE.op(lambda e: e.matmul(sb["psB"][:, :], sb["onesr"][:, :],
                                        sb["wtot"][:, :], start=True, stop=True),
                     deps=[(sy, v_ld), (ve, v_ones)])
        v_swb = VE.op(lambda e: e.tensor_copy(sb["swb"][:, :], sb["psB"][:, :]),
                      deps=[(pe, v_bc)])
    else:
        rel_wst = (ve, 0)
        rel_wabs = (pe, 0)
        rel_psW = (ve, 0)
        wmeta = [("w1T", ND, DH1, 0), ("w3T", ND, DH1, 2), ("w2T", NH1, Dm, 4)]
        for (wn, NK, NCOL, coli) in wmeta:
            NB = NCOL // 512
            ntile = NK * NB
            ti = 0
            v_g = 0
            for nb in range(NB):
                for k in range(NK):
                    v_w = SY.dma(sb["wst"][:, 0:512], dr[wn][k * 128:(k + 1) * 128,
                                                             nb * 512:(nb + 1) * 512],
                                 deps=[rel_wst])
                    v_ab = SC.op(lambda e: e.activation(
                        sb["wabs"][:, :], sb["wst"][:, 0:512], AF.Abs),
                        deps=[(sy, v_w), rel_wabs])
                    rel_wst = (ac, v_ab)
                    v_g = TE.op(lambda e, s=(ti == 0), t=(ti == ntile - 1): e.matmul(
                        sb["psW"][:, :], sb["onesb"][:, :], sb["wabs"][:, :], start=s, stop=t),
                        deps=[(ac, v_ab), (ve, v_ones)] + ([rel_psW] if ti == 0 else []))
                    rel_wabs = (pe, v_g)
                    ti += 1
            v_rs = VE.op(lambda e, coli=coli: e.reduce_sum(
                sb["wtot"][:, coli:coli + 1], sb["psW"][:, :], axis=AX.X),
                deps=[(pe, v_g)])
            rel_psW = (ve, v_rs)
            v_m = VE.op(lambda e, coli=coli: e.tensor_scalar(
                sb["wtot"][:, coli + 1:coli + 2], sb["wtot"][:, coli:coli + 1],
                float(np.float32(1.0 / (DH1 * Dm))), 1e-5,
                ALU.mult, ALU.max), deps=[(ve, v_rs)])
            v_r = VE.op(lambda e, coli=coli: e.reciprocal(
                sb["wtot"][:, coli:coli + 1], sb["wtot"][:, coli + 1:coli + 2]),
                deps=[(ve, v_m)])
        v_bc = TE.op(lambda e: e.matmul(sb["psB"][:, :], sb["onesr"][:, :], sb["wtot"][:, :],
                                        start=True, stop=True), deps=[(ve, ve.v)])
        v_swb = VE.op(lambda e: e.tensor_copy(sb["swb"][:, :], sb["psB"][:, :]),
                      deps=[(pe, v_bc)])

    rel_s3t = (ac, 0)
    rel_xn1 = (ve, 0)
    rel_xscr = (ve, 0)
    rel_xqb = (pe, 0)
    rel_pt = [(ve, 0), (ve, 0)]
    for t128 in range(NT):
        cols = slice(t128 * 128, (t128 + 1) * 128)
        v_x = SY.dma(sb["s3t"][:, :], dr["S3"][cols, :], deps=[rel_s3t])
        v_ln = emit_ln(ph, sb["s3t"], sb["xscr"], sb["stats"], 1e-5,
                       deps=[(sy, v_x), rel_xscr])
        v_n1 = SC.op(lambda e: e.activation(sb["xn1"][:, :], sb["s3t"][:, :], AF.Identity,
                                            scale=sb["stats"][:, 6:7], bias=sb["stats"][:, 7:8]),
                     deps=[(ve, v_ln), rel_xn1])
        rel_s3t = (ac, v_n1)
        v_mx = VE.op(lambda e: e.reduce_max(sb["stats"][:, 4:5], sb["xn1"][:, :], axis=AX.X,
                                            apply_absolute_value=True),
                     deps=[(ac, v_n1)])
        v_c = VE.op(lambda e: e.tensor_scalar(sb["stats"][:, 4:5], sb["stats"][:, 4:5],
                                              1e-5, None, ALU.max), deps=[(ve, v_mx)])
        v_dq = VE.op(lambda e, t128=t128: e.tensor_scalar(
            sb["sxall"][:, t128:t128 + 1], sb["stats"][:, 4:5],
            float(np.float32(1.0 / 127.0)), None, ALU.mult), deps=[(ve, v_c)])
        v_r = VE.op(lambda e: e.reciprocal(sb["stats"][:, 5:6], sb["stats"][:, 4:5]),
                    deps=[(ve, v_dq)])
        v_s = VE.op(lambda e: e.tensor_scalar(sb["stats"][:, 5:6], sb["stats"][:, 5:6],
                                              127.0, None, ALU.mult), deps=[(ve, v_r)])
        v_q1 = VE.op(lambda e: e.tensor_scalar(sb["xscr"][:, :], sb["xn1"][:, :],
                                               sb["stats"][:, 5:6], MAGIC, ALU.mult, ALU.add),
                     deps=[(ve, v_s)])
        v_q2 = VE.op(lambda e: e.tensor_scalar(sb["xqb"][:, :], sb["xscr"][:, :],
                                               MAGIC, None, ALU.subtract),
                     deps=[(ve, v_q1), rel_xqb])
        rel_xn1 = (ve, v_q2)
        rel_xscr = (ve, v_q2)
        for k in range(ND):
            pt = sb[f"pt{k % 2}"]
            v_tr = TE.op(lambda e, k=k, pt=pt: e.transpose(
                pt[:, :], sb["xqb"][:, k * 128:(k + 1) * 128], sb["c_ident"][:, :]),
                deps=[(ve, v_q2), (sy, v_id), rel_pt[k % 2]])
            v_ev = VE.op(lambda e, k=k, t128=t128, pt=pt: e.tensor_copy(
                sb[f"xqT{k}"][:, t128 * 128:(t128 + 1) * 128], pt[:, :]),
                deps=[(pe, v_tr)])
            rel_pt[k % 2] = (ve, v_ev)
        rel_xqb = (pe, pe.v)
    v_xq = ve.v
    v_d1 = VE.op(lambda e: e.tensor_scalar(sb["dq1"][:, :], sb["sxall"][:, :],
                                           sb["swb"][:, 1:2], None, ALU.mult),
                 deps=[(ve, v_swb), (ve, v_xq)])
    v_d3 = VE.op(lambda e: e.tensor_scalar(sb["dq3"][:, :], sb["sxall"][:, :],
                                           sb["swb"][:, 3:4], None, ALU.mult))
    v_hm = VE.op(lambda e: e.memset(sb["hmx"][:, :], 0.0))

    NBLK = DH1 // 512
    rel_wstg = [(ac, 0), (ac, 0)]
    rel_set = [(pe, 0), (pe, 0)]
    v_wq_set = [0, 0]
    rel_pF1 = [(ac, 0), (ac, 0)]
    rel_pF3 = [(ac, 0), (ac, 0)]
    rel_s1f = (ve, 0)
    rel_s3f = (ve, 0)
    rel_hf = (sy, 0)
    rel_qs = (ve, 0)
    sfx = ["a", "b"]

    def quant_tile(wn, qpre, swcol, k, nblk):
        nonlocal rel_qs
        st = sb[f"wst{sfx[k % 2]}"]
        ncols = slice(nblk * 512, (nblk + 1) * 512)
        v_w = SY.dma(st[:, :], dr[wn][k * 128:(k + 1) * 128, ncols],
                     deps=[rel_wstg[k % 2], rel_set[nblk % 2]])
        v_t1 = SC.op(lambda e, st=st, swcol=swcol: e.activation(
            sb["qs1"][:, :], st[:, :], AF.Identity,
            scale=sb["swb"][:, swcol:swcol + 1], bias=sb["magicc"][:, 0:1]),
            deps=[(sy, v_w), (ve, v_swb), (ve, v_mg), rel_qs])
        rel_wstg[k % 2] = (ac, v_t1)
        v_t2 = VE.op(lambda e: e.tensor_scalar(sb["qs2"][:, :], sb["qs1"][:, :],
                                               MAGIC, 1.0, ALU.subtract, ALU.min),
                     deps=[(ac, v_t1)])
        v_l = VE.op(lambda e, k=k, qpre=qpre, nblk=nblk: e.tensor_scalar(
            sb[f"{qpre}{k}{sfx[nblk % 2]}"][:, :], sb["qs2"][:, :], -1.0, None, ALU.max),
            deps=[(ve, v_t2)])
        rel_qs = (ve, v_l)
        return v_l

    for k in range(ND):
        quant_tile("w1T", "w1q", 0, k, 0)
        v_wq_set[0] = quant_tile("w3T", "w3q", 2, k, 0)
    for nblk in range(NBLK):
        sx = sfx[nblk % 2]
        v_mmlast = 0
        for t128 in range(NT):
            cols = slice(t128 * 128, (t128 + 1) * 128)
            ncols = slice(nblk * 512, (nblk + 1) * 512)
            slot = t128 % 2
            pF1, pF3 = sb[f"pF1{slot}"], sb[f"pF3{slot}"]
            fns = []
            for k in range(ND):
                fns.append(lambda e, k=k, pF1=pF1, t128=t128, sx=sx: e.matmul(
                    pF1[:, :], sb[f"xqT{k}"][:, t128 * 128:(t128 + 1) * 128],
                    sb[f"w1q{k}{sx}"][:, :], start=(k == 0), stop=(k == ND - 1)))
            v_m1 = TE.chain(fns, deps=[(ve, v_wq_set[nblk % 2]), (ve, v_xq),
                                       rel_pF1[slot]])
            fns = []
            for k in range(ND):
                fns.append(lambda e, k=k, pF3=pF3, t128=t128, sx=sx: e.matmul(
                    pF3[:, :], sb[f"xqT{k}"][:, t128 * 128:(t128 + 1) * 128],
                    sb[f"w3q{k}{sx}"][:, :], start=(k == 0), stop=(k == ND - 1)))
            v_m3 = TE.chain(fns, deps=[rel_pF3[slot]])
            v_mmlast = v_m3
            v_s1 = SC.op(lambda e, t128=t128, pF1=pF1: e.activation(
                sb["s1f"][:, :], pF1[:, :], AF.Silu, scale=sb["dq1"][:, t128:t128 + 1]),
                deps=[(pe, v_m1), (ve, v_d1), rel_s1f])
            rel_pF1[slot] = (ac, v_s1)
            v_s3 = SC.op(lambda e, t128=t128, pF3=pF3: e.activation(
                sb["s3f"][:, :], pF3[:, :], AF.Identity, scale=sb["dq3"][:, t128:t128 + 1]),
                deps=[(pe, v_m3), (ve, v_d3), rel_s3f])
            rel_pF3[slot] = (ac, v_s3)
            v_h = VE.op(lambda e: e.tensor_tensor(sb["hf"][:, :], sb["s1f"][:, :],
                                                  sb["s3f"][:, :], ALU.mult),
                        deps=[(ac, v_s1), (ac, v_s3), rel_hf])
            rel_s1f = (ve, v_h)
            rel_s3f = (ve, v_h)
            v_mx = VE.op(lambda e: e.reduce_max(sb["tmp1"][:, :], sb["hf"][:, :], axis=AX.X,
                                                apply_absolute_value=True),
                         deps=[(ve, v_h)])
            v_hm = VE.op(lambda e, t128=t128: e.tensor_tensor(
                sb["hmx"][:, t128:t128 + 1], sb["hmx"][:, t128:t128 + 1],
                sb["tmp1"][:, :], ALU.max), deps=[(ve, v_mx), (ve, v_hm)])
            v_st = SY.dma(dr["Hd"][cols, ncols], sb["hf"][:, :], deps=[(ve, v_h)])
            rel_hf = (sy, v_st)
            if nblk + 1 < NBLK and t128 < 8:
                for k in (2 * t128, 2 * t128 + 1):
                    quant_tile("w1T", "w1q", 0, k, nblk + 1)
                    v_wq_set[(nblk + 1) % 2] = quant_tile("w3T", "w3q", 2, k, nblk + 1)
        rel_set[nblk % 2] = (pe, v_mmlast)
    v_c = VE.op(lambda e: e.tensor_scalar(sb["hmx"][:, :], sb["hmx"][:, :],
                                          1e-5, None, ALU.max), deps=[(ve, ve.v)])
    v_r = VE.op(lambda e: e.reciprocal(sb["dq1"][:, :], sb["hmx"][:, :]),
                deps=[(ve, v_c), (ac, ac.v)])
    v_s = VE.op(lambda e: e.tensor_scalar(sb["dq1"][:, :], sb["dq1"][:, :],
                                          127.0, None, ALU.mult), deps=[(ve, v_r)])
    v_q = VE.op(lambda e: e.tensor_scalar(sb["dq3"][:, :], sb["hmx"][:, :],
                                          float(np.float32(1.0 / 127.0)), None, ALU.mult),
                deps=[(ve, v_s), (ac, ac.v)])
    v_q2 = VE.op(lambda e: e.tensor_scalar(sb["dq3"][:, :], sb["dq3"][:, :],
                                           sb["swb"][:, 5:6], None, ALU.mult),
                 deps=[(ve, v_q)])
    SY.dma(dr["Z0"][0:128, 0:NT], sb["dq1"][:, :], deps=[(ve, v_s)])
    SY.dma(dr["Z0"][128:256, 0:NT], sb["dq3"][:, :], deps=[(ve, v_q2)])
    SY.dma(dr["Z0"][256:384, 0:8], sb["swb"][:, :], deps=[(ve, v_swb)])
    SY.wait(sy, sy.v)
    GD.wait(gd, gd.v)

def emit_C7c(ph, dr, sb, tab):
    TE, SC, VE, SY, GD = ph.TE, ph.SC, ph.VE, ph.SY, ph.GD
    pe, ac, ve, sy, gd = (ph.ctrs[n] for n in ("pe", "ac", "ve", "sy", "gd"))
    v_id = SY.dma(sb["c_ident"][:, :], dr["ident"])
    v_sc = SY.dma(sb["shc"][:, :], dr["Z0"][0:128, 0:NT])
    v_dq = SY.dma(sb["dqh"][:, :], dr["Z0"][128:256, 0:NT])
    v_sw = SY.dma(sb["swc"][:, :], dr["Z0"][256:384, 0:8])
    v_mg = VE.op(lambda e: e.memset(sb["magicc"][:, :], MAGIC))

    rel_hrow = (ve, 0)
    rel_hqb = (pe, 0)
    rel_hqt = (sy, 0)
    rel_pt = [(ve, 0), (ve, 0)]
    for t128 in range(NT):
        cols = slice(t128 * 128, (t128 + 1) * 128)
        v_x = SY.dma(sb["hrow"][:, :], dr["Hd"][cols, :], deps=[rel_hrow])
        v_q1 = VE.op(lambda e, t128=t128: e.tensor_scalar(
            sb["hrow"][:, :], sb["hrow"][:, :], sb["shc"][:, t128:t128 + 1], MAGIC,
            ALU.mult, ALU.add), deps=[(sy, v_x), (sy, v_sc)])
        v_q2 = VE.op(lambda e: e.tensor_scalar(sb["hqb"][:, :], sb["hrow"][:, :],
                                               MAGIC, None, ALU.subtract),
                     deps=[(ve, v_q1), rel_hqb])
        rel_hrow = (ve, v_q2)
        v_tre = 0
        for k in range(NH1):
            pt = sb[f"pt{k % 2}"]
            v_tr = TE.op(lambda e, k=k, pt=pt: e.transpose(
                pt[:, :], sb["hqb"][:, k * 128:(k + 1) * 128], sb["c_ident"][:, :]),
                deps=[(ve, v_q2), (sy, v_id), rel_pt[k % 2], rel_hqt])
            v_tre = VE.op(lambda e, k=k, pt=pt: e.tensor_copy(
                sb["hqt"][:, k * 128:(k + 1) * 128], pt[:, :]),
                deps=[(pe, v_tr)])
            rel_pt[k % 2] = (ve, v_tre)
        rel_hqb = (pe, pe.v)
        v_st = SY.dma(dr["HqT"][t128].rearrange("(c p) t -> p c t", p=128),
                      sb["hqt"][:, :].rearrange("p (c t) -> p c t", t=128),
                      deps=[(ve, v_tre)])
        rel_hqt = (sy, v_st)

    v_p1 = sy.v
    rel_wstg = [(ac, 0), (ac, 0)]
    rel_set = (pe, 0)
    rel_hq = [(pe, 0), (pe, 0)]
    rel_ppA = (ac, 0)
    rel_ppB = (ac, 0)
    rel_of = (ve, 0)
    rel_oo = (sy, 0)
    rel_s3s = (ve, 0)
    rel_qs = (ve, 0)
    sfx = ["A", "B"]
    for dp in range(2):
        v_l = 0
        for sxi in range(2):
            dblk = 2 * dp + sxi
            dcols = slice(dblk * 512, (dblk + 1) * 512)
            for k in range(NH1):
                st = sb[f"wst{sfx[k % 2].lower()}"]
                v_w = SY.dma(st[:, :], dr["w2T"][k * 128:(k + 1) * 128, dcols],
                             deps=[rel_wstg[k % 2], rel_set])
                v_t1 = SC.op(lambda e, st=st: e.activation(
                    sb["qs1"][:, :], st[:, :], AF.Identity,
                    scale=sb["swc"][:, 4:5], bias=sb["magicc"][:, 0:1]),
                    deps=[(sy, v_w), (sy, v_sw), (ve, v_mg), rel_qs])
                rel_wstg[k % 2] = (ac, v_t1)
                v_t2 = VE.op(lambda e: e.tensor_scalar(sb["qs2"][:, :], sb["qs1"][:, :],
                                                       MAGIC, 1.0, ALU.subtract, ALU.min),
                             deps=[(ac, v_t1)])
                v_l = VE.op(lambda e, k=k, sxi=sxi: e.tensor_scalar(
                    sb[f"w2q{k}{sfx[sxi]}"][:, :], sb["qs2"][:, :], -1.0, None, ALU.max),
                    deps=[(ve, v_t2)])
                rel_qs = (ve, v_l)
        v_wq = v_l
        v_mmlast = 0
        for t128 in range(NT):
            cols = slice(t128 * 128, (t128 + 1) * 128)
            slot = t128 % 2
            hq = sb[f"hq{slot}"]
            v_h = SY.dma(hq[:, :].rearrange("p (c t) -> p c t", t=128),
                         dr["HqT"][t128].rearrange("(c p) t -> p c t", p=128),
                         deps=[rel_hq[slot], (sy, v_p1)])
            for sxi in range(2):
                dblk = 2 * dp + sxi
                dcols = slice(dblk * 512, (dblk + 1) * 512)
                pp = sb[f"pp{sxi}"]
                fns = []
                for k in range(NH1):
                    fns.append(lambda e, k=k, pp=pp, hq=hq, sxi=sxi: e.matmul(
                        pp[:, :], hq[:, k * 128:(k + 1) * 128],
                        sb[f"w2q{k}{sfx[sxi]}"][:, :], start=(k == 0), stop=(k == NH1 - 1)))
                v_mm = TE.chain(fns, deps=[(ve, v_wq), (sy, v_h),
                                           rel_ppA if sxi == 0 else rel_ppB])
                v_mmlast = v_mm
                v_e1 = SC.op(lambda e, t128=t128, pp=pp: e.activation(
                    sb["of"][:, :], pp[:, :], AF.Identity, scale=sb["dqh"][:, t128:t128 + 1]),
                    deps=[(pe, v_mm), (sy, v_dq), rel_of])
                if sxi == 0:
                    rel_ppA = (ac, v_e1)
                else:
                    rel_ppB = (ac, v_e1)
                v_s3 = SY.dma(sb["s3s"][:, :], dr["S3"][cols, dcols], deps=[rel_s3s])
                v_o = VE.op(lambda e: e.tensor_tensor(sb["oo"][:, :], sb["of"][:, :],
                                                      sb["s3s"][:, :], ALU.add),
                            deps=[(ac, v_e1), (sy, v_s3), rel_oo])
                rel_of = (ve, v_o)
                rel_s3s = (ve, v_o)
                v_st = SY.dma(dr["outp"][cols, dcols], sb["oo"][:, :], deps=[(ve, v_o)])
                rel_oo = (sy, v_st)
            rel_hq[slot] = (pe, v_mmlast)
        rel_set = (pe, v_mmlast)
    SY.wait(sy, sy.v)
    GD.wait(gd, gd.v)


def emit_X1(ph, dr, sb):
    GP, VE, SY, GD = ph.GP, ph.VE, ph.SY, ph.GD
    po, ve, sy, gd = (ph.ctrs[n] for n in ("po", "ve", "sy", "gd"))
    v_cc = GP.op(lambda e: e.collective_compute(
        "AllGather", ALU.bypass, replica_groups=[list(range(NCORES))],
        ins=[dr["DpL_h"].ap().opt()], outs=[dr["DpAll_h"].ap().opt()]), inc=1)
    v_ws = SY.dma(sb["wselb"][:, :], dr["wsel"])
    v_kv = VE.op(lambda e: e.memset(sb["kvacc"][:, :], 0.0))
    for r in range(NCORES):
        v_d = SY.dma(sb["dbuf"][:, :], dr["DpAll"][r * 128:(r + 1) * 128, :],
                     deps=[(ve, v_kv)] + ([(po, v_cc)] if r == 0 else []))
        for h in range(H):
            v_kv = VE.op(lambda e, r=r, h=h: e.scalar_tensor_tensor(
                sb["kvacc"][:, h * 128:(h + 1) * 128],
                sb["dbuf"][:, h * 128:(h + 1) * 128],
                sb["wselb"][:, r * 16 + h:r * 16 + h + 1],
                sb["kvacc"][:, h * 128:(h + 1) * 128], ALU.mult, ALU.add),
                deps=[(sy, v_d), (sy, v_ws), (ve, v_kv)])
    SY.dma(dr["KVIN"], sb["kvacc"][:, :], deps=[(ve, v_kv)])
    SY.wait(sy, sy.v)
    GD.wait(gd, gd.v)

def emit_X2(ph, dr, sb):
    GP, VE, SY, GD = ph.GP, ph.VE, ph.SY, ph.GD
    po, ve, sy, gd = (ph.ctrs[n] for n in ("po", "ve", "sy", "gd"))
    v_b = GD.dma(dr["HaloL"], dr["S2"][T - 3:T, :])
    v_cc = GP.op(lambda e: e.collective_compute(
        "AllGather", ALU.bypass, replica_groups=[list(range(NCORES))],
        ins=[dr["HaloL_h"].ap().opt()], outs=[dr["HaloAll_h"].ap().opt()]),
        deps=[(gd, v_b)], inc=1)
    v_mh = SY.dma(sb["mhb"][:, :], dr["mhalo"])
    v_hg = SY.dma(sb["halog"][:, :].rearrange("p (r d) -> p r d", d=Dm),
                  dr["HaloAll"].rearrange("(r p) d -> p r d", p=3),
                  deps=[(po, v_cc)])
    v_a = VE.op(lambda e: e.memset(sb["haloacc"][:, :], 0.0))
    for r in range(NCORES):
        v_a = VE.op(lambda e, r=r: e.scalar_tensor_tensor(
            sb["haloacc"][:, :], sb["halog"][:, r * Dm:(r + 1) * Dm],
            sb["mhb"][0:3, r:r + 1], sb["haloacc"][:, :], ALU.mult, ALU.add),
            deps=[(sy, max(v_hg, v_mh)), (ve, v_a)])
    SY.dma(dr["HaloSel"], sb["haloacc"][:, :], deps=[(ve, v_a)])
    SY.wait(sy, sy.v)
    GD.wait(gd, gd.v)

def build_full(tab):
    nc = bass.Bass("TRN2", target_bir_lowering=False)
    dr = {}
    def din(name, shape, dt):
        dr[name] = nc.dram_tensor(name, shape, dt, kind="ExternalInput").ap()
    def dint(name, shape, dt):
        h = nc.dram_tensor(name, shape, dt, kind="Internal")
        dr[name + "_h"] = h
        dr[name] = h.ap()
    decl_A_inputs(din)
    decl_B_weights(din)
    decl_C_weights(din)
    din("wsel", [128, 128], F32)
    din("mhalo", [128, 8], F32)
    dr["outp"] = nc.dram_tensor("outp", [T, Dm], F32, kind="ExternalOutput").ap()
    dint("qf", [2048, T], BF16)
    dint("oloc", [T, 2048], BF16)
    dint("DpL", [128, 2048], F32)
    dr["Dp"] = dr["DpL"]
    dint("DpAll", [NCORES * 128, 2048], F32)
    dint("KVIN", [128, 2048], F32)
    dint("WSC", [1, 16], F32)
    dint("S1", [T, Dm], F32)
    dint("S2", [T, Dm], F32)
    dint("HaloL", [3, Dm], F32)
    dint("HaloAll", [NCORES * 3, Dm], F32)
    dint("HaloSel", [3, Dm], F32)
    dr["halo"] = dr["HaloSel"]
    decl_C_scratch(dint)

    phases_A(nc, dr, tab)
    with contextlib.ExitStack() as st:
        sb = {}
        sb["wselb"] = st.enter_context(nc.sbuf_tensor("x1_wselb", [128, 128], F32))
        sb["kvacc"] = st.enter_context(nc.sbuf_tensor("x1_kvacc", [128, 2048], F32))
        sb["dbuf"] = st.enter_context(nc.sbuf_tensor("x1_dbuf", [128, 2048], F32))
        ph = Phase(nc, st, "X1")
        emit_X1(ph, dr, sb)
        ph.emit()
    phases_B(nc, dr, tab, fused=True)
    with contextlib.ExitStack() as st:
        sb = {}
        sb["mhb"] = st.enter_context(nc.sbuf_tensor("x2_mhb", [128, 8], F32))
        sb["halog"] = st.enter_context(nc.sbuf_tensor("x2_halog", [3, NCORES * Dm], F32))
        sb["haloacc"] = st.enter_context(nc.sbuf_tensor("x2_haloacc", [3, Dm], F32))
        ph = Phase(nc, st, "X2")
        emit_X2(ph, dr, sb)
        ph.emit()
    phases_C(nc, dr, tab, fused=True)
    return nc

def host_sel_tables(core_id):
    b, seg = divmod(core_id, SEGS)
    lam = _alibi_lam().astype(np.float64)
    bseg = np.exp(-lam * T)
    W = np.zeros((8, H), np.float64)
    for r in range(NCORES):
        rb, rs = divmod(r, SEGS)
        if rb == b and rs < seg:
            W[r] = bseg ** (seg - 1 - rs)
    wsel = np.ascontiguousarray(
        np.broadcast_to(W.reshape(1, 8 * H), (128, 8 * H)).astype(np.float32))
    m = np.zeros(8, np.float32)
    if seg > 0:
        m[core_id - 1] = 1.0
    mhalo = np.ascontiguousarray(np.broadcast_to(m.reshape(1, 8), (128, 8)))
    return wsel, mhalo


def shard_inputs(inputs):
    x = inputs["L0_attn_x"]
    shards = []
    for core in range(NCORES):
        b, seg = divmod(core, SEGS)
        shards.append(np.ascontiguousarray(x[b, seg * T:(seg + 1) * T, :]))
    return shards

def kernel(**inputs):
    w = host_weights(inputs)
    tabs = [host_tables(ci) for ci in range(NCORES)]
    xsh = shard_inputs(inputs)
    nc = build_full(tabs[0])
    ims = []
    for ci in range(NCORES):
        tab = tabs[ci]
        wsel, mhalo = host_sel_tables(ci)
        im = dict(xin=xsh[ci], wsel=wsel, mhalo=mhalo,
                  diagT=tab["diagT"], kdec=tab["kdec"], qdec_bc=tab["qdec_bc"],
                  cosT=tab["cosT"], sinT=tab["sinT"], ident=tab["ident"],
                  rotm=tab["rotm"], qdec_pp=tab["qdec_pp"])
        im.update({k: w[k] for k in ("wqT", "wkT", "wvT", "woT", "w0fT", "b0f",
                                     "winT", "convw", "convb", "woutT", "wm1T",
                                     "wm2T", "w1T", "w3T", "w2T")})
        ims.append(im)
    res = run_bass_kernel_spmd(nc, ims, core_ids=list(range(NCORES)))
    out = np.empty((B, S, Dm), np.float32)
    for ci in range(NCORES):
        b, seg = divmod(ci, SEGS)
        out[b, seg * T:(seg + 1) * T, :] = res.results[ci]["outp"]
    return out

def kernel_unfused(**inputs):
    w = host_weights(inputs)
    tabs = [host_tables(ci) for ci in range(NCORES)]
    xsh = shard_inputs(inputs)
    zeros = np.zeros((128, 2048), np.float32)

    ncA = build_A(tabs[0])
    imsA = []
    for ci in range(NCORES):
        tab = tabs[ci]
        imsA.append(dict(xin=xsh[ci], wqT=w["wqT"], wkT=w["wkT"], wvT=w["wvT"],
                         diagT=tab["diagT"], kdec=tab["kdec"], qdec_bc=tab["qdec_bc"],
                         cosT=tab["cosT"], sinT=tab["sinT"], ident=tab["ident"],
                         rotm=tab["rotm"]))
    resA = run_bass_kernel_spmd(ncA, imsA, core_ids=list(range(NCORES)))

    ncB = build_B(tabs[0])
    imsB = []
    for ci in range(NCORES):
        b, seg = divmod(ci, SEGS)
        a = resA.results[ci]
        preds = [resA.results[b * SEGS + q]["Dp"] for q in range(seg)]
        preds = [zeros] * (3 - len(preds)) + preds
        imsB.append(dict(xin=xsh[ci], qf=a["qf"], oloc=a["oloc"],
                         Dpred0=preds[0], Dpred1=preds[1], Dpred2=preds[2],
                         woT=w["woT"], w0fT=w["w0fT"], b0f=w["b0f"],
                         qdec_pp=tabs[ci]["qdec_pp"], ident=tabs[ci]["ident"]))
    resB = run_bass_kernel_spmd(ncB, imsB, core_ids=list(range(NCORES)))

    ncC = build_C(tabs[0])
    imsC = []
    for ci in range(NCORES):
        b, seg = divmod(ci, SEGS)
        if seg == 0:
            halo = np.zeros((3, Dm), np.float32)
        else:
            halo = np.ascontiguousarray(resB.results[ci - 1]["S2"][T - 3:T])
        imsC.append(dict(S2=resB.results[ci]["S2"], halo=halo,
                         winT=w["winT"], convw=w["convw"], convb=w["convb"],
                         woutT=w["woutT"], wm1T=w["wm1T"], wm2T=w["wm2T"],
                         w1T=w["w1T"], w3T=w["w3T"], w2T=w["w2T"],
                         ident=tabs[ci]["ident"]))
    resC = run_bass_kernel_spmd(ncC, imsC, core_ids=list(range(NCORES)))

    out = np.empty((B, S, Dm), np.float32)
    for ci in range(NCORES):
        b, seg = divmod(ci, SEGS)
        out[b, seg * T:(seg + 1) * T, :] = resC.results[ci]["outp"]
    return out


# version 05
# speedup vs baseline: 1.1521x; 1.0222x over previous
import contextlib
import numpy as np
import ml_dtypes
import concourse.bass as bass
import concourse.mybir as mybir
from concourse.bass_utils import run_bass_kernel_spmd

F32 = mybir.dt.float32
BF16 = mybir.dt.bfloat16
AF = mybir.ActivationFunctionType
ALU = mybir.AluOpType
AX = mybir.AxisListType
BF = ml_dtypes.bfloat16

MAGIC = float(np.float32(12582912.0))

B, S, Dm = 2, 8192, 2048
NCORES = 8
T = 2048
SEGS = 4
H, HKV, DH, CHUNK = 16, 4, 128, 64
NCH = T // CHUNK
NT = T // 128
ND = Dm // 128
GT = 4
NG = NT // GT


def _alibi_lam():
    i = np.arange(1, H + 1, dtype=np.float32)
    return np.exp2((-8.0 * i / H).astype(np.float32))

def host_tables(core_id):
    seg = core_id % SEGS
    lam = _alibi_lam()
    r = np.arange(CHUNK, dtype=np.float32)
    diff = r[:, None] - r[None, :]
    with np.errstate(under="ignore"):
        diag = np.where(diff >= 0, np.exp(-lam[:, None, None] * diff), 0.0).astype(np.float32)
        diagT = np.ascontiguousarray(diag.transpose(2, 0, 1).reshape(CHUNK, H * CHUNK))
        diagT = np.ascontiguousarray(np.concatenate([diagT, diagT], axis=0))
        kdec = np.exp(-lam[None, :] * (CHUNK - 1.0 - r[:, None])).astype(np.float32)
        kdec = np.concatenate([kdec, kdec], axis=0)
        qdec = np.exp(-lam[:, None] * (r[None, :] + 1.0)).astype(np.float32)
        qdec_bc = np.ascontiguousarray(
            np.broadcast_to(qdec.reshape(1, H * CHUNK), (128, H * CHUNK)).astype(BF))
        tloc = np.arange(T, dtype=np.float32)
        qdec_pp = np.exp(-lam[None, :] * (tloc[:, None] + 1.0)).astype(np.float32)
        blk = np.exp(-lam * CHUNK).astype(np.float32)
        blk_seg = np.exp(-lam.astype(np.float64) * T).astype(np.float32)
    d = DH
    inv_freq = (1.0 / 10000.0 ** (np.arange(0, d, 2, dtype=np.float32) / d)).astype(np.float32)
    tg = (np.arange(T, dtype=np.float32) + np.float32(seg * T)).astype(np.float32)
    freqs = (tg[:, None] * inv_freq[None, :]).astype(np.float32)
    cos = np.concatenate([np.cos(freqs), np.cos(freqs)], axis=1)
    sin = np.concatenate([np.sin(freqs), np.sin(freqs)], axis=1)
    return dict(
        diagT=diagT, kdec=kdec, qdec_bc=qdec_bc, qdec_pp=qdec_pp,
        blk=[float(x) for x in blk], blk_seg=[float(x) for x in blk_seg],
        cosT=np.ascontiguousarray(cos.T.astype(BF)),
        sinT=np.ascontiguousarray(sin.T.astype(BF)),
        ident=np.eye(128, dtype=BF),
        rotm=_rot_matrix())

def _rot_matrix():
    R = np.zeros((128, 128), dtype=BF)
    for m in range(64):
        R[m + 64, m] = -1.0
    for m in range(64, 128):
        R[m - 64, m] = 1.0
    return R

def host_weights(inp):
    c = np.ascontiguousarray
    return dict(
        wqT=c(inp["L0_attn_wq"].T), wkT=c(inp["L0_attn_wk"].T),
        wvT=c(inp["L0_attn_wv"].T), woT=c(inp["L0_attn_wo"].T),
        w0fT=c(inp["L0_ffn_weight"].T), b0f=c(inp["L0_ffn_bias"].reshape(1, Dm)),
        winT=c(inp["L1_attn_W_in"].T), convw=c(inp["L1_attn_conv_w"].reshape(4096, 4)),
        convb=c(inp["L1_attn_conv_b"].reshape(4096, 1)),
        woutT=c(inp["L1_attn_W_out"].T), wm1T=c(inp["L1_attn_W_mlp1"].T),
        wm2T=c(inp["L1_attn_W_mlp2"].T), w1T=c(inp["L1_ffn_w1"].T),
        w3T=c(inp["L1_ffn_w3"].T), w2T=c(inp["L1_ffn_w2"].T))


class Ctr:
    def __init__(self, h):
        self.h = h
        self.v = 0

class Prog:
    ENGS = ("sync", "scalar", "vector", "tensor", "gpsimd")
    def __init__(self, nc):
        self.nc = nc
        self.streams = {e: [] for e in self.ENGS}
    def emit(self):
        with self.nc.Block() as blk:
            for e in self.ENGS:
                ops = self.streams[e]
                if not ops:
                    continue
                getattr(blk, e)(lambda eng, ops=ops: [f(eng) for f in ops])

class Eng:
    def __init__(self, p, name, ctr):
        self.p, self.name, self.c = p, name, ctr
    def wait(self, ctr, v):
        if v > 0:
            self.p.streams[self.name].append(
                lambda e, h=ctr.h, v=v: e.wait_ge(h, v))
    def waits(self, deps):
        for (c, v) in deps:
            self.wait(c, v)
    def op(self, fn, deps=(), inc=1):
        self.waits(deps)
        c = self.c
        self.p.streams[self.name].append(
            lambda e, fn=fn, h=c.h, n=inc: fn(e).then_inc(h, n))
        c.v += inc
        return c.v
    def chain(self, fns, deps=(), inc=1):
        self.waits(deps)
        c = self.c
        def f(e, fns=fns, h=c.h, n=inc):
            last = None
            for fn in fns:
                last = fn(e)
            last.then_inc(h, n)
        self.p.streams[self.name].append(f)
        c.v += inc
        return c.v
    def dma(self, out, in_, deps=()):
        return self.op(lambda e, o=out, i=in_: e.dma_start(o, i), deps=deps, inc=16)

class Phase:
    def __init__(self, nc, st, tag):
        self.nc = nc
        self.tag = tag
        p = Prog(nc)
        self.p = p
        self.ctrs = {}
        for nm in ("pe", "ac", "ve", "po", "sy", "gd"):
            self.ctrs[nm] = Ctr(st.enter_context(nc.semaphore(f"{tag}_{nm}")))
        self.TE = Eng(p, "tensor", self.ctrs["pe"])
        self.SC = Eng(p, "scalar", self.ctrs["ac"])
        self.VE = Eng(p, "vector", self.ctrs["ve"])
        self.GP = Eng(p, "gpsimd", self.ctrs["po"])
        self.SY = Eng(p, "sync", self.ctrs["sy"])
        self.GD = Eng(p, "gpsimd", self.ctrs["gd"])
    def emit(self):
        with self.nc.named_scope(self.tag):
            self.p.emit()

def emit_ln(ph, xb, xscr, stats, eps, n=Dm, deps=(), rms_sqrt_sum=False):
    SC, VE = ph.SC, ph.VE
    ac, ve = ph.ctrs["ac"], ph.ctrs["ve"]
    SC.waits(deps)
    SC.op(lambda e: e.activation(xscr[:, 0:n], xb[:, 0:n], AF.Identity,
                                 accum_out=stats[:, 0:1]))
    va = SC.op(lambda e: e.activation(xscr[:, 0:n], xb[:, 0:n], AF.Square,
                                      accum_out=stats[:, 1:2]))
    inv_n = float(np.float32(1.0 / n))
    VE.op(lambda e: e.tensor_scalar(stats[:, 2:3], stats[:, 0:1], inv_n, None, ALU.mult),
          deps=[(ac, va)])
    v1 = VE.op(lambda e: e.tensor_scalar(stats[:, 3:4], stats[:, 1:2], inv_n, None, ALU.mult))
    v2 = VE.op(lambda e: e.scalar_tensor_tensor(stats[:, 4:5], stats[:, 2:3], stats[:, 2:3],
                                                stats[:, 3:4], ALU.mult, ALU.subtract),
               deps=[(ve, v1)])
    v3 = VE.op(lambda e: e.tensor_scalar(stats[:, 5:6], stats[:, 4:5], -1.0, float(eps),
                                         ALU.mult, ALU.add), deps=[(ve, v2)])
    v4 = SC.op(lambda e: e.activation(stats[:, 6:7], stats[:, 5:6], AF.Sqrt),
               deps=[(ve, v3)])
    v5 = VE.op(lambda e: e.reciprocal(stats[:, 6:7], stats[:, 6:7]), deps=[(ac, v4)])
    v6 = VE.op(lambda e: e.scalar_tensor_tensor(stats[:, 7:8], stats[:, 2:3], -1.0,
                                                stats[:, 6:7], ALU.mult, ALU.mult),
               deps=[(ve, v5)])
    return v6

def emit_tr(ph, sb, srcfn, n, dstfn, deps, rel_pt, identname="c_ident"):
    TE, VE = ph.TE, ph.VE
    pe, ve = ph.ctrs["pe"], ph.ctrs["ve"]
    v_ev = 0
    for g0 in range(0, n, 4):
        bi = (g0 // 4) % 2
        bank = sb["ptA" if bi == 0 else "ptB"]
        m = min(4, n - g0)
        fns = []
        for j in range(m):
            fns.append(lambda e, j=j, k=g0 + j, bank=bank: e.transpose(
                bank[:, j * 128:(j + 1) * 128], srcfn(k), sb[identname][:, :]))
        v_tr = TE.chain(fns, deps=list(deps) + [rel_pt[bi]])
        fns = []
        for j in range(m):
            fns.append(lambda e, j=j, k=g0 + j, bank=bank: e.tensor_copy(
                dstfn(k), bank[:, j * 128:(j + 1) * 128]))
        v_ev = VE.chain(fns, deps=[(pe, v_tr)])
        rel_pt[bi] = (ve, v_ev)
    return v_ev


def decl_A_inputs(din):
    din("xin", [T, Dm], F32)
    din("wqT", [Dm, 2048], F32); din("wkT", [Dm, 512], F32); din("wvT", [Dm, 512], F32)
    din("diagT", [128, 1024], F32); din("kdec", [128, 16], F32)
    din("rotm", [128, 128], BF16)
    din("qdec_bc", [128, 1024], BF16)
    din("cosT", [128, T], BF16); din("sinT", [128, T], BF16)
    din("ident", [128, 128], BF16)

def build_A(tab):
    nc = bass.Bass("TRN2", target_bir_lowering=False)
    dr = {}
    def din(name, shape, dt):
        dr[name] = nc.dram_tensor(name, shape, dt, kind="ExternalInput").ap()
    def dout(name, shape, dt):
        dr[name] = nc.dram_tensor(name, shape, dt, kind="ExternalOutput").ap()
    decl_A_inputs(din)
    dout("qf", [2048, T], BF16)
    dout("oloc", [T, 2048], BF16)
    dout("Dp", [128, 2048], F32)
    phases_A(nc, dr, tab)
    return nc

def phases_A(nc, dr, tab):
    with contextlib.ExitStack() as st:
        sb = {}
        def alloc(name, shape, dt, tag="a_"):
            sb[name] = st.enter_context(nc.sbuf_tensor(f"{tag}{name}", shape, dt))
        def palloc(name, shape, dt, tag="a_"):
            sb[name] = st.enter_context(nc.psum_tensor(f"{tag}{name}", shape, dt))
        alloc("xbuf0", [128, Dm], F32); alloc("xbuf1", [128, Dm], F32)
        alloc("xscr", [128, Dm], F32)
        alloc("stats", [128, 8], F32)
        alloc("xh", [128, Dm], BF16)
        for k in range(ND):
            alloc(f"xhT{k}", [128, 512], BF16)
        alloc("wbuf0", [128, 8192], BF16); alloc("wbuf1", [128, 8192], BF16)
        for h in range(H):
            alloc(f"qfg{h}", [128, 512], BF16)
        for k in range(HKV):
            alloc(f"kfg{k}", [128, 512], BF16)
        for ts_ in range(GT):
            alloc(f"ktg{ts_}", [128, 512], BF16)
            alloc(f"vtg{ts_}", [128, 512], BF16)
            alloc(f"olg{ts_}", [128, 2048], BF16)
        alloc("kv", [128, 2048], F32)
        alloc("kvbf", [128, 2048], BF16)
        alloc("ropeA", [128, 512], BF16); alloc("ropeB", [128, 512], BF16)
        alloc("ropeC", [128, 512], BF16)
        for h4 in range(4):
            alloc(f"qd{h4}", [128, 64], BF16)
            alloc(f"kd{h4}", [128, 128], BF16)
        alloc("ssT", [128, 256], BF16)
        alloc("cos", [128, T], BF16); alloc("sin", [128, T], BF16)
        alloc("c_diagT", [128, 1024], F32); alloc("c_kdec", [128, 16], F32)
        alloc("c_rotm", [128, 128], BF16)
        alloc("qdecbc", [128, 1024], BF16)
        alloc("c_ident", [128, 128], BF16)
        palloc("pq0", [128, 512], F32); palloc("pq1", [128, 512], F32)
        palloc("ptA", [128, 512], BF16); palloc("ptB", [128, 512], BF16)
        palloc("psS", [128, 256], F32)
        palloc("psO", [64, 512], F32)
        palloc("psA", [128, 512], F32)
        palloc("pr", [128, 512], F32)

        ph = Phase(nc, st, "A")
        emit_A(ph, dr, sb, tab)
        ph.emit()

def emit_A(ph, dr, sb, tab):
    TE, SC, VE, SY, GD = ph.TE, ph.SC, ph.VE, ph.SY, ph.GD
    pe, ac, ve, sy, gd = (ph.ctrs[n] for n in ("pe", "ac", "ve", "sy", "gd"))
    blk = tab["blk"]

    v_tabs = 0
    for nm, src in (("cos", "cosT"), ("sin", "sinT"), ("c_diagT", "diagT"),
                    ("c_kdec", "kdec"), ("qdecbc", "qdec_bc"), ("c_ident", "ident"),
                    ("c_rotm", "rotm")):
        v_tabs = SY.dma(sb[nm][:, :], dr[src])
    v_kv0 = VE.op(lambda e: e.memset(sb["kv"][:, :], 0.0))
    v_kvbf0 = SC.op(lambda e: e.activation(sb["kvbf"][:, :], sb["kv"][:, :], AF.Identity),
                    deps=[(ve, v_kv0)])

    wsched = [("q", 0), ("q", 4), ("q", 8), ("q", 12), ("k", 0), ("v", 0)]
    def wsrc(kind, e0):
        if kind == "q":
            s = dr["wqT"][:, e0 * 128:(e0 + 4) * 128]
        elif kind == "k":
            s = dr["wkT"]
        else:
            s = dr["wvT"]
        return s.rearrange("(c p) j -> p c j", p=128)

    rel_xbuf = [(ac, 0), (ac, 0)]
    rel_xh = (pe, 0)
    rel_xhT = (pe, 0)
    rel_wbuf = [(pe, 0), (pe, 0)]
    rel_pq = [(ac, 0), (ac, 0)]
    rel_pt = [(ve, 0), (ve, 0)]
    rel_qkv_te = (pe, 0)
    rel_pr = (ve, 0)
    rel_olg = (sy, 0)
    v_kvbf = (ac, v_kvbf0)
    v_ssT_free = (pe, 0)
    v_psS_free = (ve, 0)
    v_psO_free = (ac, 0)
    v_psA_free = (ve, 0)

    def wdst(slot):
        return sb[f"wbuf{slot}"][:, :].rearrange("p (c j) -> p c j", j=512)
    wload = {}
    wload[0] = GD.dma(wdst(0), wsrc(*wsched[0]), deps=())

    for g in range(NG):
        v_evac = {}
        for i in range(GT):
            t128 = g * GT + i
            xb = sb[f"xbuf{t128 % 2}"]
            v_x = SY.dma(xb[:, :], dr["xin"][t128 * 128:(t128 + 1) * 128, :],
                         deps=[rel_xbuf[t128 % 2]])
            v_ln = emit_ln(ph, xb, sb["xscr"], sb["stats"], 1e-5,
                           deps=[(sy, v_x)])
            v_norm = SC.op(lambda e, xb=xb: e.activation(
                sb["xh"][:, :], xb[:, :], AF.Identity,
                scale=sb["stats"][:, 6:7], bias=sb["stats"][:, 7:8]),
                deps=[(ve, v_ln), rel_xh])
            rel_xbuf[t128 % 2] = (ac, v_norm)
            emit_tr(ph, sb, lambda k: sb["xh"][:, k * 128:(k + 1) * 128], ND,
                    lambda k, i=i: sb[f"xhT{k}"][:, i * 128:(i + 1) * 128],
                    [(ac, v_norm), (sy, v_tabs), rel_xhT], rel_pt)
            rel_xh = (pe, pe.v)
        v_xhT = ve.v

        cosg = sb["cos"][:, g * 512:(g + 1) * 512]
        sing = sb["sin"][:, g * 512:(g + 1) * 512]
        v_vtg = {}
        for bi, (kind, e0) in enumerate(wsched):
            slot = bi % 2
            gi = g * len(wsched) + bi
            v_w = wload[gi]
            if gi + 1 < NG * len(wsched):
                nkind, ne0 = wsched[(gi + 1) % len(wsched)]
                wload[gi + 1] = GD.dma(wdst(1 - slot), wsrc(nkind, ne0),
                                       deps=[rel_wbuf[1 - slot]])
            if kind in ("q", "k"):
                for ec in range(4):
                    eg = (e0 + ec) if kind == "q" else ec
                    pqi = (bi * 4 + ec) % 2
                    pq = sb[f"pq{pqi}"]
                    fns = []
                    for k in range(ND):
                        lhs = sb[f"wbuf{slot}"][:, k * 512 + ec * 128: k * 512 + (ec + 1) * 128]
                        fns.append(lambda e, lhs=lhs, k=k, pq=pq, s=(k == 0), t=(k == ND - 1):
                                   e.matmul(pq[:, :], lhs, sb[f"xhT{k}"][:, :], start=s, stop=t))
                    v_mm = TE.chain(fns, deps=[(gd, v_w), (ve, v_xhT), rel_pq[pqi]])
                    dst = sb[f"qfg{eg}"] if kind == "q" else sb[f"kfg{eg}"]
                    v_eA = SC.op(lambda e, pq=pq: e.activation(sb["ropeA"][:, :], pq[:, :],
                                                               AF.Identity),
                                 deps=[(pe, v_mm), (ve, ve.v)])
                    rel_pq[pqi] = (ac, v_eA)
                    v_rm = TE.op(lambda e: e.matmul(sb["pr"][:, :], sb["c_rotm"][:, :],
                                                    sb["ropeA"][:, :], start=True, stop=True),
                                 deps=[(ac, v_eA), rel_pr])
                    v_r3 = VE.op(lambda e, cosg=cosg: e.tensor_tensor(
                        sb["ropeC"][:, :], sb["ropeA"][:, :], cosg, ALU.mult),
                        deps=[(ac, v_eA)])
                    v_r4 = VE.op(lambda e, sing=sing: e.tensor_tensor(
                        sb["ropeB"][:, :], sb["pr"][:, :], sing, ALU.mult),
                        deps=[(pe, v_rm)])
                    rel_pr = (ve, v_r4)
                    v_r5 = VE.op(lambda e, dst=dst: e.tensor_tensor(
                        dst[:, :], sb["ropeC"][:, :], sb["ropeB"][:, :], ALU.add),
                        deps=[(ve, v_r4), rel_qkv_te])
                    if kind == "q":
                        SY.dma(dr["qf"][eg * 128:(eg + 1) * 128, g * 512:(g + 1) * 512],
                               dst[:, :], deps=[(ve, v_r5)])
                    else:
                        emit_tr(ph, sb, lambda k, dst=dst: dst[:, k * 128:(k + 1) * 128], GT,
                                lambda k, eg=eg: sb[f"ktg{k}"][:, eg * 128:(eg + 1) * 128],
                                [(ve, v_r5), rel_qkv_te], rel_pt)
            else:
                for ts_ in range(GT):
                    pqi = ts_ % 2
                    pq = sb[f"pq{pqi}"]
                    fns = []
                    for k in range(ND):
                        lhs = sb[f"xhT{k}"][:, ts_ * 128:(ts_ + 1) * 128]
                        rhs = sb[f"wbuf{slot}"][:, k * 512:(k + 1) * 512]
                        fns.append(lambda e, lhs=lhs, rhs=rhs, pq=pq, s=(k == 0), t=(k == ND - 1):
                                   e.matmul(pq[:, :], lhs, rhs, start=s, stop=t))
                    v_mm = TE.chain(fns, deps=[(gd, v_w), (ve, v_xhT), rel_pq[pqi]])
                    v_e = SC.op(lambda e, pq=pq, ts_=ts_: e.activation(
                        sb[f"vtg{ts_}"][:, :], pq[:, :], AF.Identity),
                        deps=[(pe, v_mm), rel_qkv_te])
                    rel_pq[pqi] = (ac, v_e)
                    v_vtg[ts_] = v_e
            rel_wbuf[slot] = (pe, pe.v)
        v_qkv = (ve.v, ac.v)

        for c in range(2 * GT):
            ts_ = c // 2
            half = c % 2
            rows0 = half * 64
            tcol = c * 64
            for g4 in range(HKV):
                hs = [g4 * 4 + j for j in range(4)]
                v_kd = {}
                for j, h in enumerate(hs):
                    v_kd[j] = VE.op(lambda e, j=j, h=h, ts_=ts_, rows0=rows0, g4=g4:
                        e.tensor_scalar(sb[f"kd{j}"][rows0:rows0 + 64, :],
                                        sb[f"ktg{ts_}"][rows0:rows0 + 64, g4 * 128:(g4 + 1) * 128],
                                        sb["c_kdec"][rows0:rows0 + 64, h:h + 1], None, ALU.mult),
                        deps=[(ve, v_qkv[0]), (ac, v_qkv[1])])
                v_qd = {}
                for j, h in enumerate(hs):
                    v_qd[j] = VE.op(lambda e, j=j, h=h, tcol=tcol: e.tensor_tensor(
                        sb[f"qd{j}"][:, :], sb[f"qfg{h}"][:, tcol:tcol + 64],
                        sb["qdecbc"][:, h * 64:(h + 1) * 64], ALU.mult))
                fns = []
                for j, h in enumerate(hs):
                    fns.append(lambda e, j=j, h=h, g4=g4, tcol=tcol, rows0=rows0: e.matmul(
                        sb["psS"][rows0:rows0 + 64, j * 64:(j + 1) * 64],
                        sb[f"kfg{g4}"][:, tcol:tcol + 64],
                        sb[f"qfg{h}"][:, tcol:tcol + 64], start=True, stop=True))
                v_s = TE.chain(fns, deps=[v_psS_free, (ve, v_qkv[0]), (ac, v_qkv[1]),
                                          v_ssT_free])
                v_diag = VE.op(lambda e, g4=g4, rows0=rows0: e.tensor_tensor(
                    sb["ssT"][rows0:rows0 + 64, :], sb["psS"][rows0:rows0 + 64, :],
                    sb["c_diagT"][rows0:rows0 + 64, g4 * 256:(g4 + 1) * 256], ALU.mult),
                    deps=[(pe, v_s)])
                v_psS_free = (ve, v_diag)
                fns = []
                for j, h in enumerate(hs):
                    fns.append(lambda e, j=j, ts_=ts_, rows0=rows0, g4=g4: e.matmul(
                        sb["psO"][:, j * 128:(j + 1) * 128],
                        sb["ssT"][rows0:rows0 + 64, j * 64:(j + 1) * 64],
                        sb[f"vtg{ts_}"][rows0:rows0 + 64, g4 * 128:(g4 + 1) * 128],
                        start=True, stop=False))
                    fns.append(lambda e, j=j, h=h: e.matmul(
                        sb["psO"][:, j * 128:(j + 1) * 128],
                        sb[f"qd{j}"][:, :],
                        sb["kvbf"][:, h * 128:(h + 1) * 128],
                        start=False, stop=True))
                v_o = TE.chain(fns, deps=[(ve, v_diag), (ve, max(v_qd.values())),
                                          v_kvbf, v_psO_free])
                v_ssT_free = (pe, v_o)
                fns = []
                for j, h in enumerate(hs):
                    fns.append(lambda e, j=j, ts_=ts_, rows0=rows0, g4=g4: e.matmul(
                        sb["psA"][:, j * 128:(j + 1) * 128],
                        sb[f"kd{j}"][rows0:rows0 + 64, :],
                        sb[f"vtg{ts_}"][rows0:rows0 + 64, g4 * 128:(g4 + 1) * 128],
                        start=True, stop=True))
                v_A = TE.chain(fns, deps=[(ve, max(v_kd.values())), v_psA_free])
                v_oe = SC.op(lambda e, ts_=ts_, rows0=rows0, g4=g4: e.activation(
                    sb[f"olg{ts_}"][rows0:rows0 + 64, g4 * 512:(g4 + 1) * 512],
                    sb["psO"][0:64, :], AF.Identity),
                    deps=[(pe, v_o), rel_olg])
                v_psO_free = (ac, v_oe)
                v_u = 0
                for j, h in enumerate(hs):
                    v_u = VE.op(lambda e, j=j, h=h: e.scalar_tensor_tensor(
                        sb["kv"][:, h * 128:(h + 1) * 128],
                        sb["kv"][:, h * 128:(h + 1) * 128], blk[h],
                        sb["psA"][:, j * 128:(j + 1) * 128], ALU.mult, ALU.add),
                        deps=[(pe, v_A), v_kvbf])
                v_psA_free = (ve, v_u)
                v_kb = SC.op(lambda e, g4=g4: e.activation(
                    sb["kvbf"][:, g4 * 512:(g4 + 1) * 512],
                    sb["kv"][:, g4 * 512:(g4 + 1) * 512], AF.Identity),
                    deps=[(ve, v_u)])
                v_kvbf = (ac, v_kb)
        rel_qkv_te = (pe, pe.v)
        v_st = 0
        for ts_ in range(GT):
            v_st = SY.dma(dr["oloc"][(g * GT + ts_) * 128:(g * GT + ts_ + 1) * 128, :],
                          sb[f"olg{ts_}"][:, :], deps=[(ac, ac.v)])
        rel_olg = (sy, v_st)
        rel_xhT = (pe, pe.v)

    SY.dma(dr["Dp"], sb["kv"][:, :], deps=[(ve, ve.v), (ac, ac.v)])
    ph.SY.wait(ph.ctrs["sy"], ph.ctrs["sy"].v)


def decl_B_weights(din):
    din("woT", [Dm, 2048], F32)
    din("w0fT", [Dm, 2048], F32)
    din("b0f", [1, Dm], F32)
    din("qdec_pp", [T, 16], F32)

def build_B(tab):
    nc = bass.Bass("TRN2", target_bir_lowering=False)
    dr = {}
    def din(name, shape, dt):
        dr[name] = nc.dram_tensor(name, shape, dt, kind="ExternalInput").ap()
    def dout(name, shape, dt):
        dr[name] = nc.dram_tensor(name, shape, dt, kind="ExternalOutput").ap()
    din("xin", [T, Dm], F32)
    din("qf", [2048, T], BF16)
    din("oloc", [T, 2048], BF16)
    for i in range(3):
        din(f"Dpred{i}", [128, 2048], F32)
    decl_B_weights(din)
    din("ident", [128, 128], BF16)
    dout("S2", [T, Dm], F32)
    S1 = nc.dram_tensor("S1", [T, Dm], F32, kind="Internal").ap()
    dr["S1"] = S1
    phases_B(nc, dr, tab, fused=False)
    return nc

def phases_B(nc, dr, tab, fused):
    with contextlib.ExitStack() as st:
        sb = {}
        def alloc(name, shape, dt, tag="b2_"):
            sb[name] = st.enter_context(nc.sbuf_tensor(f"{tag}{name}", shape, dt))
        def palloc(name, shape, dt, tag="b2_"):
            sb[name] = st.enter_context(nc.psum_tensor(f"{tag}{name}", shape, dt))
        alloc("wo", [128, 16 * 2048], BF16)
        alloc("kvin", [128, 2048], F32)
        alloc("kvbf", [128, 2048], BF16)
        alloc("dpr", [128, 2048], F32)
        alloc("qft", [128, 16 * 128], BF16)
        alloc("olt", [128, 2048], BF16)
        alloc("o_as", [128, 2048], BF16)
        for k in range(ND):
            alloc(f"oT{k}", [128, 128], BF16)
        alloc("qpp", [128, 16], F32)
        alloc("xint", [128, Dm], F32)
        alloc("s1t", [128, Dm], F32)
        alloc("c_ident", [128, 128], BF16)
        if fused:
            alloc("wsc_st", [128, 16 * 512], F32)
            alloc("wabsb", [128, 16 * 512], BF16)
            alloc("onesb2", [128, 1], BF16)
            alloc("wtot2", [1, 16], F32)
        palloc("psC0", [128, 512], F32); palloc("psC1", [128, 512], F32)
        palloc("psW0", [128, 512], F32); palloc("psW1", [128, 512], F32)
        palloc("ptA", [128, 512], BF16); palloc("ptB", [128, 512], BF16)
        if fused:
            palloc("psW2", [1, 512], F32)
        ph = Phase(nc, st, "B2")
        emit_B2(ph, dr, sb, tab, fused=fused)
        ph.emit()

    with contextlib.ExitStack() as st:
        sb = {}
        def alloc(name, shape, dt, tag="b4_"):
            sb[name] = st.enter_context(nc.sbuf_tensor(f"{tag}{name}", shape, dt))
        def palloc(name, shape, dt, tag="b4_"):
            sb[name] = st.enter_context(nc.psum_tensor(f"{tag}{name}", shape, dt))
        for k in range(ND):
            alloc(f"xqT{k}", [128, T], BF16)
        alloc("wst", [128, 16 * 512], F32)
        for k in range(ND):
            alloc(f"wtn{k}", [128, 512], BF16)
        alloc("s1t", [128, Dm], F32)
        alloc("xn1", [128, Dm], F32)
        alloc("xn2", [128, Dm], F32)
        alloc("xqb", [128, Dm], BF16)
        alloc("stats", [128, 8], F32)
        alloc("sxall", [128, NT], F32)
        alloc("wab", [128, 512], F32)
        alloc("grow", [1, 512], F32)
        alloc("ginvb", [128, 512], F32)
        alloc("gmb", [128, 512], F32)
        alloc("bb", [128, 512], F32)
        alloc("brow", [1, 512], F32)
        alloc("onesc", [128, 1], F32)
        alloc("onesr", [1, 128], F32)
        alloc("ef", [128, 512], F32)
        alloc("e2", [128, 512], F32)
        alloc("s2t", [128, 512], F32)
        alloc("sl1", [128, 512], F32)
        alloc("c_ident2", [128, 128], BF16)
        palloc("psF0", [128, 512], F32); palloc("psF1", [128, 512], F32)
        palloc("psG", [1, 512], F32)
        palloc("psB", [128, 512], F32)
        palloc("ptA", [128, 512], BF16); palloc("ptB", [128, 512], BF16)
        ph = Phase(nc, st, "B4")
        emit_B4(ph, dr, sb, tab)
        ph.emit()

def emit_wscale(ph, dr, sb):
    TE, SC, VE, SY, GD = ph.TE, ph.SC, ph.VE, ph.SY, ph.GD
    pe, ac, ve, sy, gd = (ph.ctrs[n] for n in ("pe", "ac", "ve", "sy", "gd"))
    v_ones = VE.op(lambda e: e.memset(sb["onesb2"][:, :], 1.0))
    rel_st = (ac, 0)
    rel_ab = (pe, 0)
    rel_psW = (ve, 0)
    wmeta = [("w1T", ND, DH1, 0), ("w3T", ND, DH1, 2), ("w2T", NH1, Dm, 4)]
    for (wn, NK, NCOL, coli) in wmeta:
        rgs = [(r0, min(16, NK - r0)) for r0 in range(0, NK, 16)]
        blocks = [(r0, nt, nb) for (r0, nt) in rgs for nb in range(NCOL // 512)]
        v_g = 0
        for bi, (r0, ntl, nb) in enumerate(blocks):
            v_w = GD.dma(sb["wsc_st"][:, 0:ntl * 512].rearrange("p (c j) -> p c j", j=512),
                         dr[wn][r0 * 128:(r0 + ntl) * 128,
                                nb * 512:(nb + 1) * 512].rearrange("(c p) j -> p c j", p=128),
                         deps=[rel_st])
            v_ab = SC.op(lambda e, ntl=ntl: e.activation(
                sb["wabsb"][:, 0:ntl * 512], sb["wsc_st"][:, 0:ntl * 512], AF.Abs),
                deps=[(gd, v_w), rel_ab])
            rel_st = (ac, v_ab)
            fns = []
            for k in range(ntl):
                first = (bi == 0 and k == 0)
                last = (bi == len(blocks) - 1 and k == ntl - 1)
                fns.append(lambda e, k=k, first=first, last=last: e.matmul(
                    sb["psW2"][:, :], sb["onesb2"][:, :],
                    sb["wabsb"][:, k * 512:(k + 1) * 512], start=first, stop=last))
            v_g = TE.chain(fns, deps=[(ac, v_ab), (ve, v_ones)] +
                           ([rel_psW] if bi == 0 else []))
            rel_ab = (pe, v_g)
        v_rs = VE.op(lambda e, coli=coli: e.reduce_sum(
            sb["wtot2"][:, coli:coli + 1], sb["psW2"][:, :], axis=AX.X),
            deps=[(pe, v_g)])
        rel_psW = (ve, v_rs)
        v_m = VE.op(lambda e, coli=coli: e.tensor_scalar(
            sb["wtot2"][:, coli + 1:coli + 2], sb["wtot2"][:, coli:coli + 1],
            float(np.float32(1.0 / (DH1 * Dm))), 1e-5, ALU.mult, ALU.max),
            deps=[(ve, v_rs)])
        VE.op(lambda e, coli=coli: e.reciprocal(
            sb["wtot2"][:, coli:coli + 1], sb["wtot2"][:, coli + 1:coli + 2]),
            deps=[(ve, v_m)])
    SY.dma(dr["WSC"], sb["wtot2"][:, :], deps=[(ve, ve.v)])

def emit_B2(ph, dr, sb, tab, fused=False):
    TE, SC, VE, SY, GD = ph.TE, ph.SC, ph.VE, ph.SY, ph.GD
    pe, ac, ve, sy, gd = (ph.ctrs[n] for n in ("pe", "ac", "ve", "sy", "gd"))
    bs = tab["blk_seg"]

    v_id = SY.dma(sb["c_ident"][:, :], dr["ident"])
    v_wo = GD.dma(sb["wo"][:, :].rearrange("p (c j) -> p c j", j=2048),
                  dr["woT"].rearrange("(c p) j -> p c j", p=128))
    if fused:
        emit_wscale(ph, dr, sb)
    if fused:
        v_d = SY.dma(sb["kvin"][:, :], dr["KVIN"])
        v_kvbf = SC.op(lambda e: e.activation(sb["kvbf"][:, :], sb["kvin"][:, :],
                                              AF.Identity), deps=[(sy, v_d)])
    else:
        v_kv = VE.op(lambda e: e.memset(sb["kvin"][:, :], 0.0))
        for i in range(3):
            v_d = SY.dma(sb["dpr"][:, :], dr[f"Dpred{i}"], deps=[(ve, v_kv)])
            for h in range(H):
                v_kv = VE.op(lambda e, h=h: e.scalar_tensor_tensor(
                    sb["kvin"][:, h * 128:(h + 1) * 128],
                    sb["kvin"][:, h * 128:(h + 1) * 128], bs[h],
                    sb["dpr"][:, h * 128:(h + 1) * 128], ALU.mult, ALU.add),
                    deps=[(sy, v_d), (ve, v_kv)])
        v_kvbf = SC.op(lambda e: e.activation(sb["kvbf"][:, :], sb["kvin"][:, :],
                                              AF.Identity), deps=[(ve, v_kv)])

    rel_qft = (pe, 0); rel_olt = (ve, 0); rel_oas = (pe, 0)
    rel_psC = [(ve, 0), (ve, 0)]
    rel_psW = [(ve, 0), (ve, 0)]
    rel_pt = [(ve, 0), (ve, 0)]
    rel_oT = (pe, 0)
    rel_s1t = (sy, 0)
    rel_xint = (ve, 0)
    for t128 in range(NT):
        cols = slice(t128 * 128, (t128 + 1) * 128)
        v_q = SY.dma(sb["qft"][:, :].rearrange("p (c t) -> p c t", t=128),
                     dr["qf"][:, cols].rearrange("(c p) t -> p c t", p=128),
                     deps=[rel_qft])
        v_ol = SY.dma(sb["olt"][:, :], dr["oloc"][cols, :], deps=[rel_olt])
        v_pp = SY.dma(sb["qpp"][:, :], dr["qdec_pp"][cols, :])
        v_xi = SY.dma(sb["xint"][:, :], dr["xin"][cols, :], deps=[rel_xint])
        v_oa = 0
        for g4 in range(HKV):
            psC = sb[f"psC{g4 % 2}"]
            fns = []
            for j in range(4):
                h = g4 * 4 + j
                fns.append(lambda e, j=j, h=h, psC=psC: e.matmul(
                    psC[:, j * 128:(j + 1) * 128],
                    sb["qft"][:, h * 128:(h + 1) * 128],
                    sb["kvbf"][:, h * 128:(h + 1) * 128], start=True, stop=True))
            v_mm = TE.chain(fns, deps=[(sy, v_q), (ac, v_kvbf), rel_psC[g4 % 2]])
            for j in range(4):
                h = g4 * 4 + j
                v_oa = VE.op(lambda e, j=j, h=h, psC=psC: e.scalar_tensor_tensor(
                    sb["o_as"][:, h * 128:(h + 1) * 128], psC[:, j * 128:(j + 1) * 128],
                    sb["qpp"][:, h:h + 1], sb["olt"][:, h * 128:(h + 1) * 128],
                    ALU.mult, ALU.add),
                    deps=[(pe, v_mm), (sy, max(v_ol, v_pp)), rel_oas])
            rel_psC[g4 % 2] = (ve, v_oa)
        rel_olt = (ve, v_oa)
        rel_qft = (pe, pe.v)
        v_trd = emit_tr(ph, sb, lambda k: sb["o_as"][:, k * 128:(k + 1) * 128], ND,
                        lambda k: sb[f"oT{k}"][:, :],
                        [(ve, v_oa), (sy, v_id), rel_oT], rel_pt)
        rel_oas = (pe, pe.v)
        v_s1 = 0
        for dblk in range(4):
            psW = sb[f"psW{dblk % 2}"]
            fns = []
            for k in range(ND):
                fns.append(lambda e, k=k, dblk=dblk, psW=psW: e.matmul(
                    psW[:, :], sb[f"oT{k}"][:, :],
                    sb["wo"][:, k * 2048 + dblk * 512: k * 2048 + (dblk + 1) * 512],
                    start=(k == 0), stop=(k == ND - 1)))
            v_mm = TE.chain(fns, deps=[(ve, v_trd), (gd, v_wo), rel_psW[dblk % 2]])
            v_s1 = VE.op(lambda e, dblk=dblk, psW=psW: e.tensor_tensor(
                sb["s1t"][:, dblk * 512:(dblk + 1) * 512], psW[:, :],
                sb["xint"][:, dblk * 512:(dblk + 1) * 512], ALU.add),
                deps=[(pe, v_mm), (sy, v_xi), rel_s1t])
            rel_psW[dblk % 2] = (ve, v_s1)
        rel_oT = (pe, pe.v)
        rel_xint = (ve, v_s1)
        v_st = SY.dma(dr["S1"][cols, :], sb["s1t"][:, :], deps=[(ve, v_s1)])
        rel_s1t = (sy, v_st)
    SY.wait(sy, sy.v)
    GD.wait(gd, gd.v)

def emit_B4(ph, dr, sb, tab):
    TE, SC, VE, SY, GD = ph.TE, ph.SC, ph.VE, ph.SY, ph.GD
    pe, ac, ve, sy, gd = (ph.ctrs[n] for n in ("pe", "ac", "ve", "sy", "gd"))

    v_id = SY.dma(sb["c_ident2"][:, :], dr["ident"])
    v_ones = VE.op(lambda e: e.memset(sb["onesc"][:, :], 1.0))
    v_ones = VE.op(lambda e: e.memset(sb["onesr"][:, :], 1.0))

    rel_s1t = (ac, 0)
    rel_xn1 = (ve, 0)
    rel_xn2 = (ve, 0)
    rel_xqb = (pe, 0)
    rel_pt = [(ve, 0), (ve, 0)]
    for t128 in range(NT):
        cols = slice(t128 * 128, (t128 + 1) * 128)
        v_x = SY.dma(sb["s1t"][:, :], dr["S1"][cols, :], deps=[rel_s1t])
        v_ln = emit_ln(ph, sb["s1t"], sb["xn1"], sb["stats"], 1e-5,
                       deps=[(sy, v_x), rel_xn1])
        v_n1 = SC.op(lambda e: e.activation(sb["xn1"][:, :], sb["s1t"][:, :], AF.Identity,
                                            scale=sb["stats"][:, 6:7], bias=sb["stats"][:, 7:8]),
                     deps=[(ve, v_ln)])
        rel_s1t = (ac, v_n1)
        v_sq = SC.op(lambda e: e.activation(sb["xn2"][:, :], sb["xn1"][:, :], AF.Square,
                                            accum_out=sb["stats"][:, 1:2]),
                     deps=[rel_xn2])
        v_r1 = VE.op(lambda e: e.tensor_scalar(sb["stats"][:, 2:3], sb["stats"][:, 1:2],
                                               float(np.float32(1.0 / Dm)), 1e-8, ALU.mult, ALU.add),
                     deps=[(ac, v_sq)])
        v_r2 = SC.op(lambda e: e.activation(sb["stats"][:, 3:4], sb["stats"][:, 2:3], AF.Sqrt),
                     deps=[(ve, v_r1)])
        v_r3 = VE.op(lambda e: e.reciprocal(sb["stats"][:, 3:4], sb["stats"][:, 3:4]),
                     deps=[(ac, v_r2)])
        v_x2 = VE.op(lambda e: e.tensor_scalar(sb["xn2"][:, :], sb["xn1"][:, :],
                                               sb["stats"][:, 3:4], None, ALU.mult),
                     deps=[(ve, v_r3), (ac, v_sq)])
        v_mx = VE.op(lambda e: e.reduce_max(sb["stats"][:, 4:5], sb["xn2"][:, :], axis=AX.X,
                                            apply_absolute_value=True),
                     deps=[(ve, v_x2)])
        v_sc = VE.op(lambda e, t128=t128: e.tensor_scalar(
            sb["sxall"][:, t128:t128 + 1], sb["stats"][:, 4:5],
            float(np.float32(1.0 / 127.0)), None, ALU.mult), deps=[(ve, v_mx)])
        v_e = VE.op(lambda e, t128=t128: e.tensor_scalar(
            sb["stats"][:, 5:6], sb["sxall"][:, t128:t128 + 1], 1e-8, None, ALU.add),
            deps=[(ve, v_sc)])
        v_rc = VE.op(lambda e: e.reciprocal(sb["stats"][:, 5:6], sb["stats"][:, 5:6]),
                     deps=[(ve, v_e)])
        v_q1 = VE.op(lambda e: e.tensor_scalar(sb["xn1"][:, :], sb["xn2"][:, :],
                                               sb["stats"][:, 5:6], MAGIC, ALU.mult, ALU.add),
                     deps=[(ve, v_rc)])
        v_q2 = VE.op(lambda e: e.tensor_scalar(sb["xqb"][:, :], sb["xn1"][:, :],
                                               MAGIC, None, ALU.subtract),
                     deps=[(ve, v_q1), rel_xqb])
        rel_xn1 = (ve, v_q2)
        rel_xn2 = (ve, v_q1)
        emit_tr(ph, sb, lambda k: sb["xqb"][:, k * 128:(k + 1) * 128], ND,
                lambda k, t128=t128: sb[f"xqT{k}"][:, t128 * 128:(t128 + 1) * 128],
                [(ve, v_q2), (sy, v_id)], rel_pt, identname="c_ident2")
        rel_xqb = (pe, pe.v)
    v_xq_done = ve.v

    rel_wst = (ac, 0)
    rel_wab = (pe, 0)
    rel_psG = (ve, 0)
    rel_psF = [(ac, 0), (ac, 0)]
    rel_ef = [(ve, 0)]
    rel_s2 = (sy, 0)
    rel_sl1 = (ve, 0)
    for nblk in range(4):
        ncols = slice(nblk * 512, (nblk + 1) * 512)
        v_w = SY.dma(sb["wst"][:, :].rearrange("p (c j) -> p c j", j=512),
                     dr["w0fT"][:, ncols].rearrange("(c p) j -> p c j", p=128),
                     deps=[rel_wst])
        v_g = 0
        for k in range(ND):
            v_ab = SC.op(lambda e, k=k: e.activation(
                sb["wab"][:, :], sb["wst"][:, k * 512:(k + 1) * 512], AF.Abs),
                deps=[(sy, v_w), rel_wab])
            v_g = TE.op(lambda e, k=k, s=(k == 0), t=(k == ND - 1): e.matmul(
                sb["psG"][:, :], sb["onesc"][:, :], sb["wab"][:, :], start=s, stop=t),
                deps=[(ac, v_ab), (ve, v_ones)] + ([rel_psG] if k == 0 else []), inc=1)
            rel_wab = (pe, v_g)
        v_gr = VE.op(lambda e: e.tensor_scalar(sb["grow"][:, :], sb["psG"][:, :],
                                               float(np.float32(1.0 / Dm)), None, ALU.mult),
                     deps=[(pe, v_g)])
        rel_psG = (ve, v_gr)
        v_b1 = TE.op(lambda e: e.matmul(sb["psB"][:, :], sb["onesr"][:, :], sb["grow"][:, :],
                                        start=True, stop=True), deps=[(ve, v_gr)])
        v_gmb = VE.op(lambda e: e.tensor_copy(sb["gmb"][:, :], sb["psB"][:, :]),
                      deps=[(pe, v_b1)])
        v_gi = VE.op(lambda e: e.tensor_scalar(sb["grow"][:, :], sb["grow"][:, :],
                                               1e-8, None, ALU.add),
                     deps=[(ve, v_gr), (pe, v_b1)])
        v_rc = VE.op(lambda e: e.reciprocal(sb["grow"][:, :], sb["grow"][:, :]),
                     deps=[(ve, v_gi)])
        v_b2 = TE.op(lambda e: e.matmul(sb["psB"][:, :], sb["onesr"][:, :], sb["grow"][:, :],
                                        start=True, stop=True), deps=[(ve, v_rc), (ve, v_gmb)])
        v_giv = VE.op(lambda e: e.tensor_copy(sb["ginvb"][:, :], sb["psB"][:, :]),
                      deps=[(pe, v_b2)])
        v_br = SY.dma(sb["brow"][:, :], dr["b0f"][:, ncols])
        v_b3 = TE.op(lambda e: e.matmul(sb["psB"][:, :], sb["onesr"][:, :], sb["brow"][:, :],
                                        start=True, stop=True), deps=[(sy, v_br), (ve, v_giv)])
        v_bb = VE.op(lambda e: e.tensor_copy(sb["bb"][:, :], sb["psB"][:, :]),
                     deps=[(pe, v_b3)])
        v_wt = {}
        for k in range(ND):
            v_t1 = VE.op(lambda e, k=k: e.tensor_tensor(
                sb["ef"][:, :], sb["wst"][:, k * 512:(k + 1) * 512], sb["ginvb"][:, :], ALU.mult),
                deps=[(ve, v_giv), rel_ef[0]])
            v_t2 = VE.op(lambda e: e.tensor_scalar(sb["ef"][:, :], sb["ef"][:, :],
                                                   MAGIC, MAGIC, ALU.add, ALU.subtract),
                         deps=[(ve, v_t1)])
            v_t3 = VE.op(lambda e, k=k: e.tensor_scalar(sb[f"wtn{k}"][:, :], sb["ef"][:, :],
                                                        1.0, -1.0, ALU.min, ALU.max),
                         deps=[(ve, v_t2)])
            rel_ef[0] = (ve, v_t3)
            v_wt[k] = v_t3
        rel_wst = (ve, max(v_wt.values()))
        v_wq = max(v_wt.values())
        for t128 in range(NT):
            cols = slice(t128 * 128, (t128 + 1) * 128)
            psF = sb[f"psF{t128 % 2}"]
            fns = []
            for k in range(ND):
                fns.append(lambda e, k=k, t128=t128, psF=psF: e.matmul(
                    psF[:, :], sb[f"xqT{k}"][:, t128 * 128:(t128 + 1) * 128],
                    sb[f"wtn{k}"][:, :], start=(k == 0), stop=(k == ND - 1)))
            v_mm = TE.chain(fns, deps=[(ve, v_wq), (ve, v_xq_done), rel_psF[t128 % 2]])
            v_e1 = SC.op(lambda e, t128=t128, psF=psF: e.activation(
                sb["ef"][:, :], psF[:, :], AF.Identity, scale=sb["sxall"][:, t128:t128 + 1]),
                deps=[(pe, v_mm), rel_ef[0]])
            rel_psF[t128 % 2] = (ac, v_e1)
            v_sl = SY.dma(sb["sl1"][:, :], dr["S1"][cols, ncols], deps=[rel_sl1])
            v_e2 = VE.op(lambda e: e.tensor_tensor(sb["e2"][:, :], sb["ef"][:, :],
                                                   sb["gmb"][:, :], ALU.mult),
                         deps=[(ac, v_e1), (ve, v_bb)])
            rel_ef[0] = (ve, v_e2)
            v_e3 = VE.op(lambda e: e.tensor_tensor(sb["e2"][:, :], sb["e2"][:, :],
                                                   sb["bb"][:, :], ALU.add),
                         deps=[(ve, v_e2)])
            v_e4 = VE.op(lambda e: e.tensor_tensor(sb["s2t"][:, :], sb["e2"][:, :],
                                                   sb["sl1"][:, :], ALU.add),
                         deps=[(ve, v_e3), (sy, v_sl), rel_s2])
            rel_sl1 = (ve, v_e4)
            v_st = SY.dma(dr["S2"][cols, ncols], sb["s2t"][:, :], deps=[(ve, v_e4)])
            rel_s2 = (sy, v_st)
    SY.wait(sy, sy.v)
    GD.wait(gd, gd.v)


TEXT = T + 128
DI = 4096
DM1 = 8192
DH1 = 5632
NH1 = DH1 // 128

def decl_C_weights(din):
    din("winT", [Dm, DM1], F32)
    din("convw", [DI, 4], F32)
    din("convb", [DI, 1], F32)
    din("woutT", [DI, Dm], F32)
    din("wm1T", [Dm, DM1], F32)
    din("wm2T", [DM1, Dm], F32)
    din("w1T", [Dm, DH1], F32)
    din("w3T", [Dm, DH1], F32)
    din("w2T", [DH1, Dm], F32)

def decl_C_scratch(dint):
    dint("Z0", [T, Dm], F32)
    dint("Z1", [T, Dm], F32)
    dint("S3", [T, Dm], F32)
    dint("Y", [NT, DI, 128], BF16)
    dint("R", [NT, DM1, 128], BF16)
    dint("Hd", [T, DH1], F32)
    dint("HqT", [NT, DH1, 128], BF16)

def build_C(tab):
    nc = bass.Bass("TRN2", target_bir_lowering=False)
    dr = {}
    def din(name, shape, dt):
        dr[name] = nc.dram_tensor(name, shape, dt, kind="ExternalInput").ap()
    def dout(name, shape, dt):
        dr[name] = nc.dram_tensor(name, shape, dt, kind="ExternalOutput").ap()
    def dint(name, shape, dt):
        dr[name] = nc.dram_tensor(name, shape, dt, kind="Internal").ap()
    din("S2", [T, Dm], F32)
    din("halo", [3, Dm], F32)
    decl_C_weights(din)
    din("ident", [128, 128], BF16)
    dout("outp", [T, Dm], F32)
    decl_C_scratch(dint)
    phases_C(nc, dr, tab)
    return nc

def phases_C(nc, dr, tab, fused=False):
    with contextlib.ExitStack() as st:
        sb = {}
        def alloc(name, shape, dt):
            sb[name] = st.enter_context(nc.sbuf_tensor(f"c5_{name}", shape, dt))
        def palloc(name, shape, dt):
            sb[name] = st.enter_context(nc.psum_tensor(f"c5_{name}", shape, dt))
        for k in range(ND):
            alloc(f"hT{k}", [128, TEXT], BF16)
        alloc("s2t", [128, Dm], F32)
        alloc("xscr", [128, Dm], F32)
        alloc("z0t", [128, Dm], F32)
        alloc("hb", [128, Dm], BF16)
        alloc("stats", [128, 8], F32)
        alloc("c_ident", [128, 128], BF16)
        alloc("wb0", [128, 16 * 128], BF16); alloc("wb1", [128, 16 * 128], BF16)
        alloc("wz0", [128, 16 * 128], BF16); alloc("wz1", [128, 16 * 128], BF16)
        alloc("xps", [128, TEXT], F32)
        alloc("sz", [128, T], F32)
        alloc("cacc", [128, T], F32)
        alloc("yb", [128, T], BF16)
        alloc("cw", [128, 4], F32)
        alloc("cb", [128, 1], F32)
        palloc("pp0", [128, 512], F32); palloc("pp1", [128, 512], F32)
        palloc("ptA", [128, 512], BF16); palloc("ptB", [128, 512], BF16)
        ph = Phase(nc, st, "C5")
        emit_C5ab(ph, dr, sb, tab)
        ph.emit()

    with contextlib.ExitStack() as st:
        sb = {}
        def alloc(name, shape, dt):
            sb[name] = st.enter_context(nc.sbuf_tensor(f"c5c_{name}", shape, dt))
        def palloc(name, shape, dt):
            sb[name] = st.enter_context(nc.psum_tensor(f"c5c_{name}", shape, dt))
        alloc("wob", [128, 32 * 512], BF16)
        alloc("yc0", [128, 32 * 128], BF16); alloc("yc1", [128, 32 * 128], BF16)
        alloc("z0s", [128, 512], F32)
        alloc("z1s", [128, 512], F32)
        palloc("pp0", [128, 512], F32); palloc("pp1", [128, 512], F32)
        ph = Phase(nc, st, "C5c")
        emit_down(ph, dr, sb, "woutT", DI, "Y", "Z0", "Z1", None)
        ph.emit()

    with contextlib.ExitStack() as st:
        sb = {}
        def alloc(name, shape, dt):
            sb[name] = st.enter_context(nc.sbuf_tensor(f"c6_{name}", shape, dt))
        def palloc(name, shape, dt):
            sb[name] = st.enter_context(nc.psum_tensor(f"c6_{name}", shape, dt))
        for k in range(ND):
            alloc(f"hT{k}", [128, T], BF16)
        alloc("z1t", [128, Dm], F32)
        alloc("hb", [128, Dm], BF16)
        alloc("stats", [128, 8], F32)
        alloc("xscr", [128, Dm], F32)
        alloc("c_ident", [128, 128], BF16)
        alloc("wb0", [128, 16 * 128], BF16); alloc("wb1", [128, 16 * 128], BF16)
        alloc("rb", [128, T], BF16)
        palloc("pp0", [128, 512], F32); palloc("pp1", [128, 512], F32)
        palloc("ptA", [128, 512], BF16); palloc("ptB", [128, 512], BF16)
        ph = Phase(nc, st, "C6")
        emit_C6ab(ph, dr, sb, tab)
        ph.emit()

    with contextlib.ExitStack() as st:
        sb = {}
        def alloc(name, shape, dt):
            sb[name] = st.enter_context(nc.sbuf_tensor(f"c6c_{name}", shape, dt))
        def palloc(name, shape, dt):
            sb[name] = st.enter_context(nc.psum_tensor(f"c6c_{name}", shape, dt))
        alloc("wob", [128, 64 * 512], BF16)
        alloc("yc0", [128, 64 * 128], BF16); alloc("yc1", [128, 64 * 128], BF16)
        alloc("z0s", [128, 512], F32)
        alloc("z1s", [128, 512], F32)
        alloc("s2s", [128, 512], F32)
        palloc("pp0", [128, 512], F32); palloc("pp1", [128, 512], F32)
        ph = Phase(nc, st, "C6c")
        emit_down(ph, dr, sb, "wm2T", DM1, "R", "Z1", "S3", "S2")
        ph.emit()

    with contextlib.ExitStack() as st:
        sb = {}
        def alloc(name, shape, dt):
            sb[name] = st.enter_context(nc.sbuf_tensor(f"c7_{name}", shape, dt))
        def palloc(name, shape, dt):
            sb[name] = st.enter_context(nc.psum_tensor(f"c7_{name}", shape, dt))
        for k in range(ND):
            alloc(f"xqT{k}", [128, T], BF16)
        alloc("wst", [128, 512], F32)
        alloc("wsta", [128, 512], F32)
        alloc("wstb", [128, 512], F32)
        for k in range(ND):
            alloc(f"w1q{k}a", [128, 512], BF16)
            alloc(f"w3q{k}a", [128, 512], BF16)
            alloc(f"w1q{k}b", [128, 512], BF16)
            alloc(f"w3q{k}b", [128, 512], BF16)
        alloc("s3t", [128, Dm], F32)
        alloc("xscr", [128, Dm], F32)
        alloc("xn1", [128, Dm], F32)
        alloc("xqb", [128, Dm], BF16)
        alloc("stats", [128, 8], F32)
        alloc("sxall", [128, NT], F32)
        alloc("hmx", [128, NT], F32)
        alloc("dq1", [128, NT], F32)
        alloc("dq3", [128, NT], F32)
        alloc("wabs", [128, 512], BF16)
        alloc("onesb", [128, 1], BF16)
        alloc("onesr", [1, 128], F32)
        alloc("wsums", [1, 512], F32)
        alloc("wtot", [1, 8], F32)
        alloc("swb", [128, 8], F32)
        alloc("s1f", [128, 512], F32)
        alloc("s3f", [128, 512], F32)
        alloc("hf", [128, 512], F32)
        alloc("qs1", [128, 512], F32)
        alloc("qs2", [128, 512], F32)
        alloc("magicc", [128, 1], F32)
        alloc("tmp1", [128, 1], F32)
        alloc("c_ident", [128, 128], BF16)
        palloc("psW", [1, 512], F32)
        palloc("psB", [128, 8], F32)
        palloc("pF10", [128, 512], F32); palloc("pF11", [128, 512], F32)
        palloc("pF30", [128, 512], F32); palloc("pF31", [128, 512], F32)
        palloc("ptA", [128, 512], BF16); palloc("ptB", [128, 512], BF16)
        ph = Phase(nc, st, "C7")
        emit_C7ab(ph, dr, sb, tab, skip_wscale=fused)
        ph.emit()

    with contextlib.ExitStack() as st:
        sb = {}
        def alloc(name, shape, dt):
            sb[name] = st.enter_context(nc.sbuf_tensor(f"c7c_{name}", shape, dt))
        def palloc(name, shape, dt):
            sb[name] = st.enter_context(nc.psum_tensor(f"c7c_{name}", shape, dt))
        alloc("hrow", [128, DH1], F32)
        alloc("hqb0", [128, DH1], BF16); alloc("hqb1", [128, DH1], BF16)
        alloc("hqt", [128, NH1 * 128], BF16)
        alloc("hq0", [128, NH1 * 128], BF16); alloc("hq1", [128, NH1 * 128], BF16)
        alloc("wsta", [128, 512], F32)
        alloc("wstb", [128, 512], F32)
        for k in range(NH1):
            alloc(f"w2q{k}A", [128, 512], BF16)
            alloc(f"w2q{k}B", [128, 512], BF16)
        alloc("shc", [128, NT], F32)
        alloc("dqh", [128, NT], F32)
        alloc("s3s", [128, 512], F32)
        alloc("of", [128, 512], F32)
        alloc("oo", [128, 512], F32)
        alloc("qs1", [128, 512], F32)
        alloc("qs2", [128, 512], F32)
        alloc("magicc", [128, 1], F32)
        alloc("swc", [128, 8], F32)
        alloc("c_ident", [128, 128], BF16)
        palloc("pp0", [128, 512], F32); palloc("pp1", [128, 512], F32)
        palloc("ptA", [128, 512], BF16); palloc("ptB", [128, 512], BF16)
        ph = Phase(nc, st, "C7c")
        emit_C7c(ph, dr, sb, tab)
        ph.emit()

def emit_rms_scale(ph, sb, src, stats, deps):
    SC, VE = ph.SC, ph.VE
    ac, ve = ph.ctrs["ac"], ph.ctrs["ve"]
    v_sq = SC.op(lambda e: e.activation(sb["xscr"][:, :], src[:, :], AF.Square,
                                        accum_out=sb["stats"][:, 1:2]), deps=deps)
    v_s = SC.op(lambda e: e.activation(sb["stats"][:, 2:3], sb["stats"][:, 1:2], AF.Sqrt))
    v_m = VE.op(lambda e: e.tensor_scalar(sb["stats"][:, 3:4], sb["stats"][:, 2:3],
                                          1e-12, None, ALU.max), deps=[(ac, v_s)])
    v_r = VE.op(lambda e: e.reciprocal(sb["stats"][:, 3:4], sb["stats"][:, 3:4]),
                deps=[(ve, v_m)])
    v_f = VE.op(lambda e: e.tensor_scalar(sb["stats"][:, 3:4], sb["stats"][:, 3:4],
                                          float(np.float32(np.sqrt(np.float32(Dm)))), None,
                                          ALU.mult), deps=[(ve, v_r)])
    return v_f

def emit_C5ab(ph, dr, sb, tab):
    TE, SC, VE, SY, GD = ph.TE, ph.SC, ph.VE, ph.SY, ph.GD
    pe, ac, ve, sy, gd = (ph.ctrs[n] for n in ("pe", "ac", "ve", "sy", "gd"))
    v_id = SY.dma(sb["c_ident"][:, :], dr["ident"])

    rel_s2t = (ac, 0)
    rel_z0 = []
    rel_hb = (pe, 0)
    rel_pt = [(ve, 0), (ve, 0)]
    for it in range(NT + 1):
        tcol = it * 128
        if it == 0:
            v_ms = VE.op(lambda e: e.memset(sb["s2t"][:, :], 0.0), deps=[rel_s2t])
            v_x = SY.dma(sb["s2t"][125:128, :], dr["halo"], deps=[(ve, v_ms)])
        else:
            t128 = it - 1
            v_x = SY.dma(sb["s2t"][:, :], dr["S2"][t128 * 128:(t128 + 1) * 128, :],
                         deps=[rel_s2t])
        v_ln = emit_ln(ph, sb["s2t"], sb["xscr"], sb["stats"], 1e-5,
                       deps=[(sy, v_x)])
        v_z0 = SC.op(lambda e: e.activation(sb["z0t"][:, :], sb["s2t"][:, :], AF.Identity,
                                            scale=sb["stats"][:, 6:7], bias=sb["stats"][:, 7:8]),
                     deps=[(ve, v_ln)] + rel_z0)
        rel_s2t = (ac, v_z0)
        if it > 0:
            v_w0 = SY.dma(dr["Z0"][(it - 1) * 128:it * 128, :], sb["z0t"][:, :],
                          deps=[(ac, v_z0)])
        v_rs = emit_rms_scale(ph, sb, sb["z0t"], sb["stats"], deps=[(ac, v_z0)])
        v_hb = VE.op(lambda e: e.tensor_scalar(sb["hb"][:, :], sb["z0t"][:, :],
                                               sb["stats"][:, 3:4], None, ALU.mult),
                     deps=[(ve, v_rs), rel_hb])
        rel_z0 = [(ve, v_hb)] if it == 0 else [(ve, v_hb), (sy, v_w0)]
        emit_tr(ph, sb, lambda k: sb["hb"][:, k * 128:(k + 1) * 128], ND,
                lambda k, tcol=tcol: sb[f"hT{k}"][:, tcol:tcol + 128],
                [(ve, v_hb), (sy, v_id)], rel_pt)
        rel_hb = (pe, pe.v)
    v_hT = ve.v

    rel_wb = [(pe, 0), (pe, 0)]
    rel_wz = [(pe, 0), (pe, 0)]
    rel_pp = [(ac, 0), (ac, 0)]
    rel_xps = (ve, 0)
    rel_sz = (ve, 0)
    rel_cacc = (ac, 0)
    rel_xscr5 = (ve, 0)
    rel_yb = (sy, 0)
    tblocks = [(0, 512), (512, 512), (1024, 512), (1536, 512), (2048, 128)]
    for ep in range(DI // 128):
        slot = ep % 2
        v_wx = GD.dma(sb[f"wb{slot}"][:, :].rearrange("p (c j) -> p c j", j=128),
                      dr["winT"][:, ep * 128:(ep + 1) * 128].rearrange("(c p) j -> p c j", p=128),
                      deps=[rel_wb[slot]])
        v_wz = GD.dma(sb[f"wz{slot}"][:, :].rearrange("p (c j) -> p c j", j=128),
                      dr["winT"][:, DI + ep * 128: DI + (ep + 1) * 128].rearrange("(c p) j -> p c j", p=128),
                      deps=[rel_wz[slot]])
        v_cw = SY.dma(sb["cw"][:, :], dr["convw"][ep * 128:(ep + 1) * 128, :])
        v_cb = SY.dma(sb["cb"][:, :], dr["convb"][ep * 128:(ep + 1) * 128, :])
        v_xp = 0
        for (c0, cw_) in tblocks:
            pp = sb[f"pp{(c0 // 512) % 2}"]
            fns = []
            for k in range(ND):
                fns.append(lambda e, k=k, c0=c0, cw_=cw_, pp=pp, slot=slot: e.matmul(
                    pp[:, 0:cw_], sb[f"wb{slot}"][:, k * 128:(k + 1) * 128],
                    sb[f"hT{k}"][:, c0:c0 + cw_], start=(k == 0), stop=(k == ND - 1)))
            v_mm = TE.chain(fns, deps=[(gd, v_wx), (ve, v_hT), rel_pp[(c0 // 512) % 2]])
            v_xp = SC.op(lambda e, c0=c0, cw_=cw_, pp=pp: e.activation(
                sb["xps"][:, c0:c0 + cw_], pp[:, 0:cw_], AF.Identity),
                deps=[(pe, v_mm), rel_xps])
            rel_pp[(c0 // 512) % 2] = (ac, v_xp)
        v_sz = 0
        for c0 in range(0, T, 512):
            pp = sb[f"pp{(c0 // 512) % 2}"]
            fns = []
            for k in range(ND):
                fns.append(lambda e, k=k, c0=c0, pp=pp, slot=slot: e.matmul(
                    pp[:, :], sb[f"wz{slot}"][:, k * 128:(k + 1) * 128],
                    sb[f"hT{k}"][:, 128 + c0:128 + c0 + 512], start=(k == 0), stop=(k == ND - 1)))
            v_mm = TE.chain(fns, deps=[(gd, v_wz), (ve, v_hT), rel_pp[(c0 // 512) % 2]])
            v_sz = SC.op(lambda e, c0=c0, pp=pp: e.activation(
                sb["sz"][:, c0:c0 + 512], pp[:, :], AF.Silu),
                deps=[(pe, v_mm), rel_sz])
            rel_pp[(c0 // 512) % 2] = (ac, v_sz)
        rel_wb[slot] = (pe, pe.v)
        rel_wz[slot] = (pe, pe.v)
        v_c0 = VE.op(lambda e: e.tensor_scalar(sb["cacc"][:, :], sb["xps"][:, 125:125 + T],
                                               sb["cw"][:, 0:1], None, ALU.mult),
                     deps=[(ac, v_xp), (sy, v_cw), rel_cacc])
        v_c1 = VE.op(lambda e: e.scalar_tensor_tensor(sb["cacc"][:, :], sb["xps"][:, 126:126 + T],
                                                      sb["cw"][:, 1:2], sb["cacc"][:, :],
                                                      ALU.mult, ALU.add), deps=[(ve, v_c0)])
        v_c2 = VE.op(lambda e: e.scalar_tensor_tensor(sb["cacc"][:, :], sb["xps"][:, 127:127 + T],
                                                      sb["cw"][:, 2:3], sb["cacc"][:, :],
                                                      ALU.mult, ALU.add), deps=[(ve, v_c1)])
        v_c3 = VE.op(lambda e: e.scalar_tensor_tensor(sb["cacc"][:, :], sb["xps"][:, 128:128 + T],
                                                      sb["cw"][:, 3:4], sb["cacc"][:, :],
                                                      ALU.mult, ALU.add), deps=[(ve, v_c2)])
        rel_xps = (ve, v_c3)
        v_sx = SC.op(lambda e: e.activation(sb["xscr"][:, 0:T], sb["cacc"][:, :], AF.Silu,
                                            bias=sb["cb"][:, 0:1]),
                     deps=[(ve, v_c3), (sy, v_cb), rel_cacc, rel_xscr5])
        rel_cacc = (ac, v_sx)
        v_y = VE.op(lambda e: e.tensor_tensor(sb["yb"][:, :], sb["xscr"][:, 0:T],
                                              sb["sz"][:, :], ALU.mult),
                    deps=[(ac, v_sx), (ac, v_sz), rel_yb])
        rel_sz = (ve, v_y)
        rel_xscr5 = (ve, v_y)
        v_st = SY.dma(dr["Y"][:, ep * 128:(ep + 1) * 128, :].rearrange("tb p t -> p tb t"),
                      sb["yb"][:, :].rearrange("p (tb t) -> p tb t", t=128),
                      deps=[(ve, v_y)])
        rel_yb = (sy, v_st)
    SY.wait(sy, sy.v)
    GD.wait(gd, gd.v)

def emit_down(ph, dr, sb, wname, E, yname, resname, outname, res2name):
    TE, SC, VE, SY, GD = ph.TE, ph.SC, ph.VE, ph.SY, ph.GD
    pe, ac, ve, sy, gd = (ph.ctrs[n] for n in ("pe", "ac", "ve", "sy", "gd"))
    NE = E // 128
    rel_wob = (pe, 0)
    rel_yc = [(pe, 0), (pe, 0)]
    rel_pp = [(ve, 0), (ve, 0)]
    rel_z0 = (ve, 0)
    rel_z1 = (sy, 0)
    rel_s2 = (ve, 0)
    for dblk in range(4):
        dcols = slice(dblk * 512, (dblk + 1) * 512)
        v_w = GD.dma(sb["wob"][:, :].rearrange("p (c j) -> p c j", j=512),
                     dr[wname][:, dcols].rearrange("(c p) j -> p c j", p=128),
                     deps=[rel_wob])
        for t128 in range(NT):
            cols = slice(t128 * 128, (t128 + 1) * 128)
            slot = t128 % 2
            yc = sb[f"yc{slot}"]
            v_y = SY.dma(yc[:, :].rearrange("p (c t) -> p c t", t=128),
                         dr[yname][t128].rearrange("(c p) t -> p c t", p=128),
                         deps=[rel_yc[slot]])
            pp = sb[f"pp{slot}"]
            fns = []
            for k in range(NE):
                fns.append(lambda e, k=k, pp=pp, yc=yc: e.matmul(
                    pp[:, :], yc[:, k * 128:(k + 1) * 128],
                    sb["wob"][:, k * 512:(k + 1) * 512], start=(k == 0), stop=(k == NE - 1)))
            v_mm = TE.chain(fns, deps=[(gd, v_w), (sy, v_y), rel_pp[slot]])
            rel_yc[slot] = (pe, v_mm)
            v_z0 = SY.dma(sb["z0s"][:, :], dr[resname][cols, dcols], deps=[rel_z0])
            v_a = VE.op(lambda e, pp=pp: e.tensor_tensor(sb["z1s"][:, :], pp[:, :],
                                                         sb["z0s"][:, :], ALU.add),
                        deps=[(pe, v_mm), (sy, v_z0), rel_z1])
            rel_pp[slot] = (ve, v_a)
            rel_z0 = (ve, v_a)
            if res2name is not None:
                v_s2 = SY.dma(sb["s2s"][:, :], dr[res2name][cols, dcols], deps=[rel_s2])
                v_a = VE.op(lambda e: e.tensor_tensor(sb["z1s"][:, :], sb["z1s"][:, :],
                                                      sb["s2s"][:, :], ALU.add),
                            deps=[(ve, v_a), (sy, v_s2)])
                rel_s2 = (ve, v_a)
            v_st = SY.dma(dr[outname][cols, dcols], sb["z1s"][:, :], deps=[(ve, v_a)])
            rel_z1 = (sy, v_st)
        rel_wob = (pe, pe.v)
    SY.wait(sy, sy.v)
    GD.wait(gd, gd.v)

def emit_C6ab(ph, dr, sb, tab):
    TE, SC, VE, SY, GD = ph.TE, ph.SC, ph.VE, ph.SY, ph.GD
    pe, ac, ve, sy, gd = (ph.ctrs[n] for n in ("pe", "ac", "ve", "sy", "gd"))
    v_id = SY.dma(sb["c_ident"][:, :], dr["ident"])

    rel_z1t = (ac, 0)
    rel_hb = (pe, 0)
    rel_pt = [(ve, 0), (ve, 0)]
    for t128 in range(NT):
        v_x = SY.dma(sb["z1t"][:, :], dr["Z1"][t128 * 128:(t128 + 1) * 128, :],
                     deps=[rel_z1t])
        v_rs = emit_rms_scale(ph, sb, sb["z1t"], sb["stats"], deps=[(sy, v_x)])
        v_hb = VE.op(lambda e: e.tensor_scalar(sb["hb"][:, :], sb["z1t"][:, :],
                                               sb["stats"][:, 3:4], None, ALU.mult),
                     deps=[(ve, v_rs), rel_hb])
        rel_z1t = (ve, v_hb)
        emit_tr(ph, sb, lambda k: sb["hb"][:, k * 128:(k + 1) * 128], ND,
                lambda k, t128=t128: sb[f"hT{k}"][:, t128 * 128:(t128 + 1) * 128],
                [(ve, v_hb), (sy, v_id)], rel_pt)
        rel_hb = (pe, pe.v)
    v_hT = ve.v

    rel_wb = [(pe, 0), (pe, 0)]
    rel_pp = [(ac, 0), (ac, 0)]
    rel_rb = (sy, 0)
    for ec in range(DM1 // 128):
        slot = ec % 2
        v_w = GD.dma(sb[f"wb{slot}"][:, :].rearrange("p (c j) -> p c j", j=128),
                     dr["wm1T"][:, ec * 128:(ec + 1) * 128].rearrange("(c p) j -> p c j", p=128),
                     deps=[rel_wb[slot]])
        v_r = 0
        for c0 in range(0, T, 512):
            pp = sb[f"pp{(c0 // 512) % 2}"]
            fns = []
            for k in range(ND):
                fns.append(lambda e, k=k, c0=c0, pp=pp, slot=slot: e.matmul(
                    pp[:, :], sb[f"wb{slot}"][:, k * 128:(k + 1) * 128],
                    sb[f"hT{k}"][:, c0:c0 + 512], start=(k == 0), stop=(k == ND - 1)))
            v_mm = TE.chain(fns, deps=[(gd, v_w), (ve, v_hT), rel_pp[(c0 // 512) % 2]])
            v_r = SC.op(lambda e, c0=c0, pp=pp: e.activation(
                sb["rb"][:, c0:c0 + 512], pp[:, :], AF.Relu),
                deps=[(pe, v_mm), rel_rb])
            rel_pp[(c0 // 512) % 2] = (ac, v_r)
        rel_wb[slot] = (pe, pe.v)
        v_st = SY.dma(dr["R"][:, ec * 128:(ec + 1) * 128, :].rearrange("tb p t -> p tb t"),
                      sb["rb"][:, :].rearrange("p (tb t) -> p tb t", t=128),
                      deps=[(ac, v_r)])
        rel_rb = (sy, v_st)
    SY.wait(sy, sy.v)
    GD.wait(gd, gd.v)

def emit_C7ab(ph, dr, sb, tab, skip_wscale=False):
    TE, SC, VE, SY, GD = ph.TE, ph.SC, ph.VE, ph.SY, ph.GD
    pe, ac, ve, sy, gd = (ph.ctrs[n] for n in ("pe", "ac", "ve", "sy", "gd"))
    v_id = SY.dma(sb["c_ident"][:, :], dr["ident"])
    v_ones = VE.op(lambda e: e.memset(sb["onesb"][:, :], 1.0))
    v_ones = VE.op(lambda e: e.memset(sb["onesr"][:, :], 1.0))
    v_w0 = VE.op(lambda e: e.memset(sb["wtot"][:, :], 0.0))
    v_mg = VE.op(lambda e: e.memset(sb["magicc"][:, :], MAGIC))

    if skip_wscale:
        v_ld = SY.dma(sb["wtot"][:, :], dr["WSC"][:, 0:8])
        v_bc = TE.op(lambda e: e.matmul(sb["psB"][:, :], sb["onesr"][:, :],
                                        sb["wtot"][:, :], start=True, stop=True),
                     deps=[(sy, v_ld), (ve, v_ones)])
        v_swb = VE.op(lambda e: e.tensor_copy(sb["swb"][:, :], sb["psB"][:, :]),
                      deps=[(pe, v_bc)])
    else:
        rel_wst = (ve, 0)
        rel_wabs = (pe, 0)
        rel_psW = (ve, 0)
        wmeta = [("w1T", ND, DH1, 0), ("w3T", ND, DH1, 2), ("w2T", NH1, Dm, 4)]
        for (wn, NK, NCOL, coli) in wmeta:
            NB = NCOL // 512
            ntile = NK * NB
            ti = 0
            v_g = 0
            for nb in range(NB):
                for k in range(NK):
                    v_w = SY.dma(sb["wst"][:, 0:512], dr[wn][k * 128:(k + 1) * 128,
                                                             nb * 512:(nb + 1) * 512],
                                 deps=[rel_wst])
                    v_ab = SC.op(lambda e: e.activation(
                        sb["wabs"][:, :], sb["wst"][:, 0:512], AF.Abs),
                        deps=[(sy, v_w), rel_wabs])
                    rel_wst = (ac, v_ab)
                    v_g = TE.op(lambda e, s=(ti == 0), t=(ti == ntile - 1): e.matmul(
                        sb["psW"][:, :], sb["onesb"][:, :], sb["wabs"][:, :], start=s, stop=t),
                        deps=[(ac, v_ab), (ve, v_ones)] + ([rel_psW] if ti == 0 else []))
                    rel_wabs = (pe, v_g)
                    ti += 1
            v_rs = VE.op(lambda e, coli=coli: e.reduce_sum(
                sb["wtot"][:, coli:coli + 1], sb["psW"][:, :], axis=AX.X),
                deps=[(pe, v_g)])
            rel_psW = (ve, v_rs)
            v_m = VE.op(lambda e, coli=coli: e.tensor_scalar(
                sb["wtot"][:, coli + 1:coli + 2], sb["wtot"][:, coli:coli + 1],
                float(np.float32(1.0 / (DH1 * Dm))), 1e-5,
                ALU.mult, ALU.max), deps=[(ve, v_rs)])
            v_r = VE.op(lambda e, coli=coli: e.reciprocal(
                sb["wtot"][:, coli:coli + 1], sb["wtot"][:, coli + 1:coli + 2]),
                deps=[(ve, v_m)])
        v_bc = TE.op(lambda e: e.matmul(sb["psB"][:, :], sb["onesr"][:, :], sb["wtot"][:, :],
                                        start=True, stop=True), deps=[(ve, ve.v)])
        v_swb = VE.op(lambda e: e.tensor_copy(sb["swb"][:, :], sb["psB"][:, :]),
                      deps=[(pe, v_bc)])

    rel_s3t = (ac, 0)
    rel_xn1 = (ve, 0)
    rel_xscr = (ve, 0)
    rel_xqb = (pe, 0)
    rel_pt = [(ve, 0), (ve, 0)]
    for t128 in range(NT):
        cols = slice(t128 * 128, (t128 + 1) * 128)
        v_x = SY.dma(sb["s3t"][:, :], dr["S3"][cols, :], deps=[rel_s3t])
        v_ln = emit_ln(ph, sb["s3t"], sb["xscr"], sb["stats"], 1e-5,
                       deps=[(sy, v_x), rel_xscr])
        v_n1 = SC.op(lambda e: e.activation(sb["xn1"][:, :], sb["s3t"][:, :], AF.Identity,
                                            scale=sb["stats"][:, 6:7], bias=sb["stats"][:, 7:8]),
                     deps=[(ve, v_ln), rel_xn1])
        rel_s3t = (ac, v_n1)
        v_mx = VE.op(lambda e: e.reduce_max(sb["stats"][:, 4:5], sb["xn1"][:, :], axis=AX.X,
                                            apply_absolute_value=True),
                     deps=[(ac, v_n1)])
        v_c = VE.op(lambda e: e.tensor_scalar(sb["stats"][:, 4:5], sb["stats"][:, 4:5],
                                              1e-5, None, ALU.max), deps=[(ve, v_mx)])
        v_dq = VE.op(lambda e, t128=t128: e.tensor_scalar(
            sb["sxall"][:, t128:t128 + 1], sb["stats"][:, 4:5],
            float(np.float32(1.0 / 127.0)), None, ALU.mult), deps=[(ve, v_c)])
        v_r = VE.op(lambda e: e.reciprocal(sb["stats"][:, 5:6], sb["stats"][:, 4:5]),
                    deps=[(ve, v_dq)])
        v_s = VE.op(lambda e: e.tensor_scalar(sb["stats"][:, 5:6], sb["stats"][:, 5:6],
                                              127.0, None, ALU.mult), deps=[(ve, v_r)])
        v_q1 = VE.op(lambda e: e.tensor_scalar(sb["xscr"][:, :], sb["xn1"][:, :],
                                               sb["stats"][:, 5:6], MAGIC, ALU.mult, ALU.add),
                     deps=[(ve, v_s)])
        v_q2 = VE.op(lambda e: e.tensor_scalar(sb["xqb"][:, :], sb["xscr"][:, :],
                                               MAGIC, None, ALU.subtract),
                     deps=[(ve, v_q1), rel_xqb])
        rel_xn1 = (ve, v_q2)
        rel_xscr = (ve, v_q2)
        emit_tr(ph, sb, lambda k: sb["xqb"][:, k * 128:(k + 1) * 128], ND,
                lambda k, t128=t128: sb[f"xqT{k}"][:, t128 * 128:(t128 + 1) * 128],
                [(ve, v_q2), (sy, v_id)], rel_pt)
        rel_xqb = (pe, pe.v)
    v_xq = ve.v
    v_d1 = VE.op(lambda e: e.tensor_scalar(sb["dq1"][:, :], sb["sxall"][:, :],
                                           sb["swb"][:, 1:2], None, ALU.mult),
                 deps=[(ve, v_swb), (ve, v_xq)])
    v_d3 = VE.op(lambda e: e.tensor_scalar(sb["dq3"][:, :], sb["sxall"][:, :],
                                           sb["swb"][:, 3:4], None, ALU.mult))
    v_hm = VE.op(lambda e: e.memset(sb["hmx"][:, :], 0.0))

    NBLK = DH1 // 512
    rel_wstg = [(ac, 0), (ac, 0)]
    rel_set = [(pe, 0), (pe, 0)]
    v_wq_set = [0, 0]
    rel_pF1 = [(ac, 0), (ac, 0)]
    rel_pF3 = [(ac, 0), (ac, 0)]
    rel_s1f = (ve, 0)
    rel_s3f = (ve, 0)
    rel_hf = (sy, 0)
    rel_qs = (ve, 0)
    sfx = ["a", "b"]

    def quant_tile(wn, qpre, swcol, k, nblk):
        nonlocal rel_qs
        st = sb[f"wst{sfx[k % 2]}"]
        ncols = slice(nblk * 512, (nblk + 1) * 512)
        v_w = SY.dma(st[:, :], dr[wn][k * 128:(k + 1) * 128, ncols],
                     deps=[rel_wstg[k % 2], rel_set[nblk % 2]])
        v_t1 = SC.op(lambda e, st=st, swcol=swcol: e.activation(
            sb["qs1"][:, :], st[:, :], AF.Identity,
            scale=sb["swb"][:, swcol:swcol + 1], bias=sb["magicc"][:, 0:1]),
            deps=[(sy, v_w), (ve, v_swb), (ve, v_mg), rel_qs])
        rel_wstg[k % 2] = (ac, v_t1)
        v_t2 = VE.op(lambda e: e.tensor_scalar(sb["qs2"][:, :], sb["qs1"][:, :],
                                               MAGIC, 1.0, ALU.subtract, ALU.min),
                     deps=[(ac, v_t1)])
        v_l = VE.op(lambda e, k=k, qpre=qpre, nblk=nblk: e.tensor_scalar(
            sb[f"{qpre}{k}{sfx[nblk % 2]}"][:, :], sb["qs2"][:, :], -1.0, None, ALU.max),
            deps=[(ve, v_t2)])
        rel_qs = (ve, v_l)
        return v_l

    for k in range(ND):
        quant_tile("w1T", "w1q", 0, k, 0)
        v_wq_set[0] = quant_tile("w3T", "w3q", 2, k, 0)
    for nblk in range(NBLK):
        sx = sfx[nblk % 2]
        v_mmlast = 0
        for t128 in range(NT):
            cols = slice(t128 * 128, (t128 + 1) * 128)
            ncols = slice(nblk * 512, (nblk + 1) * 512)
            slot = t128 % 2
            pF1, pF3 = sb[f"pF1{slot}"], sb[f"pF3{slot}"]
            fns = []
            for k in range(ND):
                fns.append(lambda e, k=k, pF1=pF1, t128=t128, sx=sx: e.matmul(
                    pF1[:, :], sb[f"xqT{k}"][:, t128 * 128:(t128 + 1) * 128],
                    sb[f"w1q{k}{sx}"][:, :], start=(k == 0), stop=(k == ND - 1)))
            v_m1 = TE.chain(fns, deps=[(ve, v_wq_set[nblk % 2]), (ve, v_xq),
                                       rel_pF1[slot]])
            fns = []
            for k in range(ND):
                fns.append(lambda e, k=k, pF3=pF3, t128=t128, sx=sx: e.matmul(
                    pF3[:, :], sb[f"xqT{k}"][:, t128 * 128:(t128 + 1) * 128],
                    sb[f"w3q{k}{sx}"][:, :], start=(k == 0), stop=(k == ND - 1)))
            v_m3 = TE.chain(fns, deps=[rel_pF3[slot]])
            v_mmlast = v_m3
            v_s1 = SC.op(lambda e, t128=t128, pF1=pF1: e.activation(
                sb["s1f"][:, :], pF1[:, :], AF.Silu, scale=sb["dq1"][:, t128:t128 + 1]),
                deps=[(pe, v_m1), (ve, v_d1), rel_s1f])
            rel_pF1[slot] = (ac, v_s1)
            v_s3 = SC.op(lambda e, t128=t128, pF3=pF3: e.activation(
                sb["s3f"][:, :], pF3[:, :], AF.Identity, scale=sb["dq3"][:, t128:t128 + 1]),
                deps=[(pe, v_m3), (ve, v_d3), rel_s3f])
            rel_pF3[slot] = (ac, v_s3)
            v_h = VE.op(lambda e: e.tensor_tensor(sb["hf"][:, :], sb["s1f"][:, :],
                                                  sb["s3f"][:, :], ALU.mult),
                        deps=[(ac, v_s1), (ac, v_s3), rel_hf])
            rel_s1f = (ve, v_h)
            rel_s3f = (ve, v_h)
            v_mx = VE.op(lambda e: e.reduce_max(sb["tmp1"][:, :], sb["hf"][:, :], axis=AX.X,
                                                apply_absolute_value=True),
                         deps=[(ve, v_h)])
            v_hm = VE.op(lambda e, t128=t128: e.tensor_tensor(
                sb["hmx"][:, t128:t128 + 1], sb["hmx"][:, t128:t128 + 1],
                sb["tmp1"][:, :], ALU.max), deps=[(ve, v_mx), (ve, v_hm)])
            v_st = SY.dma(dr["Hd"][cols, ncols], sb["hf"][:, :], deps=[(ve, v_h)])
            rel_hf = (sy, v_st)
            if nblk + 1 < NBLK and t128 < 8:
                for k in (2 * t128, 2 * t128 + 1):
                    quant_tile("w1T", "w1q", 0, k, nblk + 1)
                    v_wq_set[(nblk + 1) % 2] = quant_tile("w3T", "w3q", 2, k, nblk + 1)
        rel_set[nblk % 2] = (pe, v_mmlast)
    v_c = VE.op(lambda e: e.tensor_scalar(sb["hmx"][:, :], sb["hmx"][:, :],
                                          1e-5, None, ALU.max), deps=[(ve, ve.v)])
    v_r = VE.op(lambda e: e.reciprocal(sb["dq1"][:, :], sb["hmx"][:, :]),
                deps=[(ve, v_c), (ac, ac.v)])
    v_s = VE.op(lambda e: e.tensor_scalar(sb["dq1"][:, :], sb["dq1"][:, :],
                                          127.0, None, ALU.mult), deps=[(ve, v_r)])
    v_q = VE.op(lambda e: e.tensor_scalar(sb["dq3"][:, :], sb["hmx"][:, :],
                                          float(np.float32(1.0 / 127.0)), None, ALU.mult),
                deps=[(ve, v_s), (ac, ac.v)])
    v_q2 = VE.op(lambda e: e.tensor_scalar(sb["dq3"][:, :], sb["dq3"][:, :],
                                           sb["swb"][:, 5:6], None, ALU.mult),
                 deps=[(ve, v_q)])
    SY.dma(dr["Z0"][0:128, 0:NT], sb["dq1"][:, :], deps=[(ve, v_s)])
    SY.dma(dr["Z0"][128:256, 0:NT], sb["dq3"][:, :], deps=[(ve, v_q2)])
    SY.dma(dr["Z0"][256:384, 0:8], sb["swb"][:, :], deps=[(ve, v_swb)])
    SY.wait(sy, sy.v)
    GD.wait(gd, gd.v)

def emit_C7c(ph, dr, sb, tab):
    TE, SC, VE, SY, GD = ph.TE, ph.SC, ph.VE, ph.SY, ph.GD
    pe, ac, ve, sy, gd = (ph.ctrs[n] for n in ("pe", "ac", "ve", "sy", "gd"))
    v_id = SY.dma(sb["c_ident"][:, :], dr["ident"])
    v_sc = SY.dma(sb["shc"][:, :], dr["Z0"][0:128, 0:NT])
    v_dq = SY.dma(sb["dqh"][:, :], dr["Z0"][128:256, 0:NT])
    v_sw = SY.dma(sb["swc"][:, :], dr["Z0"][256:384, 0:8])
    v_mg = VE.op(lambda e: e.memset(sb["magicc"][:, :], MAGIC))

    rel_hrow = (ve, 0)
    rel_hqb = [(pe, 0), (pe, 0)]
    rel_hqt = (sy, 0)
    rel_pt = [(ve, 0), (ve, 0)]
    for t128 in range(NT):
        sl = t128 % 2
        cols = slice(t128 * 128, (t128 + 1) * 128)
        hrow, hqb, hqt = sb["hrow"], sb[f"hqb{sl}"], sb["hqt"]
        v_x = SY.dma(hrow[:, :], dr["Hd"][cols, :], deps=[rel_hrow])
        v_q1 = VE.op(lambda e, t128=t128, hrow=hrow: e.tensor_scalar(
            hrow[:, :], hrow[:, :], sb["shc"][:, t128:t128 + 1], MAGIC,
            ALU.mult, ALU.add), deps=[(sy, v_x), (sy, v_sc)])
        v_q2 = VE.op(lambda e, hrow=hrow, hqb=hqb: e.tensor_scalar(
            hqb[:, :], hrow[:, :], MAGIC, None, ALU.subtract),
            deps=[(ve, v_q1), rel_hqb[sl]])
        rel_hrow = (ve, v_q2)
        v_tre = emit_tr(ph, sb, lambda k, hqb=hqb: hqb[:, k * 128:(k + 1) * 128], NH1,
                        lambda k, hqt=hqt: hqt[:, k * 128:(k + 1) * 128],
                        [(ve, v_q2), (sy, v_id), rel_hqt], rel_pt)
        rel_hqb[sl] = (pe, pe.v)
        v_st = SY.dma(dr["HqT"][t128].rearrange("(c p) t -> p c t", p=128),
                      hqt[:, :].rearrange("p (c t) -> p c t", t=128),
                      deps=[(ve, v_tre)])
        rel_hqt = (sy, v_st)

    v_p1 = sy.v
    rel_wstg = [(ac, 0), (ac, 0)]
    rel_set = (pe, 0)
    rel_hq = [(pe, 0), (pe, 0)]
    rel_ppA = (ac, 0)
    rel_ppB = (ac, 0)
    rel_of = (ve, 0)
    rel_oo = (sy, 0)
    rel_s3s = (ve, 0)
    rel_qs = (ve, 0)
    sfx = ["A", "B"]
    for dp in range(2):
        v_l = 0
        for sxi in range(2):
            dblk = 2 * dp + sxi
            dcols = slice(dblk * 512, (dblk + 1) * 512)
            for k in range(NH1):
                st = sb[f"wst{sfx[k % 2].lower()}"]
                v_w = SY.dma(st[:, :], dr["w2T"][k * 128:(k + 1) * 128, dcols],
                             deps=[rel_wstg[k % 2], rel_set])
                v_t1 = SC.op(lambda e, st=st: e.activation(
                    sb["qs1"][:, :], st[:, :], AF.Identity,
                    scale=sb["swc"][:, 4:5], bias=sb["magicc"][:, 0:1]),
                    deps=[(sy, v_w), (sy, v_sw), (ve, v_mg), rel_qs])
                rel_wstg[k % 2] = (ac, v_t1)
                v_t2 = VE.op(lambda e: e.tensor_scalar(sb["qs2"][:, :], sb["qs1"][:, :],
                                                       MAGIC, 1.0, ALU.subtract, ALU.min),
                             deps=[(ac, v_t1)])
                v_l = VE.op(lambda e, k=k, sxi=sxi: e.tensor_scalar(
                    sb[f"w2q{k}{sfx[sxi]}"][:, :], sb["qs2"][:, :], -1.0, None, ALU.max),
                    deps=[(ve, v_t2)])
                rel_qs = (ve, v_l)
        v_wq = v_l
        v_mmlast = 0
        for t128 in range(NT):
            cols = slice(t128 * 128, (t128 + 1) * 128)
            slot = t128 % 2
            hq = sb[f"hq{slot}"]
            v_h = SY.dma(hq[:, :].rearrange("p (c t) -> p c t", t=128),
                         dr["HqT"][t128].rearrange("(c p) t -> p c t", p=128),
                         deps=[rel_hq[slot], (sy, v_p1)])
            for sxi in range(2):
                dblk = 2 * dp + sxi
                dcols = slice(dblk * 512, (dblk + 1) * 512)
                pp = sb[f"pp{sxi}"]
                fns = []
                for k in range(NH1):
                    fns.append(lambda e, k=k, pp=pp, hq=hq, sxi=sxi: e.matmul(
                        pp[:, :], hq[:, k * 128:(k + 1) * 128],
                        sb[f"w2q{k}{sfx[sxi]}"][:, :], start=(k == 0), stop=(k == NH1 - 1)))
                v_mm = TE.chain(fns, deps=[(ve, v_wq), (sy, v_h),
                                           rel_ppA if sxi == 0 else rel_ppB])
                v_mmlast = v_mm
                v_e1 = SC.op(lambda e, t128=t128, pp=pp: e.activation(
                    sb["of"][:, :], pp[:, :], AF.Identity, scale=sb["dqh"][:, t128:t128 + 1]),
                    deps=[(pe, v_mm), (sy, v_dq), rel_of])
                if sxi == 0:
                    rel_ppA = (ac, v_e1)
                else:
                    rel_ppB = (ac, v_e1)
                v_s3 = SY.dma(sb["s3s"][:, :], dr["S3"][cols, dcols], deps=[rel_s3s])
                v_o = VE.op(lambda e: e.tensor_tensor(sb["oo"][:, :], sb["of"][:, :],
                                                      sb["s3s"][:, :], ALU.add),
                            deps=[(ac, v_e1), (sy, v_s3), rel_oo])
                rel_of = (ve, v_o)
                rel_s3s = (ve, v_o)
                v_st = SY.dma(dr["outp"][cols, dcols], sb["oo"][:, :], deps=[(ve, v_o)])
                rel_oo = (sy, v_st)
            rel_hq[slot] = (pe, v_mmlast)
        rel_set = (pe, v_mmlast)
    SY.wait(sy, sy.v)
    GD.wait(gd, gd.v)


def emit_X1(ph, dr, sb):
    GP, VE, SY, GD = ph.GP, ph.VE, ph.SY, ph.GD
    po, ve, sy, gd = (ph.ctrs[n] for n in ("po", "ve", "sy", "gd"))
    v_cc = GP.op(lambda e: e.collective_compute(
        "AllGather", ALU.bypass, replica_groups=[list(range(NCORES))],
        ins=[dr["DpL_h"].ap().opt()], outs=[dr["DpAll_h"].ap().opt()]), inc=1)
    v_ws = SY.dma(sb["wselb"][:, :], dr["wsel"])
    v_kv = VE.op(lambda e: e.memset(sb["kvacc"][:, :], 0.0))
    for r in range(NCORES):
        v_d = SY.dma(sb["dbuf"][:, :], dr["DpAll"][r * 128:(r + 1) * 128, :],
                     deps=[(ve, v_kv)] + ([(po, v_cc)] if r == 0 else []))
        for h in range(H):
            v_kv = VE.op(lambda e, r=r, h=h: e.scalar_tensor_tensor(
                sb["kvacc"][:, h * 128:(h + 1) * 128],
                sb["dbuf"][:, h * 128:(h + 1) * 128],
                sb["wselb"][:, r * 16 + h:r * 16 + h + 1],
                sb["kvacc"][:, h * 128:(h + 1) * 128], ALU.mult, ALU.add),
                deps=[(sy, v_d), (sy, v_ws), (ve, v_kv)])
    SY.dma(dr["KVIN"], sb["kvacc"][:, :], deps=[(ve, v_kv)])
    SY.wait(sy, sy.v)
    GD.wait(gd, gd.v)

def emit_X2(ph, dr, sb):
    GP, VE, SY, GD = ph.GP, ph.VE, ph.SY, ph.GD
    po, ve, sy, gd = (ph.ctrs[n] for n in ("po", "ve", "sy", "gd"))
    v_b = GD.dma(dr["HaloL"], dr["S2"][T - 3:T, :])
    v_cc = GP.op(lambda e: e.collective_compute(
        "AllGather", ALU.bypass, replica_groups=[list(range(NCORES))],
        ins=[dr["HaloL_h"].ap().opt()], outs=[dr["HaloAll_h"].ap().opt()]),
        deps=[(gd, v_b)], inc=1)
    v_mh = SY.dma(sb["mhb"][:, :], dr["mhalo"])
    v_hg = SY.dma(sb["halog"][:, :].rearrange("p (r d) -> p r d", d=Dm),
                  dr["HaloAll"].rearrange("(r p) d -> p r d", p=3),
                  deps=[(po, v_cc)])
    v_a = VE.op(lambda e: e.memset(sb["haloacc"][:, :], 0.0))
    for r in range(NCORES):
        v_a = VE.op(lambda e, r=r: e.scalar_tensor_tensor(
            sb["haloacc"][:, :], sb["halog"][:, r * Dm:(r + 1) * Dm],
            sb["mhb"][0:3, r:r + 1], sb["haloacc"][:, :], ALU.mult, ALU.add),
            deps=[(sy, max(v_hg, v_mh)), (ve, v_a)])
    SY.dma(dr["HaloSel"], sb["haloacc"][:, :], deps=[(ve, v_a)])
    SY.wait(sy, sy.v)
    GD.wait(gd, gd.v)

def build_full(tab):
    nc = bass.Bass("TRN2", target_bir_lowering=False)
    dr = {}
    def din(name, shape, dt):
        dr[name] = nc.dram_tensor(name, shape, dt, kind="ExternalInput").ap()
    def dint(name, shape, dt):
        h = nc.dram_tensor(name, shape, dt, kind="Internal")
        dr[name + "_h"] = h
        dr[name] = h.ap()
    decl_A_inputs(din)
    decl_B_weights(din)
    decl_C_weights(din)
    din("wsel", [128, 128], F32)
    din("mhalo", [128, 8], F32)
    dr["outp"] = nc.dram_tensor("outp", [T, Dm], F32, kind="ExternalOutput").ap()
    dint("qf", [2048, T], BF16)
    dint("oloc", [T, 2048], BF16)
    dint("DpL", [128, 2048], F32)
    dr["Dp"] = dr["DpL"]
    dint("DpAll", [NCORES * 128, 2048], F32)
    dint("KVIN", [128, 2048], F32)
    dint("WSC", [1, 16], F32)
    dint("S1", [T, Dm], F32)
    dint("S2", [T, Dm], F32)
    dint("HaloL", [3, Dm], F32)
    dint("HaloAll", [NCORES * 3, Dm], F32)
    dint("HaloSel", [3, Dm], F32)
    dr["halo"] = dr["HaloSel"]
    decl_C_scratch(dint)

    phases_A(nc, dr, tab)
    with contextlib.ExitStack() as st:
        sb = {}
        sb["wselb"] = st.enter_context(nc.sbuf_tensor("x1_wselb", [128, 128], F32))
        sb["kvacc"] = st.enter_context(nc.sbuf_tensor("x1_kvacc", [128, 2048], F32))
        sb["dbuf"] = st.enter_context(nc.sbuf_tensor("x1_dbuf", [128, 2048], F32))
        ph = Phase(nc, st, "X1")
        emit_X1(ph, dr, sb)
        ph.emit()
    phases_B(nc, dr, tab, fused=True)
    with contextlib.ExitStack() as st:
        sb = {}
        sb["mhb"] = st.enter_context(nc.sbuf_tensor("x2_mhb", [128, 8], F32))
        sb["halog"] = st.enter_context(nc.sbuf_tensor("x2_halog", [3, NCORES * Dm], F32))
        sb["haloacc"] = st.enter_context(nc.sbuf_tensor("x2_haloacc", [3, Dm], F32))
        ph = Phase(nc, st, "X2")
        emit_X2(ph, dr, sb)
        ph.emit()
    phases_C(nc, dr, tab, fused=True)
    return nc

def host_sel_tables(core_id):
    b, seg = divmod(core_id, SEGS)
    lam = _alibi_lam().astype(np.float64)
    bseg = np.exp(-lam * T)
    W = np.zeros((8, H), np.float64)
    for r in range(NCORES):
        rb, rs = divmod(r, SEGS)
        if rb == b and rs < seg:
            W[r] = bseg ** (seg - 1 - rs)
    wsel = np.ascontiguousarray(
        np.broadcast_to(W.reshape(1, 8 * H), (128, 8 * H)).astype(np.float32))
    m = np.zeros(8, np.float32)
    if seg > 0:
        m[core_id - 1] = 1.0
    mhalo = np.ascontiguousarray(np.broadcast_to(m.reshape(1, 8), (128, 8)))
    return wsel, mhalo


def shard_inputs(inputs):
    x = inputs["L0_attn_x"]
    shards = []
    for core in range(NCORES):
        b, seg = divmod(core, SEGS)
        shards.append(np.ascontiguousarray(x[b, seg * T:(seg + 1) * T, :]))
    return shards

def kernel(**inputs):
    w = host_weights(inputs)
    tabs = [host_tables(ci) for ci in range(NCORES)]
    xsh = shard_inputs(inputs)
    nc = build_full(tabs[0])
    ims = []
    for ci in range(NCORES):
        tab = tabs[ci]
        wsel, mhalo = host_sel_tables(ci)
        im = dict(xin=xsh[ci], wsel=wsel, mhalo=mhalo,
                  diagT=tab["diagT"], kdec=tab["kdec"], qdec_bc=tab["qdec_bc"],
                  cosT=tab["cosT"], sinT=tab["sinT"], ident=tab["ident"],
                  rotm=tab["rotm"], qdec_pp=tab["qdec_pp"])
        im.update({k: w[k] for k in ("wqT", "wkT", "wvT", "woT", "w0fT", "b0f",
                                     "winT", "convw", "convb", "woutT", "wm1T",
                                     "wm2T", "w1T", "w3T", "w2T")})
        ims.append(im)
    res = run_bass_kernel_spmd(nc, ims, core_ids=list(range(NCORES)))
    out = np.empty((B, S, Dm), np.float32)
    for ci in range(NCORES):
        b, seg = divmod(ci, SEGS)
        out[b, seg * T:(seg + 1) * T, :] = res.results[ci]["outp"]
    return out

def kernel_unfused(**inputs):
    w = host_weights(inputs)
    tabs = [host_tables(ci) for ci in range(NCORES)]
    xsh = shard_inputs(inputs)
    zeros = np.zeros((128, 2048), np.float32)

    ncA = build_A(tabs[0])
    imsA = []
    for ci in range(NCORES):
        tab = tabs[ci]
        imsA.append(dict(xin=xsh[ci], wqT=w["wqT"], wkT=w["wkT"], wvT=w["wvT"],
                         diagT=tab["diagT"], kdec=tab["kdec"], qdec_bc=tab["qdec_bc"],
                         cosT=tab["cosT"], sinT=tab["sinT"], ident=tab["ident"],
                         rotm=tab["rotm"]))
    resA = run_bass_kernel_spmd(ncA, imsA, core_ids=list(range(NCORES)))

    ncB = build_B(tabs[0])
    imsB = []
    for ci in range(NCORES):
        b, seg = divmod(ci, SEGS)
        a = resA.results[ci]
        preds = [resA.results[b * SEGS + q]["Dp"] for q in range(seg)]
        preds = [zeros] * (3 - len(preds)) + preds
        imsB.append(dict(xin=xsh[ci], qf=a["qf"], oloc=a["oloc"],
                         Dpred0=preds[0], Dpred1=preds[1], Dpred2=preds[2],
                         woT=w["woT"], w0fT=w["w0fT"], b0f=w["b0f"],
                         qdec_pp=tabs[ci]["qdec_pp"], ident=tabs[ci]["ident"]))
    resB = run_bass_kernel_spmd(ncB, imsB, core_ids=list(range(NCORES)))

    ncC = build_C(tabs[0])
    imsC = []
    for ci in range(NCORES):
        b, seg = divmod(ci, SEGS)
        if seg == 0:
            halo = np.zeros((3, Dm), np.float32)
        else:
            halo = np.ascontiguousarray(resB.results[ci - 1]["S2"][T - 3:T])
        imsC.append(dict(S2=resB.results[ci]["S2"], halo=halo,
                         winT=w["winT"], convw=w["convw"], convb=w["convb"],
                         woutT=w["woutT"], wm1T=w["wm1T"], wm2T=w["wm2T"],
                         w1T=w["w1T"], w3T=w["w3T"], w2T=w["w2T"],
                         ident=tabs[ci]["ident"]))
    resC = run_bass_kernel_spmd(ncC, imsC, core_ids=list(range(NCORES)))

    out = np.empty((B, S, Dm), np.float32)
    for ci in range(NCORES):
        b, seg = divmod(ci, SEGS)
        out[b, seg * T:(seg + 1) * T, :] = resC.results[ci]["outp"]
    return out
